# version 13
# baseline (speedup 1.0000x reference)
import contextlib
import numpy as np
import ml_dtypes
import concourse.bass as bass
import concourse.mybir as mybir
from concourse.bass_utils import run_bass_kernel_spmd

F32 = mybir.dt.float32
BF16 = mybir.dt.bfloat16
ALU = mybir.AluOpType
AF = mybir.ActivationFunctionType
AXX = mybir.AxisListType.X

NCORES = 8
T = 2048
D = 2048
KD = 16
DI = 4096
NH = 64
BL = 64
NCH = 32
NPAIR = 16
NT = 4
NTT = 16
CLAMP = 60.0
FFN_NO_EXPERT = False
FFN_NO_BASE = False
FFN_EXPERT_FIRST = True

ENGS = ("sync", "scalar", "vector", "tensor", "gpsimd")


class Sched:
    def __init__(self, nc, es):
        self.nc = nc
        self.q = {e: [] for e in ENGS}
        self.sems = {}
        self.count = {}
        self.water = {}
        self.writer = {}
        self.readers = {}
        self.es = es

    def sem(self, name):
        h = self.es.enter_context(self.nc.semaphore(name))
        self.sems[name] = h
        self.count[name] = 0
        return h

    def _wait(self, eng, semname, val, src_eng=None, hard=False):
        if val <= 0:
            return
        if (not hard) and src_eng == eng and semname in ("cT", "cV", "cA", "cP"):
            return
        if self.water.get((eng, semname), 0) >= val:
            return
        self.water[(eng, semname)] = val
        h = self.sems[semname]
        self.q[eng].append(lambda e, h=h, v=val: e.wait_ge(h, v))

    def emit(self, eng, fn, reads=(), writes=(), sem=None, n=1, hard=False):
        for b in reads:
            w = self.writer.get(b)
            if w is not None:
                self._wait(eng, w[0], w[1], w[2], w[3])
        for b in writes:
            w = self.writer.get(b)
            if w is not None:
                self._wait(eng, w[0], w[1], w[2], w[3])
            for (s, v, se, hh) in self.readers.get(b, ()):
                self._wait(eng, s, v, se, hh)
        tok = None
        if sem is not None:
            self.count[sem] += n
            tok = (sem, self.count[sem], eng, hard)
            h = self.sems[sem]
            self.q[eng].append(lambda e, fn=fn, h=h, n=n: fn(e).then_inc(h, n))
        else:
            self.q[eng].append(lambda e, fn=fn: fn(e))
        if tok is not None:
            for b in writes:
                self.writer[b] = tok
                self.readers[b] = []
            for b in reads:
                self.readers.setdefault(b, []).append(tok)
        return tok

    def bump(self, keys, sem):
        v = self.count[sem]
        for k in keys:
            if k in self.writer and self.writer[k] is not None:
                self.writer[k] = (sem, v, self.writer[k][2], self.writer[k][3])

    def fence(self, engines=ENGS):
        snap = [(s, v) for s, v in self.count.items() if v > 0]
        for eng in engines:
            for s, v in snap:
                self._wait(eng, s, v, None, True)


def build_nc(dbg=(), upto="full"):
    nc = bass.Bass("TRN2", target_bir_lowering=False)
    ge = contextlib.ExitStack()

    def din(name, shape, dt=F32):
        return nc.dram_tensor(name, shape, dt, kind="ExternalInput").ap()

    xin = din("xin", [T + 3, D])
    wtile = din("wtile", [67, 128, KD, 128])
    cwdiag = din("cwdiag", [34, 128, 4, 128])
    convb_in = din("convb", [128, 34])
    dtb_in = din("dtb", [64, 1])
    alog_in = din("alog", [64, 1])
    dmat_in = din("dmat", [128, DI])
    nw_in = din("nw_i", [128, 32])
    optile = din("optile", [16, 128, 32, 128])
    ftile = din("ftile", [16, 128, KD, 128])
    ast_in = din("ast", [128, KD, 128])
    bst_in = din("bst", [128, 16, 128])
    rwt_in = din("rwt", [128, KD, 8])
    identb_in = din("identb_i", [128, 128], BF16)
    identf_in = din("identf_i", [128, 128])
    triu_in = din("triu_i", [128, 64])
    dblk_in = din("dblk", [8, 128])
    hm_in = din("hm", [128, 7])
    hm1_in = din("hm1", [128, 7])

    out_d = nc.dram_tensor("out", [T, D], F32, kind="ExternalOutput").ap()

    xn_tm_d = nc.dram_tensor("xn_tm_d", [T, D], BF16)
    z_d = nc.dram_tensor("z_d", [DI, T], BF16)
    xssm_d = nc.dram_tensor("xssm_d", [DI, T], BF16)
    xd_tm_d = nc.dram_tensor("xd_tm_d", [T, DI], BF16)
    acumT_d = nc.dram_tensor("acumT_d", [T, 64], F32)
    asr_d = nc.dram_tensor("asr_d", [32, 64], F32)
    y_tm_d = nc.dram_tensor("y_tm_d", [T, DI], BF16)
    g_d = nc.dram_tensor("g_d", [D, T], BF16)
    sg_d = nc.dram_tensor("sg_d", [D, T], BF16)
    cc_in_d = nc.dram_tensor("cc_in_d", [129, DI], F32)
    cc_out_d = nc.dram_tensor("cc_out_d", [NCORES * 129, DI], F32, addr_space="Shared")

    dbg_specs = {
        "dbg_xn": ([T, D], BF16), "dbg_dt": ([64, T], F32),
        "dbg_acum": ([64, T], F32), "dbg_bc": ([128, 2 * T], BF16),
        "dbg_xbc": ([DI, T], BF16), "dbg_z": ([DI, T], BF16),
        "dbg_S": ([128, DI], F32), "dbg_gtm": ([128, NPAIR * 64], F32),
        "dbg_y": ([T, DI], BF16), "dbg_wem": ([8, T], BF16),
        "dbg_g": ([D, T], BF16), "dbg_sg": ([D, T], BF16),
        "dbg_sin": ([128, DI], F32), "dbg_atp": ([128, NPAIR * 64], F32),
        "dbg_ypre": ([T, DI], BF16), "dbg_lg": ([T, 8], F32),
        "dbg_m": ([T, DI], BF16),
        "dbg_axs": ([128, T], BF16), "dbg_wexp": ([128, T], BF16), "dbg_axw": ([128, T], BF16),
        "dbg_eab": ([128, NPAIR * 64], F32), "dbg_dcy": ([128, NPAIR * 64], BF16),
        "dbg_dscale": ([128, NCH * 64], F32), "dbg_xd": ([T, DI], BF16),
    }
    dbg_outs = {}
    for name in dbg:
        shape, dt = dbg_specs[name]
        dbg_outs[name] = nc.dram_tensor(name, shape, dt, kind="ExternalOutput").ap()

    def TS_(name, shape, dt=F32, es=ge):
        return es.enter_context(nc.sbuf_tensor(name, shape, dt))

    identb = TS_("identb", [128, 128], BF16)
    identf = TS_("identf", [128, 128])
    triu = TS_("triu", [128, 64])
    nwb = TS_("nwb", [128, 32], BF16)
    nw = TS_("nw", [128, 32])
    convb_s = TS_("convb_s", [128, 34])
    dtb_s = TS_("dtb_s", [64, 1])
    acol = TS_("acol", [64, 1])
    ones1 = TS_("ones1", [1, 128])
    ones64 = TS_("ones64", [64, 1])
    epscol = TS_("epscol", [128, 1])
    rowb = TS_("rowb", [1, 2048])
    hmask = TS_("hmask", [128, 7])
    hmask1 = TS_("hmask1", [128, 7])
    xbchalo = TS_("xbchalo", [128, 34 * 3], BF16)
    xnhalo = TS_("xnhalo", [128, KD * 3], BF16)
    dblkb = TS_("dblkb", [8, 128], BF16)
    gss = contextlib.ExitStack()
    dmatb = TS_("dmatb", [128, DI], BF16, gss)
    bc = TS_("bc", [128, 2 * T], BF16, gss)
    S = TS_("S", [128, DI], F32, gss)
    atp = TS_("atp", [128, NPAIR * 64], F32, gss)
    eab = TS_("eab", [128, NPAIR * 64], F32, gss)
    dcyb = TS_("dcyb", [128, NPAIR * 64], BF16, gss)
    gtm = TS_("gtm", [128, NPAIR * 64], F32, gss)
    dscale = TS_("dscale", [128, NCH * 64], F32, gss)

    pe_ = contextlib.ExitStack()
    ge.enter_context(pe_)
    pb = [pe_.enter_context(nc.psum_tensor(f"pb{i}", [128, 512], F32)) for i in range(6)]
    pA = pe_.enter_context(nc.psum_tensor("pA", [128, 512], F32))
    pB = pe_.enter_context(nc.psum_tensor("pB", [128, 512], F32))
    pBb = pB.ap().bitcast(BF16)

    sch = Sched(nc, ge)
    for s in ("cT", "cV", "cA", "cP", "ccs"):
        sch.sem(s)
    for s in ("qc", "qx0", "qx1", "qxn0", "qxn1", "qxnr", "qw1a", "qw1b",
              "qcv0", "qcv1", "qst0", "qst1", "qst2", "qst3", "qxr", "qxd",
              "qat", "qcc", "qab", "qzr", "qyw", "qyr0", "qyr1", "qw2a", "qw2b", "qw3",
              "qgw", "qgr", "qsg", "qsr0", "qsr1", "qout", "qm", "qrb", "qxd2", "qx2", "qx3", "qdbg"):
        sch.sem(s)
    E = sch.emit

    def dbg_dump(name, src_fn, reads=()):
        if name in dbg_outs:
            sch.fence(("gpsimd",))
            E("gpsimd", lambda e, f=src_fn: e.dma_start(out=dbg_outs[name], in_=f()),
              reads=reads, sem="qdbg", n=16)

    for dst, src, key in ((identb, identb_in, "identb"), (identf, identf_in, "identf"),
                          (triu, triu_in, "triu"), (nw, nw_in, "nw"),
                          (convb_s, convb_in, "convb"), (dtb_s, dtb_in, "dtb"),
                          (hmask, hm_in, "hmask"), (hmask1, hm1_in, "hmask1")):
        E("sync", lambda e, d=dst, s=src: e.dma_start(out=d[:, :], in_=s),
          writes=(key,), sem="qc", n=16)
    E("sync", lambda e: e.dma_start(out=acol[:, :], in_=alog_in), writes=("acol",), sem="qc", n=16)
    E("sync", lambda e: e.dma_start(out=S[:, :], in_=dmat_in), writes=("S",), sem="qc", n=16)
    E("scalar", lambda e: e.copy(dmatb[:, :], S[:, :]), reads=("S",), writes=("dmatb",), sem="cA")
    E("scalar", lambda e: e.copy(nwb[:, :], nw[:, :]), reads=("nw",), writes=("nwb",), sem="cA")
    E("vector", lambda e: e.memset(ones1[:, :], 1.0), writes=("ones1",), sem="cV", hard=True)
    E("vector", lambda e: e.memset(ones64[:, :], 1.0), writes=("ones64",), sem="cV", hard=True)
    E("vector", lambda e: e.memset(epscol[:, :], 1e-5), writes=("epscol",), sem="cV", hard=True)
    E("scalar", lambda e: e.activation(acol[:, :], acol[:, :], AF.Exp), reads=("acol",), writes=("acol",), sem="cA", hard=True)
    E("vector", lambda e: e.tensor_scalar_mul(acol[:, :], acol[:, :], -1.0),
      reads=("acol",), writes=("acol",), sem="cV", hard=True)
    E("gpsimd", lambda e: e.dma_start(out=dblkb[:, :], in_=dblk_in), writes=("dblkb",), sem="qm", n=16)
    sch.bump(("identb", "identf", "triu", "nw", "convb", "dtb", "hmask", "hmask1", "acol", "S"), "qc")

    bln = contextlib.ExitStack()
    xt = [TS_(f"xt{i}", [128, D], F32, bln) for i in range(2)]
    lnscr = TS_("lnscr", [128, D], F32, bln)
    xnst = [TS_(f"xnst{i}", [128, D], BF16, bln) for i in range(2)]
    lncol = TS_("lncol", [128, 32], F32, bln)
    xnhalo_tm = TS_("xnhalo_tm", [128, D], BF16, bln)

    def ln_tile(rows0, nrows, buf, dram_row, sbuf_dst=None):
        xb = xt[buf]
        st = sbuf_dst if sbuf_dst is not None else xnst[buf]
        kx = f"xt{buf}"
        ks = "xnhalo_tm" if sbuf_dst is not None else f"xnst{buf}"
        E("sync", lambda e, r=rows0, n=nrows, xb=xb: e.dma_start(out=xb[0:n, :], in_=xin[r:r + n, :]),
          writes=(kx,), sem=f"qx{buf}", n=16)
        E("vector", lambda e, n=nrows, xb=xb: e.reduce_sum(lncol[0:n, 0:1], xb[0:n, :], axis=AXX),
          reads=(kx,), writes=("lnm",), sem="cV", hard=True)
        E("vector", lambda e, n=nrows: e.tensor_scalar_mul(lncol[0:n, 0:1], lncol[0:n, 0:1], 1.0 / D),
          reads=("lnm",), writes=("lnm",), sem="cV", hard=True)
        E("scalar", lambda e, n=nrows, xb=xb: e.activation(lnscr[0:n, :], xb[0:n, :], AF.Square),
          reads=(kx,), writes=("lnscr",), sem="cA")
        E("vector", lambda e, n=nrows: e.reduce_sum(lncol[0:n, 8:9], lnscr[0:n, :], axis=AXX),
          reads=("lnscr",), writes=("lnsq",), sem="cV", hard=True)
        E("vector", lambda e, n=nrows: e.tensor_scalar_mul(lncol[0:n, 16:17], lncol[0:n, 8:9], 1.0 / D),
          reads=("lnsq",), writes=("lnv",), sem="cV", hard=True)
        E("vector", lambda e, n=nrows: e.tensor_tensor(out=lncol[0:n, 24:25], in0=lncol[0:n, 0:1],
                                                       in1=lncol[0:n, 0:1], op=ALU.mult),
          reads=("lnm",), writes=("lnt",), sem="cV", hard=True)
        E("vector", lambda e, n=nrows: e.tensor_tensor(out=lncol[0:n, 16:17], in0=lncol[0:n, 16:17],
                                                       in1=lncol[0:n, 24:25], op=ALU.subtract),
          reads=("lnt", "lnv"), writes=("lnv",), sem="cV", hard=True)
        E("scalar", lambda e, n=nrows: e.activation(lncol[0:n, 16:17], lncol[0:n, 16:17], AF.Ln,
                                                    bias=epscol[0:n, :]),
          reads=("lnv", "epscol"), writes=("lnv",), sem="cA", hard=True)
        E("scalar", lambda e, n=nrows: e.activation(lncol[0:n, 16:17], lncol[0:n, 16:17], AF.Exp, scale=-0.5),
          reads=("lnv",), writes=("lnv",), sem="cA", hard=True)
        E("vector", lambda e, n=nrows, xb=xb, st=st: e.tensor_scalar(
            out=st[0:n, :], in0=xb[0:n, :], scalar1=lncol[0:n, 0:1], scalar2=lncol[0:n, 16:17],
            op0=ALU.subtract, op1=ALU.mult),
          reads=(kx, "lnm", "lnv"), writes=(ks,), sem="cV")
        if sbuf_dst is None:
            E("sync", lambda e, r=dram_row, n=nrows, st=st: e.dma_start(
                out=xn_tm_d.ap()[r:r + n, :], in_=st[0:n, :]),
              reads=(ks,), writes=("xn_d",), sem=f"qxn{buf}", n=16)

    for i in range(NTT):
        ln_tile(3 + i * 128, 128, i % 2, i * 128)
    ln_tile(0, 3, 0, 0, sbuf_dst=xnhalo_tm)
    for k in range(KD):
        E("tensor", lambda e, k=k: e.transpose(pBb[0:128, 0:3], xnhalo_tm[0:3, k * 128:(k + 1) * 128],
                                               identb[0:3, 0:3]),
          reads=("xnhalo_tm", "identb"), writes=("pB",), sem="cT")
        E("vector", lambda e, k=k: e.tensor_copy(xnhalo[:, k * 3:(k + 1) * 3], pBb[0:128, 0:3]),
          reads=("pB",), writes=("xnhalo",), sem="cV")

    dbg_dump("dbg_xn", lambda: xn_tm_d.ap())
    if upto == "ln":
        return _finish(nc, sch, ge, [bln], out_d)

    sch.fence()
    bln.close()
    bip = contextlib.ExitStack()
    xnf = TS_("xnf", [128, KD * T], BF16, bip)
    wslot = [TS_(f"wslot{i}", [128, KD * 128], BF16, bip) for i in range(2)]
    cvw = [TS_(f"cvw{i}", [128, 4 * 128], BF16, bip) for i in range(2)]
    preconv = TS_("preconv", [128, 3 + 1024], BF16, bip)
    aDT = TS_("aDT", [64, T], F32, bip)
    aSC = TS_("aSC", [64, T], F32, bip)
    aAC = TS_("aAC", [64, T], F32, bip)
    aDC = TS_("aDC", [64, T], F32, bip)
    ascol = TS_("ascol", [64, 8], F32, bip)
    stg = [TS_(f"stg{i}", [128, 512], BF16, bip) for i in range(4)]

    for k in range(KD):
        E("sync", lambda e, k=k: e.dma_start_transpose(xnf[:, k * T:(k + 1) * T],
                                                       xn_tm_d.ap()[:, k * 128:(k + 1) * 128]),
          reads=("xn_d",), writes=(f"xnf{k}",), sem="qxnr", n=16)
    sch.bump(tuple(f"xnf{k}" for k in range(KD)), "qxnr")

    def w_load(mc, slot_i):
        slot, sk = wslot[slot_i], f"wslot{slot_i}"
        E("gpsimd", lambda e, mc=mc, slot=slot: e.dma_start(
            out=slot[:, :].rearrange("p (k c) -> p k c", k=KD), in_=wtile[mc]),
          writes=(sk,), sem=("qw1a" if slot_i == 0 else "qw1b"), n=16)

    for i, mc in enumerate(range(32, 66)):
        w_load(mc, i % 2)
        slot, sk = wslot[i % 2], f"wslot{i % 2}"
        bank, bk = pb[i % 2], f"pb{i % 2}"
        for k in range(KD):
            E("tensor", lambda e, k=k, slot=slot, bank=bank: e.matmul(
                bank.ap()[:, 0:3], slot[:, k * 128:(k + 1) * 128],
                xnhalo[:, k * 3:(k + 1) * 3], start=(k == 0), stop=(k == KD - 1)),
              reads=(sk, "xnhalo"), writes=(bk,), sem=("cT" if k == KD - 1 else None))
        E("scalar", lambda e, i=i, bank=bank: e.copy(xbchalo[:, i * 3:(i + 1) * 3], bank.ap()[:, 0:3]),
          reads=(bk,), writes=("xbchalo",), sem="cA", hard=True)

    xpair1 = TS_("xpair1", [128, DI], BF16, bip)
    xdpair1 = TS_("xdpair1", [128, DI], BF16, bip)
    xdc1 = TS_("xdc1", [64, DI], BF16, bip)
    btpc1 = TS_("btpc1", [64, 128], BF16, bip)

    def emit_sweep_pair(p):
        E("sync", lambda e, p=p: e.dma_start_transpose(xpair1[:, :],
                                                       xssm_d.ap()[:, p * 128:(p + 1) * 128]),
          reads=("xssm_d",), writes=("xpair1",), sem="qxr", n=16)
        E("vector", lambda e, p=p: e.tensor_tensor(
            out=xdpair1[:, :].rearrange("s (h q) -> s h q", h=NH),
            in0=xpair1[:, :].rearrange("s (h q) -> s h q", h=NH),
            in1=bass.AP(dcyb, p * 64, [[NPAIR * 64, 128], [1, 64], [0, 64]]),
            op=ALU.mult),
          reads=("xpair1", "dcyb"), writes=("xdpair1",), sem="cV")
        E("sync", lambda e, p=p: e.dma_start(out=xd_tm_d.ap()[p * 128:(p + 1) * 128, :], in_=xdpair1[:, :]),
          reads=("xdpair1",), writes=("xd_d",), sem="qxd", n=16)
        for ci in range(2):
            c = 2 * p + ci
            E("tensor", lambda e, c=c: e.transpose(pBb[0:64, 0:128], bc[:, c * 64:(c + 1) * 64],
                                                   identb[:, :]),
              reads=("bc", "identb"), writes=("pB",), sem="cT")
            E("vector", lambda e: e.tensor_copy(btpc1[:, :], pBb[0:64, 0:128]),
              reads=("pB",), writes=("btpc1",), sem="cV")
            E("sync", lambda e, c=c: e.dma_start(out=xdc1[:, :], in_=xd_tm_d.ap()[c * 64:(c + 1) * 64, :]),
              reads=("xd_d",), writes=("xdc1",), sem="qxd", n=16)
            emit_T_and_scan(sch, S, dscale, btpc1, "btpc1", xdc1, "xdc1", pb, c)
            tp = (0, 64) if ci == 1 else None
            E("tensor", lambda e, c=c, ci=ci, tp=tp: e.matmul(
                pA.ap()[64 * ci:64 * ci + 64, 0:64], bc[:, c * 64:(c + 1) * 64],
                bc[:, T + c * 64: T + (c + 1) * 64], start=True, stop=True,
                tile_position=tp),
              reads=("bc",), writes=("pA",), sem=("cT" if ci == 1 else None))
            if ci == 1:
                E("vector", lambda e, p=p: e.tensor_tensor(out=gtm[:, p * 64:(p + 1) * 64],
                                                           in0=pA.ap()[:, 0:64],
                                                           in1=triu[:, :], op=ALU.mult),
                  reads=("pA", "triu"), writes=("gtm",), sem="cV", hard=True)
    dbg_dump("dbg_S", lambda: S[:, :], reads=("S",))
    dbg_dump("dbg_gtm", lambda: gtm[:, :], reads=("gtm",))
    dbg_dump("dbg_xd", lambda: xd_tm_d.ap())
    if upto == "sweep1":
        return _finish(nc, sch, ge, [bip], out_d)


    mc_order = [66] + list(range(32, 66)) + list(range(0, 32))
    bank_rot = 0
    stg_rot = 0

    def emit_conv(ci, n, cs, csk):
        nonlocal stg_rot
        off = 512 * (n % 2)
        if n % 2 == 0 and n > 0:
            E("scalar", lambda e: e.copy(preconv[:, 0:3], preconv[:, 1024:1027]),
              reads=("preconv",), writes=("pchead",), sem="cA", hard=True)
        cbank_i = 4 + (n % 2)
        cbank, cbk = pb[cbank_i], f"pb{cbank_i}"
        for tap in range(4):
            E("tensor", lambda e, tap=tap, off=off, cs=cs, cbank=cbank: e.matmul(
                cbank.ap()[:, :], cs[:, tap * 128:(tap + 1) * 128],
                preconv[:, off + tap:off + tap + 512], start=(tap == 0), stop=(tap == 3)),
              reads=(csk, "preconv", "pchead"), writes=(cbk,),
              sem=("cT" if tap == 3 else None))
        si = stg_rot % 4
        st_, stk = stg[si], f"stg{si}"
        stg_rot += 1
        E("scalar", lambda e, ci=ci, cbank=cbank, st_=st_: e.activation(
            st_[:, :], cbank.ap()[:, :], AF.Silu, bias=convb_s[:, ci:ci + 1]),
          reads=(cbk, "convb"), writes=(stk,), sem="cA")
        if ci < 32:
            E("sync", lambda e, ci=ci, n=n, st_=st_: e.dma_start(
                out=xssm_d.ap()[ci * 128:(ci + 1) * 128, n * 512:(n + 1) * 512], in_=st_[:, :]),
              reads=(stk,), writes=("xssm_d",), sem=f"qst{si}", n=16)
        else:
            boff = (ci - 32) * T
            E("vector", lambda e, boff=boff, n=n, st_=st_: e.tensor_copy(
                bc[:, boff + n * 512: boff + (n + 1) * 512], st_[:, :]),
              reads=(stk,), writes=("bc",), sem="cV")

    for mi, mc in enumerate(mc_order):
        w_load(mc, mi % 2)
        slot, sk = wslot[mi % 2], f"wslot{mi % 2}"
        is_xbc = 32 <= mc < 66
        ci = mc - 32 if is_xbc else None
        if is_xbc:
            cs, csk = cvw[mi % 2], f"cvw{mi % 2}"
            E("gpsimd", lambda e, ci=ci, cs=cs: e.dma_start(
                out=cs[:, :].rearrange("p (t c) -> p t c", t=4), in_=cwdiag[ci]),
              writes=(csk,), sem=("qcv0" if mi % 2 == 0 else "qcv1"), n=16)
            E("scalar", lambda e, ci=ci: e.copy(preconv[:, 0:3], xbchalo[:, ci * 3:(ci + 1) * 3]),
              reads=("xbchalo",), writes=("pchead",), sem="cA", hard=True)
        for n in range(NT + 1):
            if n < NT:
                bank_i = bank_rot % 4
                bank, bk = pb[bank_i], f"pb{bank_i}"
                bank_rot += 1
                for k in range(KD):
                    E("tensor", lambda e, k=k, n=n, slot=slot, bank=bank: e.matmul(
                        bank.ap()[:, :], slot[:, k * 128:(k + 1) * 128],
                        xnf[:, k * T + n * 512: k * T + (n + 1) * 512],
                        start=(k == 0), stop=(k == KD - 1)),
                      reads=(sk, f"xnf{k}"), writes=(bk,), sem=("cT" if k == KD - 1 else None))
            if is_xbc and n >= 1:
                emit_conv(ci, n - 1, cs, csk)
            if n < NT:
                if mc == 66:
                    E("vector", lambda e, n=n, bank=bank: e.tensor_copy(
                        aDT[:, n * 512:(n + 1) * 512], bank.ap()[0:64, :]),
                      reads=(bk,), writes=("a1lo",), sem="cV")
                elif is_xbc:
                    E("scalar", lambda e, n=n, bank=bank: e.copy(
                        preconv[:, 3 + 512 * (n % 2):3 + 512 * (n % 2) + 512], bank.ap()[:, :]),
                      reads=(bk, "pchead"), writes=("preconv",), sem="cA", hard=True)
                else:
                    si = stg_rot % 4
                    st_, stk = stg[si], f"stg{si}"
                    stg_rot += 1
                    E("scalar", lambda e, bank=bank, st_=st_: e.copy(st_[:, :], bank.ap()[:, :]),
                      reads=(bk,), writes=(stk,), sem="cA")
                    E("sync", lambda e, mc=mc, n=n, st_=st_: e.dma_start(
                        out=z_d.ap()[mc * 128:(mc + 1) * 128, n * 512:(n + 1) * 512], in_=st_[:, :]),
                      reads=(stk,), writes=("z_d",), sem=f"qst{si}", n=16)
        if mc == 66:
            emit_dt_chain(sch, aDT, aSC, aAC, aDC, ascol, dtb_s, acol, ones64, identf,
                          atp, eab, dcyb, dscale, acumT_d, asr_d, rowb, ones1, pA, pB)
        if mc < 32 and mc % 2 == 1:
            if mc == 1:
                E("vector", lambda e: e.memset(S[:, :], 0.0), reads=("S",), writes=("S",), sem="cV")
            emit_sweep_pair(mc // 2)

    dbg_dump("dbg_dt", lambda: aDT[:, :], reads=("a1lo",))
    dbg_dump("dbg_acum", lambda: aAC[:, :], reads=("a2hi",))
    dbg_dump("dbg_atp", lambda: atp[:, :], reads=("atp",))
    dbg_dump("dbg_eab", lambda: eab[:, :], reads=("eab",))
    dbg_dump("dbg_dcy", lambda: dcyb[:, :], reads=("dcyb",))
    dbg_dump("dbg_dscale", lambda: dscale[:, :], reads=("dscale",))
    dbg_dump("dbg_bc", lambda: bc[:, :], reads=("bc",))
    dbg_dump("dbg_xbc", lambda: xssm_d.ap())
    dbg_dump("dbg_z", lambda: z_d.ap())
    if upto == "inproj":
        return _finish(nc, sch, ge, [bip], out_d)

    E("sync", lambda e: e.dma_start(out=cc_in_d.ap()[0:128, :], in_=S[:, :]),
      reads=("S",), writes=("ccin",), sem="qcc", n=16)
    E("vector", lambda e: e.reduce_sum(
        ascol[:, 0:1],
        bass.AP(aAC, 63, [[T, 64], [64, 32]]), axis=AXX),
      reads=("a2hi",), writes=("asumcore",), sem="cV", hard=True)
    E("tensor", lambda e: e.transpose(pA.ap()[0:1, 0:64], ascol[:, 0:1], identf[0:64, 0:64]),
      reads=("asumcore", "identf"), writes=("pA",), sem="cT")
    E("vector", lambda e: e.tensor_copy(rowb[0:1, 0:64], pA.ap()[0:1, 0:64]),
      reads=("pA",), writes=("rowb",), sem="cV")
    E("sync", lambda e: e.dma_start(out=cc_in_d.ap()[128:129, 0:64], in_=rowb[0:1, 0:64]),
      reads=("rowb",), writes=("ccin",), sem="qcc", n=16)
    sch.fence(("gpsimd",))
    E("gpsimd", lambda e: e.collective_compute(
        "AllGather", ALU.bypass, replica_groups=[list(range(NCORES))],
        ins=[cc_in_d.ap()], outs=[cc_out_d.ap()]),
      reads=("ccin",), writes=("ccout",), sem="ccs", n=1)

    sch.fence()
    bip.close()
    bho = contextlib.ExitStack()
    tjb = [TS_("tjb0", [128, DI], F32, bho)]
    dbc = TS_("dbc", [128, 64], F32, bho)
    E("vector", lambda e: e.memset(S[:, :], 0.0), reads=("S",), writes=("S",), sem="cV")
    for j in range(7):
        tj, tk = tjb[0], "tjb0"
        E("sync", lambda e, j=j, tj=tj: e.dma_start(out=tj[:, :],
                                                    in_=cc_out_d.ap()[j * 129:j * 129 + 128, :]),
          reads=("ccout",), writes=(tk,), sem="qcc", n=16)
        E("sync", lambda e, j=j: e.dma_start(out=rowb[0:1, 0:64],
                                             in_=cc_out_d.ap()[j * 129 + 128:j * 129 + 129, 0:64]),
          reads=("ccout",), writes=("rowb",), sem="qrb", n=16)
        E("scalar", lambda e: e.activation(rowb[0:1, 0:64], rowb[0:1, 0:64], AF.Exp),
          reads=("rowb",), writes=("rowb",), sem="cA")
        E("tensor", lambda e: e.matmul(pA.ap()[:, 0:64], ones1[:, :], rowb[0:1, 0:64],
                                       start=True, stop=True),
          reads=("rowb", "ones1"), writes=("pA",), sem="cT")
        E("vector", lambda e, j=j: e.tensor_scalar(out=dbc[:, :], in0=pA.ap()[:, 0:64],
                                                   scalar1=hmask[:, j:j + 1], scalar2=hmask1[:, j:j + 1],
                                                   op0=ALU.mult, op1=ALU.add),
          reads=("pA", "hmask", "hmask1"), writes=("dbc",), sem="cV", hard=True)
        E("vector", lambda e: e.tensor_tensor(
            out=S[:, :].rearrange("n (h q) -> n h q", h=NH),
            in0=S[:, :].rearrange("n (h q) -> n h q", h=NH),
            in1=bass.AP(dbc, 0, [[64, 128], [1, 64], [0, 64]]), op=ALU.mult),
          reads=("S", "dbc"), writes=("S",), sem="cV")
        E("vector", lambda e, j=j, tj=tj: e.scalar_tensor_tensor(
            out=S[:, :], in0=tj[:, :], scalar=hmask[:, j:j + 1], in1=S[:, :],
            op0=ALU.mult, op1=ALU.add),
          reads=(tk, "S", "hmask"), writes=("S",), sem="cV")
    dbg_dump("dbg_sin", lambda: S[:, :], reads=("S",))
    if upto == "exchange":
        return _finish(nc, sch, ge, [bho], out_d)

    sch.fence()
    bho.close()
    bcm = contextlib.ExitStack()
    sbf = [TS_(f"sbf{i}", [128, DI], BF16, bcm) for i in range(2)]
    abbuf = TS_("abbuf", [128, DI], F32, bcm)
    argbuf = TS_("argbuf", [128, DI], F32, bcm)
    mbuf = [TS_(f"mbuf{i}", [128, DI], BF16, bcm) for i in range(2)]
    xbuf_ = [TS_("xbufA", [128, DI], BF16, bcm)]
    xdc2_ = [TS_(f"xdc2{i}", [64, DI], BF16, bcm) for i in range(2)]
    zbuf_ = [TS_("zbufA", [128, DI], BF16, bcm)]
    t1 = argbuf
    ystage = TS_("ystage", [128, DI], BF16, bcm)
    btpc2 = TS_("btpc2", [64, 128], BF16, bcm)
    rmsc = TS_("rmsc", [128, 16], F32, bcm)
    fxn = TS_("fxn", [128, KD * 512], BF16, bcm)
    fslot = [TS_(f"fslot{i}", [128, KD * 128], BF16, bcm) for i in range(2)]
    axsb = TS_("axsb", [128, 512], BF16, bcm)
    axw = TS_("axw", [128, 512], BF16, bcm)
    wexpb = TS_("wexpb", [128, 512], BF16, bcm)
    wem = TS_("wem", [8, 512], BF16, bcm)
    astb = TS_("astb", [128, KD * 128], BF16, bcm)
    bstb = TS_("bstb", [128, 16 * 128], BF16, bcm)
    rwtb = TS_("rwtb", [128, KD * 8], BF16, bcm)
    fsc = TS_("fsc", [128, 40], F32, bcm)
    fb2 = TS_("fb2", [128, 8], F32, bcm)
    gst = TS_("gst", [128, 512], BF16, bcm)

    E("gpsimd", lambda e: e.dma_start(out=astb[:, :].rearrange("p (k c) -> p k c", k=KD), in_=ast_in),
      writes=("astb",), sem="qm", n=16)
    E("gpsimd", lambda e: e.dma_start(out=bstb[:, :].rearrange("p (m c) -> p m c", m=16), in_=bst_in),
      writes=("bstb",), sem="qm", n=16)
    E("gpsimd", lambda e: e.dma_start(out=rwtb[:, :].rearrange("p (k c) -> p k c", k=KD), in_=rwt_in),
      writes=("rwtb",), sem="qm", n=16)
    sch.bump(("dblkb", "astb", "bstb", "rwtb"), "qm")

    def cmain_pair(p):
        xbuf, xbk = xbuf_[0], "xbufA"
        zbuf, zbk = zbuf_[0], "zbufA"
        E("gpsimd", lambda e, p=p: e.dma_start(
            out=abbuf[0:64, :], in_=bass.AP(acumT_d, (2 * p) * 64 * 64, [[0, 64], [64, 64], [1, 64]])),
          reads=("acumT_d",), writes=("abbuf",), sem="qab", n=16)
        E("gpsimd", lambda e, p=p: e.dma_start(
            out=abbuf[64:128, :], in_=bass.AP(acumT_d, (2 * p + 1) * 64 * 64, [[0, 64], [64, 64], [1, 64]])),
          reads=("acumT_d",), writes=("abbuf",), sem="qab", n=16)
        E("sync", lambda e, p=p, xbuf=xbuf: e.dma_start_transpose(xbuf[:, :], xssm_d.ap()[:, p * 128:(p + 1) * 128]),
          reads=("xssm_d",), writes=(xbk,), sem="qxr", n=16)
        E("sync", lambda e, p=p, zbuf=zbuf: e.dma_start_transpose(zbuf[:, :], z_d.ap()[:, p * 128:(p + 1) * 128]),
          reads=("z_d",), writes=(zbk,), sem="qzr", n=16)
        mb, mk = mbuf[p % 2], f"mbuf{p % 2}"
        E("vector", lambda e, p=p: e.tensor_tensor(
            out=argbuf[:, :].rearrange("s (l h) -> s l h", l=64),
            in0=abbuf[:, :].rearrange("s (l h) -> s l h", l=64),
            in1=bass.AP(atp, p * 64, [[NPAIR * 64, 128], [0, 64], [1, 64]]),
            op=ALU.subtract),
          reads=("abbuf", "atp"), writes=("argbuf",), sem="cV")
        E("vector", lambda e: e.tensor_scalar_min(argbuf[:, :], argbuf[:, :], CLAMP),
          reads=("argbuf",), writes=("argbuf",), sem="cV", hard=True)
        E("scalar", lambda e: e.activation(argbuf[:, :], argbuf[:, :], AF.Exp),
          reads=("argbuf",), writes=("argbuf",), sem="cA")
        E("vector", lambda e, p=p, mb=mb, mk=mk: e.tensor_tensor(
            out=mb[:, :].rearrange("s (l h) -> s l h", l=64),
            in0=argbuf[:, :].rearrange("s (l h) -> s l h", l=64),
            in1=bass.AP(gtm, p * 64, [[NPAIR * 64, 128], [1, 64], [0, 64]]),
            op=ALU.mult),
          reads=("argbuf", "gtm"), writes=(mk,), sem="cV")
        E("vector", lambda e, mb=mb, mk=mk: e.tensor_tensor(out=mb[:, :], in0=mb[:, :], in1=dmatb[:, :],
                                                            op=ALU.add),
          reads=("dmatb", mk), writes=(mk,), sem="cV")
        if "dbg_m" in dbg_outs:
            E("sync", lambda e, p=p, mb=mb: e.dma_start(out=dbg_outs["dbg_m"][p * 128:(p + 1) * 128, :],
                                                        in_=mb[:, :]),
              reads=(mk,), sem="qdbg", n=16)
        for ci in range(2):
            c = 2 * p + ci
            E("tensor", lambda e, c=c: e.transpose(pBb[0:64, 0:128], bc[:, c * 64:(c + 1) * 64],
                                                   identb[:, :]),
              reads=("bc", "identb"), writes=("pB",), sem="cT")
            E("vector", lambda e: e.tensor_copy(btpc2[:, :], pBb[0:64, 0:128]),
              reads=("pB",), writes=("btpc2",), sem="cV")
            xdc2, xdk2 = xdc2_[ci], f"xdc2{ci}"
            E("gpsimd", lambda e, c=c, xdc2=xdc2: e.dma_start(out=xdc2[:, :], in_=xd_tm_d.ap()[c * 64:(c + 1) * 64, :]),
              reads=("xd_d",), writes=(xdk2,), sem="qxd2", n=16)
            emit_T_and_scan(sch, S, dscale, btpc2, "btpc2", xdc2, xdk2, pb, c,
                            sbf=sbf[ci], sbfk=f"sbf{ci}")
        for j in range(8):
            ybank, ybk = pb[j % 2], f"pb{j % 2}"
            obank, obk = pb[2 + j % 2], f"pb{2 + j % 2}"
            for ci in range(2):
                r0 = 64 * ci
                tpd = (64, 64) if ci == 1 else None
                for h8 in range(8):
                    h = 8 * j + h8
                    E("tensor", lambda e, h=h, h8=h8, r0=r0, mb=mb, mk=mk, ybank=ybank, tp=tpd,
                      first=(h8 == 0), last=(h8 == 7): e.matmul(
                        ybank.ap()[r0:r0 + 64, h8 * 64:(h8 + 1) * 64],
                        mb[:, :].rearrange("s (l h2) -> s l h2", l=64)[r0:r0 + 64, :, h],
                        xbuf[r0:r0 + 64, h * 64:(h + 1) * 64],
                        start=first, stop=last, tile_position=tp),
                      reads=(mk, xbk), writes=(ybk,),
                      sem=("cT" if (ci == 1 and h8 == 7) else None))
            for ci in range(2):
                c = 2 * p + ci
                tpo = (0, 64) if ci == 1 else None
                E("tensor", lambda e, j=j, c=c, ci=ci, obank=obank, tp=tpo: e.matmul(
                    obank.ap()[64 * ci:64 * ci + 64, :],
                    bc[:, T + c * 64: T + (c + 1) * 64],
                    sbf[ci][:, j * 512:(j + 1) * 512],
                    start=True, stop=True, tile_position=tp),
                  reads=("bc", f"sbf{ci}"), writes=(obk,),
                  sem=("cT" if ci == 1 else None))
            E("vector", lambda e, j=j, p=p, obank=obank: e.tensor_tensor(
                out=t1[:, j * 512:(j + 1) * 512].rearrange("s (h q) -> s h q", h=8),
                in0=obank.ap()[:, :].rearrange("s (h q) -> s h q", h=8),
                in1=bass.AP(eab, p * 64 + j * 8, [[NPAIR * 64, 128], [1, 8], [0, 64]]),
                op=ALU.mult),
              reads=(obk, "eab"), writes=("argbuf",), sem="cV")
            E("vector", lambda e, j=j, ybank=ybank: e.tensor_tensor(
                out=ystage[:, j * 512:(j + 1) * 512], in0=t1[:, j * 512:(j + 1) * 512],
                in1=ybank.ap()[:, :], op=ALU.add),
              reads=(ybk, "argbuf"), writes=("ystage",), sem="cV")
        if "dbg_ypre" in dbg_outs:
            E("sync", lambda e, p=p: e.dma_start(out=dbg_outs["dbg_ypre"][p * 128:(p + 1) * 128, :],
                                                 in_=ystage[:, :]),
              reads=("ystage",), sem="qdbg", n=16)
        E("scalar", lambda e, zbuf=zbuf: e.activation(zbuf[:, :], zbuf[:, :], AF.Silu),
          reads=(zbk,), writes=(zbk,), sem="cA")
        E("vector", lambda e, zbuf=zbuf: e.tensor_tensor(out=ystage[:, :], in0=ystage[:, :], in1=zbuf[:, :],
                                                         op=ALU.mult),
          reads=(zbk,), writes=("ystage",), sem="cV")
        E("scalar", lambda e: e.activation(t1[:, :], ystage[:, :], AF.Square),
          reads=("ystage",), writes=("argbuf",), sem="cA")
        E("vector", lambda e: e.reduce_sum(rmsc[:, 0:1], t1[:, :], axis=AXX),
          reads=("argbuf",), writes=("rmsc",), sem="cV", hard=True)
        E("vector", lambda e: e.tensor_scalar(out=rmsc[:, 8:9], in0=rmsc[:, 0:1], scalar1=1.0 / DI,
                                              scalar2=1e-5, op0=ALU.mult, op1=ALU.add),
          reads=("rmsc",), writes=("rmsc2",), sem="cV", hard=True)
        E("scalar", lambda e: e.activation(rmsc[:, 8:9], rmsc[:, 8:9], AF.Ln),
          reads=("rmsc2", "epscol"), writes=("rmsc2",), sem="cA", hard=True)
        E("scalar", lambda e: e.activation(rmsc[:, 8:9], rmsc[:, 8:9], AF.Exp, scale=-0.5),
          reads=("rmsc2",), writes=("rmsc2",), sem="cA", hard=True)
        E("vector", lambda e: e.tensor_scalar_mul(ystage[:, :], ystage[:, :], rmsc[:, 8:9]),
          reads=("rmsc2",), writes=("ystage",), sem="cV")
        E("gpsimd", lambda e, p=p: e.dma_start(out=y_tm_d.ap()[p * 128:(p + 1) * 128, :], in_=ystage[:, :]),
          reads=("ystage",), writes=("y_d",), sem="qyw", n=16)

    def ffn_nblock(n):
        for k in range(KD):
            E("sync", lambda e, k=k, n=n: e.dma_start_transpose(
                fxn[:, k * 512:(k + 1) * 512],
                xn_tm_d.ap()[n * 512:(n + 1) * 512, k * 128:(k + 1) * 128]),
              reads=("xn_d",), writes=(f"fxn{k}",), sem="qxnr", n=16)
        sch.bump(tuple(f"fxn{k}" for k in range(KD)), "qxnr")
        for tt in range(4):
            for k in range(KD):
                E("tensor", lambda e, k=k, tt=tt: e.matmul(
                    pB.ap()[:, 0:8], fxn[:, k * 512 + tt * 128: k * 512 + (tt + 1) * 128],
                    rwtb[:, k * 8:(k + 1) * 8], start=(k == 0), stop=(k == KD - 1)),
                  reads=(f"fxn{k}", "rwtb"), writes=("pB",), sem=("cT" if k == KD - 1 else None))
            E("vector", lambda e: e.tensor_copy(fsc[:, 0:8], pB.ap()[:, 0:8]),
              reads=("pB",), writes=("flg",), sem="cV", hard=True)
            if "dbg_lg" in dbg_outs:
                E("sync", lambda e, n=n, tt=tt: e.dma_start(
                    out=dbg_outs["dbg_lg"][n * 512 + tt * 128: n * 512 + (tt + 1) * 128, :],
                    in_=fsc[:, 0:8]),
                  reads=("flg",), sem="qdbg", n=16)
            E("vector", lambda e: e.reduce_max(fsc[:, 8:9], fsc[:, 0:8], axis=AXX),
              reads=("flg",), writes=("fm1",), sem="cV", hard=True)
            E("vector", lambda e: e.tensor_tensor(out=fb2[:, :], in0=fsc[:, 0:8],
                                                  in1=fsc[:, 8:9].broadcast_to([128, 8]),
                                                  op=ALU.is_ge),
              reads=("flg", "fm1"), writes=("fb2",), sem="cV", hard=True)
            E("vector", lambda e: e.scalar_tensor_tensor(out=fsc[:, 0:8], in0=fb2[:, :], scalar=-1e30,
                                                         in1=fsc[:, 0:8], op0=ALU.mult, op1=ALU.add),
              reads=("fb2", "flg"), writes=("flg",), sem="cV", hard=True)
            E("vector", lambda e: e.reduce_max(fsc[:, 16:17], fsc[:, 0:8], axis=AXX),
              reads=("flg",), writes=("fm2",), sem="cV", hard=True)
            E("vector", lambda e: e.tensor_tensor(out=fsc[:, 0:8], in0=fsc[:, 0:8],
                                                  in1=fsc[:, 16:17].broadcast_to([128, 8]),
                                                  op=ALU.is_ge),
              reads=("flg", "fm2"), writes=("flg",), sem="cV", hard=True)
            E("vector", lambda e: e.tensor_tensor(out=fsc[:, 24:25], in0=fsc[:, 8:9], in1=fsc[:, 16:17],
                                                  op=ALU.subtract),
              reads=("fm1", "fm2"), writes=("fw",), sem="cV", hard=True)
            E("scalar", lambda e: e.activation(fsc[:, 24:25], fsc[:, 24:25], AF.Sigmoid),
              reads=("fw",), writes=("fw",), sem="cA", hard=True)
            E("vector", lambda e: e.tensor_scalar(out=fsc[:, 32:33], in0=fsc[:, 24:25], scalar1=-1.0,
                                                  scalar2=1.0, op0=ALU.mult, op1=ALU.add),
              reads=("fw",), writes=("fw2",), sem="cV", hard=True)
            E("vector", lambda e: e.tensor_scalar_mul(fb2[:, :], fb2[:, :], fsc[:, 24:25]),
              reads=("fb2", "fw"), writes=("fb2",), sem="cV", hard=True)
            E("vector", lambda e: e.scalar_tensor_tensor(out=fsc[:, 0:8], in0=fsc[:, 0:8],
                                                         scalar=fsc[:, 32:33], in1=fb2[:, :],
                                                         op0=ALU.mult, op1=ALU.add),
              reads=("fb2", "fw2", "flg"), writes=("flg",), sem="cV", hard=True)
            E("tensor", lambda e: e.transpose(pA.ap()[0:8, 0:128], fsc[:, 0:8], identf[:, :]),
              reads=("flg", "identf"), writes=("pA",), sem="cT")
            E("scalar", lambda e, tt=tt: e.copy(wem[0:8, tt * 128:(tt + 1) * 128], pA.ap()[0:8, 0:128]),
              reads=("pA",), writes=("wem",), sem="cA")
        dbg_dump("dbg_wem", (lambda n=n: (lambda: dbg_outs["dbg_wem"]))(), reads=("wem",)) if False else None
        if "dbg_wem" in dbg_outs:
            sch.fence(("gpsimd",))
            E("gpsimd", lambda e, n=n: e.dma_start(out=dbg_outs["dbg_wem"][0:8, n * 512:(n + 1) * 512],
                                                   in_=wem[0:8, :]),
              reads=("wem",), sem="qdbg", n=16)
        E("tensor", lambda e: e.matmul(pB.ap()[:, :], dblkb[:, :], wem[0:8, :], start=True, stop=True),
          reads=("dblkb", "wem"), writes=("pB",), sem="cT")
        E("scalar", lambda e: e.copy(wexpb[:, :], pB.ap()[:, :]), reads=("pB",), writes=("wexpb",), sem="cA")
        for k in range(KD):
            E("tensor", lambda e, k=k: e.matmul(pA.ap()[:, :], astb[:, k * 128:(k + 1) * 128],
                                                fxn[:, k * 512:(k + 1) * 512],
                                                start=(k == 0), stop=(k == KD - 1)),
              reads=("astb", f"fxn{k}"), writes=("pA",), sem=("cT" if k == KD - 1 else None))
        E("scalar", lambda e: e.copy(axsb[:, :], pA.ap()[:, :]), reads=("pA",), writes=("axsb",), sem="cA")
        E("vector", lambda e: e.tensor_tensor(out=axw[:, :], in0=axsb[:, :], in1=wexpb[:, :], op=ALU.mult),
          reads=("axsb", "wexpb"), writes=("axw",), sem="cV")
        for nm, buf, key in (("dbg_axs", axsb, "axsb"), ("dbg_wexp", wexpb, "wexpb"), ("dbg_axw", axw, "axw")):
            if nm in dbg_outs:
                E("sync", lambda e, nm=nm, buf=buf, n=n: e.dma_start(
                    out=dbg_outs[nm][:, n * 512:(n + 1) * 512], in_=buf[:, :]),
                  reads=(key,), sem="qdbg", n=16)
        for m in range(16):
            slot, sk = fslot[m % 2], f"fslot{m % 2}"
            E("gpsimd", lambda e, m=m, slot=slot: e.dma_start(
                out=slot[:, :].rearrange("p (k c) -> p k c", k=KD), in_=ftile[m]),
              writes=(sk,), sem=("qw2a" if m % 2 == 0 else "qw2b"), n=16)
            bank, bk = (pA, "pA") if m % 2 == 0 else (pB, "pB")
            if FFN_EXPERT_FIRST and not FFN_NO_EXPERT:
                E("tensor", lambda e, m=m, bank=bank: e.matmul(
                    bank.ap()[:, :], bstb[:, m * 128:(m + 1) * 128], axw[:, :],
                    start=True, stop=False),
                  reads=("bstb", "axw"), writes=(bk,), sem="cT")
            for k in range(KD):
                E("tensor", lambda e, k=k, slot=slot, bank=bank: e.matmul(
                    bank.ap()[:, :], slot[:, k * 128:(k + 1) * 128], fxn[:, k * 512:(k + 1) * 512],
                    start=(k == 0 and not FFN_EXPERT_FIRST),
                    stop=(k == KD - 1 and (FFN_NO_EXPERT or FFN_EXPERT_FIRST))),
                  reads=(sk, f"fxn{k}"), writes=(bk,),
                  sem=("cT" if (m == 15 or (k == KD - 1 and (FFN_NO_EXPERT or FFN_EXPERT_FIRST))) else None))
            if not FFN_NO_EXPERT and not FFN_EXPERT_FIRST:
                E("tensor", lambda e, m=m, bank=bank: e.matmul(
                    bank.ap()[:, :], bstb[:, m * 128:(m + 1) * 128], axw[:, :],
                    start=FFN_NO_BASE, stop=True),
                  reads=("bstb", "axw"), writes=(bk,), sem="cT")
            E("scalar", lambda e, bank=bank: e.activation(gst[:, :], bank.ap()[:, :], AF.Silu),
              reads=(bk,), writes=("gst",), sem="cA")
            E("sync", lambda e, m=m, n=n: e.dma_start(
                out=g_d.ap()[m * 128:(m + 1) * 128, n * 512:(n + 1) * 512], in_=gst[:, :]),
              reads=("gst",), writes=("g_d",), sem="qgw", n=16)

    for p in range(NPAIR):
        cmain_pair(p)
        if p % 4 == 3:
            ffn_nblock(p // 4)
    dbg_dump("dbg_y", lambda: y_tm_d.ap())
    dbg_dump("dbg_g", lambda: g_d.ap())
    if upto == "cmain":
        return _finish(nc, sch, ge, [bcm], out_d)

    sch.fence()
    bcm.close()
    gss.close()
    bop = contextlib.ExitStack()
    oslot = [TS_(f"oslot{i}", [128, 32 * 128], BF16, bop) for i in range(16)]
    ybuf = [TS_(f"ybuf{i}", [128, 512], BF16, bop) for i in range(2)]
    sst = TS_("sst", [128, 512], BF16, bop)
    gbuf = TS_("gbuf", [128, 512], BF16, bop)
    sgst = TS_("sgst", [128, 512], BF16, bop)
    x5 = [TS_("x50", [128, D], F32, bop)]
    sgb = [TS_("sgb0", [128, D], BF16, bop)]
    sgf = TS_("sgf", [128, D], F32, bop)
    ob = [TS_("ob0", [128, D], F32, bop)]

    for m in range(16):
        slot, sk = oslot[m], f"oslot{m}"
        E("gpsimd", lambda e, m=m, slot=slot: e.dma_start(
            out=slot[:, :].rearrange("p (k c) -> p k c", k=32), in_=optile[m]),
          writes=(sk,), sem="qw3", n=16)
    sch.bump(tuple(f"oslot{m}" for m in range(16)), "qw3")
    for m in range(16):
        slot, sk = oslot[m], f"oslot{m}"
        E("vector", lambda e, slot=slot: e.tensor_tensor(
            out=slot[:, :].rearrange("p (k c) -> p k c", k=32),
            in0=slot[:, :].rearrange("p (k c) -> p k c", k=32),
            in1=bass.AP(nwb, 0, [[32, 128], [1, 32], [0, 128]]), op=ALU.mult),
          reads=(sk, "nwb"), writes=(sk,), sem="cV")

    for n in range(NT):
        for mg in range(4):
            for k in range(32):
                yb, yk = ybuf[k % 2], f"ybuf{k % 2}"
                E("sync", lambda e, k=k, n=n, yb=yb: e.dma_start_transpose(
                    yb[:, :], y_tm_d.ap()[n * 512:(n + 1) * 512, k * 128:(k + 1) * 128]),
                  reads=("y_d",), writes=(yk,), sem=("qyr0" if k % 2 == 0 else "qyr1"), n=16)
                for i in range(4):
                    m = mg * 4 + i
                    E("tensor", lambda e, m=m, i=i, k=k, yb=yb: e.matmul(
                        pb[i].ap()[:, :], oslot[m][:, k * 128:(k + 1) * 128], yb[:, :],
                        start=(k == 0), stop=(k == 31)),
                      reads=(f"oslot{m}", yk), writes=(f"pb{i}",),
                      sem=("cT" if (i == 3 or k == 31) else None))
            for i in range(4):
                m = mg * 4 + i
                E("scalar", lambda e, i=i: e.copy(sst[:, :], pb[i].ap()[:, :]),
                  reads=(f"pb{i}",), writes=("sst",), sem="cA")
                E("gpsimd", lambda e, m=m, n=n: e.dma_start(
                    out=gbuf[:, :], in_=g_d.ap()[m * 128:(m + 1) * 128, n * 512:(n + 1) * 512]),
                  reads=("g_d",), writes=("gbuf",), sem="qgr", n=16)
                E("vector", lambda e: e.tensor_tensor(out=sgst[:, :], in0=sst[:, :], in1=gbuf[:, :],
                                                      op=ALU.mult),
                  reads=("sst", "gbuf"), writes=("sgst",), sem="cV")
                E("sync", lambda e, m=m, n=n: e.dma_start(
                    out=sg_d.ap()[m * 128:(m + 1) * 128, n * 512:(n + 1) * 512], in_=sgst[:, :]),
                  reads=("sgst",), writes=("sg_d",), sem="qsg", n=16)
        for tt in range(4):
            i = n * 4 + tt
            xb, xk = x5[0], "x50"
            sb, sk = sgb[0], "sgb0"
            o, ok = ob[0], "ob0"
            E("gpsimd", lambda e, i=i, xb=xb: e.dma_start(out=xb[:, :],
                                                          in_=xin[3 + i * 128: 3 + (i + 1) * 128, :]),
              writes=(xk,), sem=("qx2" if i % 2 == 0 else "qx3"), n=16)
            E("sync", lambda e, i=i, sb=sb: e.dma_start_transpose(sb[:, :],
                                                                   sg_d.ap()[:, i * 128:(i + 1) * 128]),
              reads=("sg_d",), writes=(sk,), sem=("qsr0" if i % 2 == 0 else "qsr1"), n=16)
            E("scalar", lambda e, sb=sb: e.copy(sgf[:, :], sb[:, :]),
              reads=(sk,), writes=("sgf",), sem="cA")
            E("vector", lambda e, xb=xb, o=o: e.tensor_tensor(out=o[:, :], in0=xb[:, :], in1=sgf[:, :],
                                                              op=ALU.add),
              reads=(xk, "sgf"), writes=(ok,), sem="cV")
            E("gpsimd", lambda e, i=i, o=o: e.dma_start(out=out_d[i * 128:(i + 1) * 128, :], in_=o[:, :]),
              reads=(ok,), writes=("outw",), sem="qout", n=16)
    dbg_dump("dbg_sg", lambda: sg_d.ap())
    return _finish(nc, sch, ge, [bop], out_d)



def emit_dt_chain(sch, aDT, aSC, aAC, aDC, ascol, dtb_s, acol, ones64, identf,
                  atp, eab, dcyb, dscale, acumT_d, asr_d, rowb, ones1, pA, pB):
    E = sch.emit
    E("vector", lambda e: e.tensor_scalar_add(aDT[:, :], aDT[:, :], dtb_s[:, :]),
      reads=("a1lo", "dtb"), writes=("a1lo",), sem="cV")
    E("scalar", lambda e: e.activation(aSC[:, :], aDT[:, :], AF.Abs),
      reads=("a1lo",), writes=("a2lo",), sem="cA")
    E("scalar", lambda e: e.activation(aSC[:, :], aSC[:, :], AF.Exp, scale=-1.0),
      reads=("a2lo",), writes=("a2lo",))
    E("scalar", lambda e: e.activation(aSC[:, :], aSC[:, :], AF.Ln, bias=1.0),
      reads=("a2lo",), writes=("a2lo",), sem="cA")
    E("vector", lambda e: e.tensor_scalar_max(aDT[:, :], aDT[:, :], 0.0),
      reads=("a1lo",), writes=("a1lo",), sem="cV")
    E("vector", lambda e: e.tensor_tensor(out=aDT[:, :], in0=aDT[:, :], in1=aSC[:, :], op=ALU.add),
      reads=("a1lo", "a2lo"), writes=("a1lo",), sem="cV")
    E("vector", lambda e: e.tensor_scalar_mul(aSC[:, :], aDT[:, :], acol[:, :]),
      reads=("a1lo", "acol"), writes=("a2lo",), sem="cV")
    for c in range(NCH):
        E("vector", lambda e, c=c: e.tensor_tensor_scan(
            out=aAC[:, c * 64:(c + 1) * 64],
            data0=ones64[:, 0:1].broadcast_to([64, 64]),
            data1=aSC[:, c * 64:(c + 1) * 64],
            initial=0.0, op0=ALU.mult, op1=ALU.add),
          reads=("a2lo", "ones64"), writes=("a2hi",), sem="cV", hard=(c == NCH - 1))
    for c in range(NCH):
        E("vector", lambda e, c=c: e.tensor_scalar(
            out=aDC[:, c * 64:(c + 1) * 64], in0=aAC[:, c * 64:(c + 1) * 64],
            scalar1=aAC[:, c * 64 + 63:c * 64 + 64], scalar2=None, op0=ALU.subtract),
          reads=("a2hi",), writes=("adc",), sem="cV", hard=(c == NCH - 1))
    E("scalar", lambda e: e.activation(aDC[:, :], aDC[:, :], AF.Exp, scale=-1.0),
      reads=("adc",), writes=("adc",), sem="cA")
    E("vector", lambda e: e.tensor_tensor(out=aDC[:, :], in0=aDC[:, :], in1=aDT[:, :], op=ALU.mult),
      reads=("adc", "a1lo"), writes=("adc",), sem="cV")
    E("scalar", lambda e: e.activation(aSC[:, :], aDT[:, :], AF.Ln),
      reads=("a1lo", "a2lo"), writes=("a2lo",), sem="cA")
    E("vector", lambda e: e.tensor_tensor(out=aSC[:, :], in0=aAC[:, :], in1=aSC[:, :],
                                          op=ALU.subtract),
      reads=("a2hi", "a2lo"), writes=("a2lo",), sem="cV")
    for p in range(NPAIR):
        E("tensor", lambda e, p=p: e.transpose(pA.ap()[0:128, 0:64], aAC[:, p * 128:(p + 1) * 128],
                                               identf[0:64, 0:64]),
          reads=("a2hi", "identf"), writes=("pA",), sem="cT")
        E("vector", lambda e, p=p: e.tensor_copy(atp[:, p * 64:(p + 1) * 64], pA.ap()[0:128, 0:64]),
          reads=("pA",), writes=("atp_t",), sem="cV")
        E("sync", lambda e, p=p: e.dma_start(out=acumT_d.ap()[p * 128:(p + 1) * 128, :],
                                             in_=atp[:, p * 64:(p + 1) * 64]),
          reads=("atp_t",), writes=("acumT_d",), sem="qat", n=16)
        E("scalar", lambda e, p=p: e.activation(eab[:, p * 64:(p + 1) * 64], pA.ap()[0:128, 0:64], AF.Exp),
          reads=("pA",), writes=("eab",), sem="cA", hard=True)
        E("tensor", lambda e, p=p: e.transpose(pB.ap()[0:128, 0:64], aSC[:, p * 128:(p + 1) * 128],
                                               identf[0:64, 0:64]),
          reads=("a2lo", "identf"), writes=("pB",), sem="cT")
        E("vector", lambda e, p=p: e.tensor_copy(atp[:, p * 64:(p + 1) * 64], pB.ap()[0:128, 0:64]),
          reads=("pB",), writes=("atp", "atp_t"), sem="cV", hard=True)
    for p in range(NPAIR):
        E("tensor", lambda e, p=p: e.transpose(pA.ap()[0:128, 0:64], aDC[:, p * 128:(p + 1) * 128],
                                               identf[0:64, 0:64]),
          reads=("adc", "identf"), writes=("pA",), sem="cT")
        E("scalar", lambda e, p=p: e.copy(dcyb[:, p * 64:(p + 1) * 64], pA.ap()[0:128, 0:64]),
          reads=("pA",), writes=("dcyb",), sem="cA", hard=True)
    E("tensor", lambda e: e.transpose(pB.ap()[0:32, 0:64],
                                      bass.AP(aAC, 63, [[T, 64], [64, 32]]),
                                      identf[0:64, 0:64]),
      reads=("a2hi", "identf"), writes=("pB",), sem="cT")
    E("vector", lambda e: e.tensor_copy(dscale[0:32, 0:64], pB.ap()[0:32, 0:64]),
      reads=("pB",), writes=("dsc_t",), sem="cV")
    E("sync", lambda e: e.dma_start(out=asr_d.ap(), in_=dscale[0:32, 0:64]),
      reads=("dsc_t",), writes=("asr_d",), sem="qat", n=16)
    E("sync", lambda e: e.dma_start(out=rowb[0:1, :], in_=bass.AP(asr_d, 0, [[0, 1], [1, 2048]])),
      reads=("asr_d",), writes=("rowb",), sem="qat", n=16)
    E("scalar", lambda e: e.activation(rowb[0:1, :], rowb[0:1, :], AF.Exp),
      reads=("rowb",), writes=("rowb",), sem="cA")
    for j in range(4):
        E("tensor", lambda e, j=j: e.matmul(pA.ap()[:, :], ones1[:, :], rowb[0:1, j * 512:(j + 1) * 512],
                                            start=True, stop=True),
          reads=("rowb", "ones1", "dsc_t"), writes=("pA",), sem="cT")
        E("vector", lambda e, j=j: e.tensor_copy(dscale[:, j * 512:(j + 1) * 512], pA.ap()[:, :]),
          reads=("pA",), writes=("dscale",), sem="cV")


def emit_T_and_scan(sch, S, dscale, btpc, btk, xdc, xdk, pb, c, sbf=None, sbfk=None):
    E = sch.emit
    E("vector", lambda e, c=c: e.tensor_tensor(
        out=S[:, :].rearrange("n (h q) -> n h q", h=NH),
        in0=S[:, :].rearrange("n (h q) -> n h q", h=NH),
        in1=bass.AP(dscale, c * 64, [[NCH * 64, 128], [1, 64], [0, 64]]), op=ALU.mult),
      reads=("S", "dscale"), writes=("S",), sem="cV")
    for j in range(8):
        bank_i = 4 + j % 2
        bank, bk = pb[bank_i], f"pb{bank_i}"
        E("tensor", lambda e, j=j, bank=bank, btpc=btpc, xdc=xdc: e.matmul(
            bank.ap()[:, :], btpc[0:64, :], xdc[0:64, j * 512:(j + 1) * 512],
            start=True, stop=True),
          reads=(btk, xdk), writes=(bk,), sem="cT")
        E("vector", lambda e, j=j, bank=bank: e.tensor_tensor(
            out=S[:, j * 512:(j + 1) * 512], in0=S[:, j * 512:(j + 1) * 512], in1=bank.ap()[:, :],
            op=ALU.add),
          reads=(bk, "S"), writes=("S",), sem="cV")
    if sbf is not None:
        E("scalar", lambda e, sbf=sbf: e.copy(sbf[:, :], S[:, :]),
          reads=("S",), writes=(sbfk,), sem="cA")


def _finish(nc, sch, ge, stacks, out_d):
    n_out = sch.count.get("qout", 0)
    if n_out:
        sch.q["gpsimd"].append(lambda e, h=sch.sems["qout"], v=n_out: e.wait_ge(h, v))
    v = sch.count.get("qdbg", 0)
    if v:
        sch.q["gpsimd"].append(lambda e, h=sch.sems["qdbg"], v=v: e.wait_ge(h, v))
    with nc.Block() as block:
        @block.sync
        def _(eng):
            for f in sch.q["sync"]:
                f(eng)

        @block.scalar
        def _(eng):
            for f in sch.q["scalar"]:
                f(eng)

        @block.vector
        def _(eng):
            for f in sch.q["vector"]:
                f(eng)

        @block.tensor
        def _(eng):
            for f in sch.q["tensor"]:
                f(eng)

        @block.gpsimd
        def _(eng):
            for f in sch.q["gpsimd"]:
                f(eng)
    for st in stacks:
        try:
            st.close()
        except Exception:
            pass
    try:
        ge.close()
    except Exception:
        pass
    return nc



def _prep_inputs(inputs):
    x = np.ascontiguousarray(inputs["L0_attn_x"][0])
    W = np.asarray(inputs["L0_attn_in_proj_w"])
    W_pad = np.concatenate([W, np.zeros((64, D), np.float32)], 0)
    wtile = np.ascontiguousarray(W_pad.reshape(67, 128, KD, 128).transpose(0, 3, 2, 1))
    cw = np.asarray(inputs["L0_attn_conv_w"])[:, 0, :]
    cwd = np.zeros((34, 128, 4, 128), np.float32)
    idx = np.arange(128)
    cwr = cw.reshape(34, 128, 4)
    cwd[:, idx, :, idx] = cwr.transpose(1, 0, 2)
    convb = np.ascontiguousarray(np.asarray(inputs["L0_attn_conv_b"]).reshape(34, 128).T)
    dtb = np.ascontiguousarray(np.asarray(inputs["L0_attn_dt_bias"]).reshape(64, 1))
    alog = np.ascontiguousarray(np.asarray(inputs["L0_attn_A_log"]).reshape(64, 1))
    Dv = np.asarray(inputs["L0_attn_D"])
    dmat = np.zeros((128, 64, 64), np.float32)
    for s_ in range(128):
        dmat[s_, s_ % 64, :] = Dv
    dmat = dmat.reshape(128, DI)
    nw = np.ascontiguousarray(np.asarray(inputs["L0_attn_norm_w"]).reshape(32, 128).T)
    OP = np.asarray(inputs["L0_attn_out_proj_w"])
    optile = np.ascontiguousarray(OP.reshape(16, 128, 32, 128).transpose(0, 3, 2, 1))
    FW = np.asarray(inputs["L0_ffn_weight"])
    ftile = np.ascontiguousarray(FW.reshape(16, 128, KD, 128).transpose(0, 3, 2, 1))
    A = np.asarray(inputs["L0_ffn_lora_A"])
    ast = np.ascontiguousarray(A.transpose(2, 0, 1).reshape(KD, 128, 128).transpose(1, 0, 2))
    B = np.asarray(inputs["L0_ffn_lora_B"])
    bst = np.ascontiguousarray(B.transpose(0, 2, 1).reshape(128, 16, 128))
    RW = np.asarray(inputs["L0_ffn_router_w"])
    rwt = np.ascontiguousarray(RW.T.reshape(KD, 128, 8).transpose(1, 0, 2))
    identb = np.eye(128, dtype=ml_dtypes.bfloat16)
    identf = np.eye(128, dtype=np.float32)
    triu = (np.arange(64)[None, :] >= (np.arange(128) % 64)[:, None]).astype(np.float32)
    triu = np.ascontiguousarray(triu)
    dblk = np.zeros((8, 128), np.float32)
    for e_ in range(8):
        dblk[e_, e_ * 16:(e_ + 1) * 16] = 2.0
    maps = []
    for c in range(NCORES):
        r0 = c * T
        halo = x[r0 - 3:r0] if c > 0 else np.zeros((3, D), np.float32)
        xin_ = np.ascontiguousarray(np.concatenate([halo, x[r0:r0 + T]], 0))
        hm = np.zeros((128, 7), np.float32)
        hm[:, :c] = 1.0
        maps.append({
            "xin": xin_, "wtile": wtile, "cwdiag": cwd, "convb": convb,
            "dtb": dtb, "alog": alog, "dmat": dmat, "nw_i": nw,
            "optile": optile, "ftile": ftile, "ast": ast, "bst": bst, "rwt": rwt,
            "identb_i": identb, "identf_i": identf, "triu_i": triu, "dblk": dblk,
            "hm": hm, "hm1": np.ascontiguousarray(1.0 - hm),
        })
    return maps


def kernel(**inputs):
    maps = _prep_inputs(inputs)
    nc = build_nc()
    res = run_bass_kernel_spmd(nc, maps, core_ids=list(range(NCORES)))
    out = np.concatenate([r["out"] for r in res.results], axis=0)
    return np.ascontiguousarray(out[None, :, :]).astype(np.float32)


# version 14
# speedup vs baseline: 1.0081x; 1.0081x over previous
import contextlib
import numpy as np
import ml_dtypes
import concourse.bass as bass
import concourse.mybir as mybir
from concourse.bass_utils import run_bass_kernel_spmd

F32 = mybir.dt.float32
BF16 = mybir.dt.bfloat16
ALU = mybir.AluOpType
AF = mybir.ActivationFunctionType
AXX = mybir.AxisListType.X

NCORES = 8
T = 2048
D = 2048
KD = 16
DI = 4096
NH = 64
BL = 64
NCH = 32
NPAIR = 16
NT = 4
NTT = 16
CLAMP = 60.0
FFN_NO_EXPERT = False
FFN_NO_BASE = False
FFN_EXPERT_FIRST = True

ENGS = ("sync", "scalar", "vector", "tensor", "gpsimd")


class Sched:
    def __init__(self, nc, es):
        self.nc = nc
        self.q = {e: [] for e in ENGS}
        self.sems = {}
        self.count = {}
        self.water = {}
        self.writer = {}
        self.readers = {}
        self.es = es

    def sem(self, name):
        h = self.es.enter_context(self.nc.semaphore(name))
        self.sems[name] = h
        self.count[name] = 0
        return h

    def _wait(self, eng, semname, val, src_eng=None, hard=False):
        if val <= 0:
            return
        if (not hard) and src_eng == eng and semname in ("cT", "cV", "cA", "cP"):
            return
        if self.water.get((eng, semname), 0) >= val:
            return
        self.water[(eng, semname)] = val
        h = self.sems[semname]
        self.q[eng].append(lambda e, h=h, v=val: e.wait_ge(h, v))

    def emit(self, eng, fn, reads=(), writes=(), sem=None, n=1, hard=False):
        for b in reads:
            w = self.writer.get(b)
            if w is not None:
                self._wait(eng, w[0], w[1], w[2], w[3])
        for b in writes:
            w = self.writer.get(b)
            if w is not None:
                self._wait(eng, w[0], w[1], w[2], w[3])
            for (s, v, se, hh) in self.readers.get(b, ()):
                self._wait(eng, s, v, se, hh)
        tok = None
        if sem is not None:
            self.count[sem] += n
            tok = (sem, self.count[sem], eng, hard)
            h = self.sems[sem]
            self.q[eng].append(lambda e, fn=fn, h=h, n=n: fn(e).then_inc(h, n))
        else:
            self.q[eng].append(lambda e, fn=fn: fn(e))
        if tok is not None:
            for b in writes:
                self.writer[b] = tok
                self.readers[b] = []
            for b in reads:
                self.readers.setdefault(b, []).append(tok)
        return tok

    def bump(self, keys, sem):
        v = self.count[sem]
        for k in keys:
            if k in self.writer and self.writer[k] is not None:
                self.writer[k] = (sem, v, self.writer[k][2], self.writer[k][3])

    def fence(self, engines=ENGS):
        snap = [(s, v) for s, v in self.count.items() if v > 0]
        for eng in engines:
            for s, v in snap:
                self._wait(eng, s, v, None, True)


def build_nc(dbg=(), upto="full"):
    nc = bass.Bass("TRN2", target_bir_lowering=False)
    ge = contextlib.ExitStack()

    def din(name, shape, dt=F32):
        return nc.dram_tensor(name, shape, dt, kind="ExternalInput").ap()

    xin = din("xin", [T + 3, D])
    wtile = din("wtile", [67, 128, KD, 128])
    cwdiag = din("cwdiag", [34, 128, 4, 128])
    convb_in = din("convb", [128, 34])
    dtb_in = din("dtb", [64, 1])
    alog_in = din("alog", [64, 1])
    dmat_in = din("dmat", [128, DI])
    nw_in = din("nw_i", [128, 32])
    optile = din("optile", [16, 128, 32, 128])
    ftile = din("ftile", [16, 128, KD, 128])
    ast_in = din("ast", [128, KD, 128])
    bst_in = din("bst", [128, 16, 128])
    rwt_in = din("rwt", [128, KD, 8])
    identb_in = din("identb_i", [128, 128], BF16)
    identf_in = din("identf_i", [128, 128])
    triu_in = din("triu_i", [128, 64])
    dblk_in = din("dblk", [8, 128])
    hm_in = din("hm", [128, 7])
    hm1_in = din("hm1", [128, 7])

    out_d = nc.dram_tensor("out", [T, D], F32, kind="ExternalOutput").ap()

    xn_tm_d = nc.dram_tensor("xn_tm_d", [T, D], BF16)
    z_d = nc.dram_tensor("z_d", [DI, T], BF16)
    xssm_d = nc.dram_tensor("xssm_d", [DI, T], BF16)
    xd_tm_d = nc.dram_tensor("xd_tm_d", [T, DI], BF16)
    acumT_d = nc.dram_tensor("acumT_d", [T, 64], F32)
    asr_d = nc.dram_tensor("asr_d", [32, 64], F32)
    y_tm_d = nc.dram_tensor("y_tm_d", [T, DI], BF16)
    g_d = nc.dram_tensor("g_d", [D, T], BF16)
    sg_d = nc.dram_tensor("sg_d", [D, T], BF16)
    cc_in_d = nc.dram_tensor("cc_in_d", [129, DI], F32)
    cc_out_d = nc.dram_tensor("cc_out_d", [NCORES * 129, DI], F32, addr_space="Shared")

    dbg_specs = {
        "dbg_xn": ([T, D], BF16), "dbg_dt": ([64, T], F32),
        "dbg_acum": ([64, T], F32), "dbg_bc": ([128, 2 * T], BF16),
        "dbg_xbc": ([DI, T], BF16), "dbg_z": ([DI, T], BF16),
        "dbg_S": ([128, DI], F32), "dbg_gtm": ([128, NPAIR * 64], F32),
        "dbg_y": ([T, DI], BF16), "dbg_wem": ([8, T], BF16),
        "dbg_g": ([D, T], BF16), "dbg_sg": ([D, T], BF16),
        "dbg_sin": ([128, DI], F32), "dbg_atp": ([128, NPAIR * 64], F32),
        "dbg_ypre": ([T, DI], BF16), "dbg_lg": ([T, 8], F32),
        "dbg_m": ([T, DI], BF16),
        "dbg_axs": ([128, T], BF16), "dbg_wexp": ([128, T], BF16), "dbg_axw": ([128, T], BF16),
        "dbg_eab": ([128, NPAIR * 64], F32), "dbg_dcy": ([128, NPAIR * 64], BF16),
        "dbg_dscale": ([128, NCH * 64], F32), "dbg_xd": ([T, DI], BF16),
    }
    dbg_outs = {}
    for name in dbg:
        shape, dt = dbg_specs[name]
        dbg_outs[name] = nc.dram_tensor(name, shape, dt, kind="ExternalOutput").ap()

    def TS_(name, shape, dt=F32, es=ge):
        return es.enter_context(nc.sbuf_tensor(name, shape, dt))

    identb = TS_("identb", [128, 128], BF16)
    identf = TS_("identf", [128, 128])
    triu = TS_("triu", [128, 64])
    nwb = TS_("nwb", [128, 32], BF16)
    nw = TS_("nw", [128, 32])
    convb_s = TS_("convb_s", [128, 34])
    dtb_s = TS_("dtb_s", [64, 1])
    acol = TS_("acol", [64, 1])
    ones1 = TS_("ones1", [1, 128])
    ones64 = TS_("ones64", [64, 1])
    epscol = TS_("epscol", [128, 1])
    rowb = TS_("rowb", [1, 2048])
    hmask = TS_("hmask", [128, 7])
    hmask1 = TS_("hmask1", [128, 7])
    xbchalo = TS_("xbchalo", [128, 34 * 3], BF16)
    xnhalo = TS_("xnhalo", [128, KD * 3], BF16)
    dblkb = TS_("dblkb", [8, 128], BF16)
    gss = contextlib.ExitStack()
    dmatb = TS_("dmatb", [128, DI], BF16, gss)
    bc = TS_("bc", [128, 2 * T], BF16, gss)
    S = TS_("S", [128, DI], F32, gss)
    atp = TS_("atp", [128, NPAIR * 64], F32, gss)
    eab = TS_("eab", [128, NPAIR * 64], F32, gss)
    dcyb = TS_("dcyb", [128, NPAIR * 64], BF16, gss)
    gtm = TS_("gtm", [128, NPAIR * 64], F32, gss)
    dscale = TS_("dscale", [128, NCH * 64], F32, gss)

    pe_ = contextlib.ExitStack()
    ge.enter_context(pe_)
    pb = [pe_.enter_context(nc.psum_tensor(f"pb{i}", [128, 512], F32)) for i in range(6)]
    pA = pe_.enter_context(nc.psum_tensor("pA", [128, 512], F32))
    pB = pe_.enter_context(nc.psum_tensor("pB", [128, 512], F32))
    pBb = pB.ap().bitcast(BF16)

    sch = Sched(nc, ge)
    for s in ("cT", "cV", "cA", "cP", "ccs"):
        sch.sem(s)
    for s in ("qc", "qx0", "qx1", "qxn0", "qxn1", "qxnr", "qw1a", "qw1b",
              "qcv0", "qcv1", "qst0", "qst1", "qst2", "qst3", "qxr", "qxd",
              "qat", "qcc", "qab", "qzr", "qyw", "qyr0", "qyr1", "qw2a", "qw2b", "qw3",
              "qgw", "qgr", "qsg", "qsr0", "qsr1", "qout", "qm", "qrb", "qxd2", "qx2", "qx3", "qdbg"):
        sch.sem(s)
    E = sch.emit

    def dbg_dump(name, src_fn, reads=()):
        if name in dbg_outs:
            sch.fence(("gpsimd",))
            E("gpsimd", lambda e, f=src_fn: e.dma_start(out=dbg_outs[name], in_=f()),
              reads=reads, sem="qdbg", n=16)

    for dst, src, key in ((identb, identb_in, "identb"), (identf, identf_in, "identf"),
                          (triu, triu_in, "triu"), (nw, nw_in, "nw"),
                          (convb_s, convb_in, "convb"), (dtb_s, dtb_in, "dtb"),
                          (hmask, hm_in, "hmask"), (hmask1, hm1_in, "hmask1")):
        E("sync", lambda e, d=dst, s=src: e.dma_start(out=d[:, :], in_=s),
          writes=(key,), sem="qc", n=16)
    E("sync", lambda e: e.dma_start(out=acol[:, :], in_=alog_in), writes=("acol",), sem="qc", n=16)
    E("sync", lambda e: e.dma_start(out=S[:, :], in_=dmat_in), writes=("S",), sem="qc", n=16)
    E("scalar", lambda e: e.copy(dmatb[:, :], S[:, :]), reads=("S",), writes=("dmatb",), sem="cA")
    E("scalar", lambda e: e.copy(nwb[:, :], nw[:, :]), reads=("nw",), writes=("nwb",), sem="cA")
    E("vector", lambda e: e.memset(ones1[:, :], 1.0), writes=("ones1",), sem="cV", hard=True)
    E("vector", lambda e: e.memset(ones64[:, :], 1.0), writes=("ones64",), sem="cV", hard=True)
    E("vector", lambda e: e.memset(epscol[:, :], 1e-5), writes=("epscol",), sem="cV", hard=True)
    E("scalar", lambda e: e.activation(acol[:, :], acol[:, :], AF.Exp), reads=("acol",), writes=("acol",), sem="cA", hard=True)
    E("vector", lambda e: e.tensor_scalar_mul(acol[:, :], acol[:, :], -1.0),
      reads=("acol",), writes=("acol",), sem="cV", hard=True)
    E("gpsimd", lambda e: e.dma_start(out=dblkb[:, :], in_=dblk_in), writes=("dblkb",), sem="qm", n=16)
    sch.bump(("identb", "identf", "triu", "nw", "convb", "dtb", "hmask", "hmask1", "acol", "S"), "qc")

    bln = contextlib.ExitStack()
    xt = [TS_(f"xt{i}", [128, D], F32, bln) for i in range(2)]
    lnscr = TS_("lnscr", [128, D], F32, bln)
    xnst = [TS_(f"xnst{i}", [128, D], BF16, bln) for i in range(2)]
    lncol = TS_("lncol", [128, 32], F32, bln)
    xnhalo_tm = TS_("xnhalo_tm", [128, D], BF16, bln)

    def ln_tile(rows0, nrows, buf, dram_row, sbuf_dst=None):
        xb = xt[buf]
        st = sbuf_dst if sbuf_dst is not None else xnst[buf]
        kx = f"xt{buf}"
        ks = "xnhalo_tm" if sbuf_dst is not None else f"xnst{buf}"
        E("sync", lambda e, r=rows0, n=nrows, xb=xb: e.dma_start(out=xb[0:n, :], in_=xin[r:r + n, :]),
          writes=(kx,), sem=f"qx{buf}", n=16)
        E("vector", lambda e, n=nrows, xb=xb: e.reduce_sum(lncol[0:n, 0:1], xb[0:n, :], axis=AXX),
          reads=(kx,), writes=("lnm",), sem="cV", hard=True)
        E("vector", lambda e, n=nrows: e.tensor_scalar_mul(lncol[0:n, 0:1], lncol[0:n, 0:1], 1.0 / D),
          reads=("lnm",), writes=("lnm",), sem="cV", hard=True)
        E("scalar", lambda e, n=nrows, xb=xb: e.activation(lnscr[0:n, :], xb[0:n, :], AF.Square),
          reads=(kx,), writes=("lnscr",), sem="cA")
        E("vector", lambda e, n=nrows: e.reduce_sum(lncol[0:n, 8:9], lnscr[0:n, :], axis=AXX),
          reads=("lnscr",), writes=("lnsq",), sem="cV", hard=True)
        E("vector", lambda e, n=nrows: e.tensor_scalar_mul(lncol[0:n, 16:17], lncol[0:n, 8:9], 1.0 / D),
          reads=("lnsq",), writes=("lnv",), sem="cV", hard=True)
        E("vector", lambda e, n=nrows: e.tensor_tensor(out=lncol[0:n, 24:25], in0=lncol[0:n, 0:1],
                                                       in1=lncol[0:n, 0:1], op=ALU.mult),
          reads=("lnm",), writes=("lnt",), sem="cV", hard=True)
        E("vector", lambda e, n=nrows: e.tensor_tensor(out=lncol[0:n, 16:17], in0=lncol[0:n, 16:17],
                                                       in1=lncol[0:n, 24:25], op=ALU.subtract),
          reads=("lnt", "lnv"), writes=("lnv",), sem="cV", hard=True)
        E("scalar", lambda e, n=nrows: e.activation(lncol[0:n, 16:17], lncol[0:n, 16:17], AF.Ln,
                                                    bias=epscol[0:n, :]),
          reads=("lnv", "epscol"), writes=("lnv",), sem="cA", hard=True)
        E("scalar", lambda e, n=nrows: e.activation(lncol[0:n, 16:17], lncol[0:n, 16:17], AF.Exp, scale=-0.5),
          reads=("lnv",), writes=("lnv",), sem="cA", hard=True)
        E("vector", lambda e, n=nrows, xb=xb, st=st: e.tensor_scalar(
            out=st[0:n, :], in0=xb[0:n, :], scalar1=lncol[0:n, 0:1], scalar2=lncol[0:n, 16:17],
            op0=ALU.subtract, op1=ALU.mult),
          reads=(kx, "lnm", "lnv"), writes=(ks,), sem="cV")
        if sbuf_dst is None:
            E("sync", lambda e, r=dram_row, n=nrows, st=st: e.dma_start(
                out=xn_tm_d.ap()[r:r + n, :], in_=st[0:n, :]),
              reads=(ks,), writes=("xn_d",), sem=f"qxn{buf}", n=16)

    for i in range(NTT):
        ln_tile(3 + i * 128, 128, i % 2, i * 128)
    ln_tile(0, 3, 0, 0, sbuf_dst=xnhalo_tm)
    for k in range(KD):
        E("tensor", lambda e, k=k: e.transpose(pBb[0:128, 0:3], xnhalo_tm[0:3, k * 128:(k + 1) * 128],
                                               identb[0:3, 0:3]),
          reads=("xnhalo_tm", "identb"), writes=("pB",), sem="cT")
        E("vector", lambda e, k=k: e.tensor_copy(xnhalo[:, k * 3:(k + 1) * 3], pBb[0:128, 0:3]),
          reads=("pB",), writes=("xnhalo",), sem="cV")

    dbg_dump("dbg_xn", lambda: xn_tm_d.ap())
    if upto == "ln":
        return _finish(nc, sch, ge, [bln], out_d)

    sch.fence()
    bln.close()
    bip = contextlib.ExitStack()
    xnf = TS_("xnf", [128, KD * T], BF16, bip)
    wslot = [TS_(f"wslot{i}", [128, KD * 128], BF16, bip) for i in range(2)]
    cvw = [TS_(f"cvw{i}", [128, 4 * 128], BF16, bip) for i in range(2)]
    preconv = TS_("preconv", [128, 3 + 1024], BF16, bip)
    aDT = TS_("aDT", [64, T], F32, bip)
    aSC = TS_("aSC", [64, T], F32, bip)
    aAC = TS_("aAC", [64, T], F32, bip)
    aDC = TS_("aDC", [64, T], F32, bip)
    ascol = TS_("ascol", [64, 8], F32, bip)
    stg = [TS_(f"stg{i}", [128, 512], BF16, bip) for i in range(4)]

    for k in range(KD):
        E("sync", lambda e, k=k: e.dma_start_transpose(xnf[:, k * T:(k + 1) * T],
                                                       xn_tm_d.ap()[:, k * 128:(k + 1) * 128]),
          reads=("xn_d",), writes=(f"xnf{k}",), sem="qxnr", n=16)
    sch.bump(tuple(f"xnf{k}" for k in range(KD)), "qxnr")

    def w_load(mc, slot_i):
        slot, sk = wslot[slot_i], f"wslot{slot_i}"
        E("gpsimd", lambda e, mc=mc, slot=slot: e.dma_start(
            out=slot[:, :].rearrange("p (k c) -> p k c", k=KD), in_=wtile[mc]),
          writes=(sk,), sem=("qw1a" if slot_i == 0 else "qw1b"), n=16)

    for i, mc in enumerate(range(32, 66)):
        w_load(mc, i % 2)
        slot, sk = wslot[i % 2], f"wslot{i % 2}"
        bank, bk = pb[i % 2], f"pb{i % 2}"
        for k in range(KD):
            E("tensor", lambda e, k=k, slot=slot, bank=bank: e.matmul(
                bank.ap()[:, 0:3], slot[:, k * 128:(k + 1) * 128],
                xnhalo[:, k * 3:(k + 1) * 3], start=(k == 0), stop=(k == KD - 1)),
              reads=(sk, "xnhalo"), writes=(bk,), sem=("cT" if k == KD - 1 else None))
        E("scalar", lambda e, i=i, bank=bank: e.copy(xbchalo[:, i * 3:(i + 1) * 3], bank.ap()[:, 0:3]),
          reads=(bk,), writes=("xbchalo",), sem="cA", hard=True)

    xpair1 = TS_("xpair1", [128, DI], BF16, bip)
    xdpair1 = TS_("xdpair1", [128, DI], BF16, bip)
    xdc1 = TS_("xdc1", [64, DI], BF16, bip)
    btpc1 = TS_("btpc1", [64, 128], BF16, bip)

    def emit_sweep_pair(p):
        E("sync", lambda e, p=p: e.dma_start_transpose(xpair1[:, :],
                                                       xssm_d.ap()[:, p * 128:(p + 1) * 128]),
          reads=("xssm_d",), writes=("xpair1",), sem="qxr", n=16)
        E("vector", lambda e, p=p: e.tensor_tensor(
            out=xdpair1[:, :].rearrange("s (h q) -> s h q", h=NH),
            in0=xpair1[:, :].rearrange("s (h q) -> s h q", h=NH),
            in1=bass.AP(dcyb, p * 64, [[NPAIR * 64, 128], [1, 64], [0, 64]]),
            op=ALU.mult),
          reads=("xpair1", "dcyb"), writes=("xdpair1",), sem="cV")
        E("sync", lambda e, p=p: e.dma_start(out=xd_tm_d.ap()[p * 128:(p + 1) * 128, :], in_=xdpair1[:, :]),
          reads=("xdpair1",), writes=("xd_d",), sem="qxd", n=16)
        for ci in range(2):
            c = 2 * p + ci
            E("tensor", lambda e, c=c: e.transpose(pBb[0:64, 0:128], bc[:, c * 64:(c + 1) * 64],
                                                   identb[:, :]),
              reads=("bc", "identb"), writes=("pB",), sem="cT")
            E("vector", lambda e: e.tensor_copy(btpc1[:, :], pBb[0:64, 0:128]),
              reads=("pB",), writes=("btpc1",), sem="cV")
            E("sync", lambda e, c=c: e.dma_start(out=xdc1[:, :], in_=xd_tm_d.ap()[c * 64:(c + 1) * 64, :]),
              reads=("xd_d",), writes=("xdc1",), sem="qxd", n=16)
            emit_T_and_scan(sch, S, dscale, btpc1, "btpc1", xdc1, "xdc1", pb, c)
            tp = (0, 64) if ci == 1 else None
            E("tensor", lambda e, c=c, ci=ci, tp=tp: e.matmul(
                pA.ap()[64 * ci:64 * ci + 64, 0:64], bc[:, c * 64:(c + 1) * 64],
                bc[:, T + c * 64: T + (c + 1) * 64], start=True, stop=True,
                tile_position=tp),
              reads=("bc",), writes=("pA",), sem=("cT" if ci == 1 else None))
            if ci == 1:
                E("vector", lambda e, p=p: e.tensor_tensor(out=gtm[:, p * 64:(p + 1) * 64],
                                                           in0=pA.ap()[:, 0:64],
                                                           in1=triu[:, :], op=ALU.mult),
                  reads=("pA", "triu"), writes=("gtm",), sem="cV", hard=True)
    dbg_dump("dbg_S", lambda: S[:, :], reads=("S",))
    dbg_dump("dbg_gtm", lambda: gtm[:, :], reads=("gtm",))
    dbg_dump("dbg_xd", lambda: xd_tm_d.ap())
    if upto == "sweep1":
        return _finish(nc, sch, ge, [bip], out_d)


    mc_order = [66] + list(range(32, 66)) + list(range(0, 32))
    bank_rot = 0
    stg_rot = 0

    def emit_conv(ci, n, cs, csk):
        nonlocal stg_rot
        off = 512 * (n % 2)
        if n % 2 == 0 and n > 0:
            E("scalar", lambda e: e.copy(preconv[:, 0:3], preconv[:, 1024:1027]),
              reads=("preconv",), writes=("pchead",), sem="cA", hard=True)
        cbank_i = 4 + (n % 2)
        cbank, cbk = pb[cbank_i], f"pb{cbank_i}"
        for tap in range(4):
            E("tensor", lambda e, tap=tap, off=off, cs=cs, cbank=cbank: e.matmul(
                cbank.ap()[:, :], cs[:, tap * 128:(tap + 1) * 128],
                preconv[:, off + tap:off + tap + 512], start=(tap == 0), stop=(tap == 3)),
              reads=(csk, "preconv", "pchead"), writes=(cbk,),
              sem=("cT" if tap == 3 else None))
        si = stg_rot % 4
        st_, stk = stg[si], f"stg{si}"
        stg_rot += 1
        E("scalar", lambda e, ci=ci, cbank=cbank, st_=st_: e.activation(
            st_[:, :], cbank.ap()[:, :], AF.Silu, bias=convb_s[:, ci:ci + 1]),
          reads=(cbk, "convb"), writes=(stk,), sem="cA")
        if ci < 32:
            E("sync", lambda e, ci=ci, n=n, st_=st_: e.dma_start(
                out=xssm_d.ap()[ci * 128:(ci + 1) * 128, n * 512:(n + 1) * 512], in_=st_[:, :]),
              reads=(stk,), writes=("xssm_d",), sem=f"qst{si}", n=16)
        else:
            boff = (ci - 32) * T
            E("vector", lambda e, boff=boff, n=n, st_=st_: e.tensor_copy(
                bc[:, boff + n * 512: boff + (n + 1) * 512], st_[:, :]),
              reads=(stk,), writes=("bc",), sem="cV")

    for mi, mc in enumerate(mc_order):
        w_load(mc, mi % 2)
        slot, sk = wslot[mi % 2], f"wslot{mi % 2}"
        is_xbc = 32 <= mc < 66
        ci = mc - 32 if is_xbc else None
        if is_xbc:
            cs, csk = cvw[mi % 2], f"cvw{mi % 2}"
            E("gpsimd", lambda e, ci=ci, cs=cs: e.dma_start(
                out=cs[:, :].rearrange("p (t c) -> p t c", t=4), in_=cwdiag[ci]),
              writes=(csk,), sem=("qcv0" if mi % 2 == 0 else "qcv1"), n=16)
            E("scalar", lambda e, ci=ci: e.copy(preconv[:, 0:3], xbchalo[:, ci * 3:(ci + 1) * 3]),
              reads=("xbchalo",), writes=("pchead",), sem="cA", hard=True)
        for n in range(NT + 1):
            if n < NT:
                bank_i = bank_rot % 4
                bank, bk = pb[bank_i], f"pb{bank_i}"
                bank_rot += 1
                for k in range(KD):
                    E("tensor", lambda e, k=k, n=n, slot=slot, bank=bank: e.matmul(
                        bank.ap()[:, :], slot[:, k * 128:(k + 1) * 128],
                        xnf[:, k * T + n * 512: k * T + (n + 1) * 512],
                        start=(k == 0), stop=(k == KD - 1)),
                      reads=(sk, f"xnf{k}"), writes=(bk,), sem=("cT" if k == KD - 1 else None))
            if is_xbc and n >= 1:
                emit_conv(ci, n - 1, cs, csk)
            if n < NT:
                if mc == 66:
                    E("vector", lambda e, n=n, bank=bank: e.tensor_copy(
                        aDT[:, n * 512:(n + 1) * 512], bank.ap()[0:64, :]),
                      reads=(bk,), writes=("a1lo",), sem="cV")
                elif is_xbc:
                    E("scalar", lambda e, n=n, bank=bank: e.copy(
                        preconv[:, 3 + 512 * (n % 2):3 + 512 * (n % 2) + 512], bank.ap()[:, :]),
                      reads=(bk, "pchead"), writes=("preconv",), sem="cA", hard=True)
                else:
                    si = stg_rot % 4
                    st_, stk = stg[si], f"stg{si}"
                    stg_rot += 1
                    E("scalar", lambda e, bank=bank, st_=st_: e.copy(st_[:, :], bank.ap()[:, :]),
                      reads=(bk,), writes=(stk,), sem="cA")
                    E("sync", lambda e, mc=mc, n=n, st_=st_: e.dma_start(
                        out=z_d.ap()[mc * 128:(mc + 1) * 128, n * 512:(n + 1) * 512], in_=st_[:, :]),
                      reads=(stk,), writes=("z_d",), sem=f"qst{si}", n=16)
        if mc == 66:
            emit_dt_chain(sch, aDT, aSC, aAC, aDC, ascol, dtb_s, acol, ones64, identf,
                          atp, eab, dcyb, dscale, acumT_d, asr_d, rowb, ones1, pA, pB)
        if mc < 32 and mc % 2 == 1:
            if mc == 1:
                E("vector", lambda e: e.memset(S[:, :], 0.0), reads=("S",), writes=("S",), sem="cV")
            emit_sweep_pair(mc // 2)

    dbg_dump("dbg_dt", lambda: aDT[:, :], reads=("a1lo",))
    dbg_dump("dbg_acum", lambda: aAC[:, :], reads=("a2hi",))
    dbg_dump("dbg_atp", lambda: atp[:, :], reads=("atp",))
    dbg_dump("dbg_eab", lambda: eab[:, :], reads=("eab",))
    dbg_dump("dbg_dcy", lambda: dcyb[:, :], reads=("dcyb",))
    dbg_dump("dbg_dscale", lambda: dscale[:, :], reads=("dscale",))
    dbg_dump("dbg_bc", lambda: bc[:, :], reads=("bc",))
    dbg_dump("dbg_xbc", lambda: xssm_d.ap())
    dbg_dump("dbg_z", lambda: z_d.ap())
    if upto == "inproj":
        return _finish(nc, sch, ge, [bip], out_d)

    E("sync", lambda e: e.dma_start(out=cc_in_d.ap()[0:128, :], in_=S[:, :]),
      reads=("S",), writes=("ccin",), sem="qcc", n=16)
    E("vector", lambda e: e.reduce_sum(
        ascol[:, 0:1],
        bass.AP(aAC, 63, [[T, 64], [64, 32]]), axis=AXX),
      reads=("a2hi",), writes=("asumcore",), sem="cV", hard=True)
    E("tensor", lambda e: e.transpose(pA.ap()[0:1, 0:64], ascol[:, 0:1], identf[0:64, 0:64]),
      reads=("asumcore", "identf"), writes=("pA",), sem="cT")
    E("vector", lambda e: e.tensor_copy(rowb[0:1, 0:64], pA.ap()[0:1, 0:64]),
      reads=("pA",), writes=("rowb",), sem="cV")
    E("sync", lambda e: e.dma_start(out=cc_in_d.ap()[128:129, 0:64], in_=rowb[0:1, 0:64]),
      reads=("rowb",), writes=("ccin",), sem="qcc", n=16)
    sch.fence(("gpsimd",))
    E("gpsimd", lambda e: e.collective_compute(
        "AllGather", ALU.bypass, replica_groups=[list(range(NCORES))],
        ins=[cc_in_d.ap()], outs=[cc_out_d.ap()]),
      reads=("ccin",), writes=("ccout",), sem="ccs", n=1)

    sch.fence()
    bip.close()
    bho = contextlib.ExitStack()
    tjb = [TS_("tjb0", [128, DI], F32, bho)]
    dbc = TS_("dbc", [128, 64], F32, bho)
    E("vector", lambda e: e.memset(S[:, :], 0.0), reads=("S",), writes=("S",), sem="cV")
    for j in range(7):
        tj, tk = tjb[0], "tjb0"
        E("sync", lambda e, j=j, tj=tj: e.dma_start(out=tj[:, :],
                                                    in_=cc_out_d.ap()[j * 129:j * 129 + 128, :]),
          reads=("ccout",), writes=(tk,), sem="qcc", n=16)
        E("sync", lambda e, j=j: e.dma_start(out=rowb[0:1, 0:64],
                                             in_=cc_out_d.ap()[j * 129 + 128:j * 129 + 129, 0:64]),
          reads=("ccout",), writes=("rowb",), sem="qrb", n=16)
        E("scalar", lambda e: e.activation(rowb[0:1, 0:64], rowb[0:1, 0:64], AF.Exp),
          reads=("rowb",), writes=("rowb",), sem="cA")
        E("tensor", lambda e: e.matmul(pA.ap()[:, 0:64], ones1[:, :], rowb[0:1, 0:64],
                                       start=True, stop=True),
          reads=("rowb", "ones1"), writes=("pA",), sem="cT")
        E("vector", lambda e, j=j: e.tensor_scalar(out=dbc[:, :], in0=pA.ap()[:, 0:64],
                                                   scalar1=hmask[:, j:j + 1], scalar2=hmask1[:, j:j + 1],
                                                   op0=ALU.mult, op1=ALU.add),
          reads=("pA", "hmask", "hmask1"), writes=("dbc",), sem="cV", hard=True)
        E("vector", lambda e: e.tensor_tensor(
            out=S[:, :].rearrange("n (h q) -> n h q", h=NH),
            in0=S[:, :].rearrange("n (h q) -> n h q", h=NH),
            in1=bass.AP(dbc, 0, [[64, 128], [1, 64], [0, 64]]), op=ALU.mult),
          reads=("S", "dbc"), writes=("S",), sem="cV")
        E("vector", lambda e, j=j, tj=tj: e.scalar_tensor_tensor(
            out=S[:, :], in0=tj[:, :], scalar=hmask[:, j:j + 1], in1=S[:, :],
            op0=ALU.mult, op1=ALU.add),
          reads=(tk, "S", "hmask"), writes=("S",), sem="cV")
    dbg_dump("dbg_sin", lambda: S[:, :], reads=("S",))
    if upto == "exchange":
        return _finish(nc, sch, ge, [bho], out_d)

    sch.fence()
    bho.close()
    bcm = contextlib.ExitStack()
    sbf = [TS_(f"sbf{i}", [128, DI], BF16, bcm) for i in range(2)]
    abbuf = TS_("abbuf", [128, DI], F32, bcm)
    argbuf = TS_("argbuf", [128, DI], F32, bcm)
    mbuf = [TS_(f"mbuf{i}", [128, DI], BF16, bcm) for i in range(2)]
    xbuf_ = [TS_("xbufA", [128, DI], BF16, bcm)]
    xdc2_ = [TS_(f"xdc2{i}", [64, DI], BF16, bcm) for i in range(2)]
    zbuf_ = [TS_("zbufA", [128, DI], BF16, bcm)]
    t1 = argbuf
    ystage = TS_("ystage", [128, DI], BF16, bcm)
    btpc2 = TS_("btpc2", [64, 128], BF16, bcm)
    rmsc = TS_("rmsc", [128, 16], F32, bcm)
    fxn = TS_("fxn", [128, KD * 512], BF16, bcm)
    fslot = [TS_(f"fslot{i}", [128, KD * 128], BF16, bcm) for i in range(2)]
    axsb = TS_("axsb", [128, 512], BF16, bcm)
    axw = TS_("axw", [128, 512], BF16, bcm)
    wexpb = TS_("wexpb", [128, 512], BF16, bcm)
    wem = TS_("wem", [8, 512], BF16, bcm)
    astb = TS_("astb", [128, KD * 128], BF16, bcm)
    bstb = TS_("bstb", [128, 16 * 128], BF16, bcm)
    rwtb = TS_("rwtb", [128, KD * 8], BF16, bcm)
    fsc = TS_("fsc", [128, 40], F32, bcm)
    fb2 = TS_("fb2", [128, 8], F32, bcm)
    gst = TS_("gst", [128, 512], BF16, bcm)

    E("gpsimd", lambda e: e.dma_start(out=astb[:, :].rearrange("p (k c) -> p k c", k=KD), in_=ast_in),
      writes=("astb",), sem="qm", n=16)
    E("gpsimd", lambda e: e.dma_start(out=bstb[:, :].rearrange("p (m c) -> p m c", m=16), in_=bst_in),
      writes=("bstb",), sem="qm", n=16)
    E("gpsimd", lambda e: e.dma_start(out=rwtb[:, :].rearrange("p (k c) -> p k c", k=KD), in_=rwt_in),
      writes=("rwtb",), sem="qm", n=16)
    sch.bump(("dblkb", "astb", "bstb", "rwtb"), "qm")

    def cmain_pair(p):
        xbuf, xbk = xbuf_[0], "xbufA"
        zbuf, zbk = zbuf_[0], "zbufA"
        E("gpsimd", lambda e, p=p: e.dma_start(
            out=abbuf[0:64, :], in_=bass.AP(acumT_d, (2 * p) * 64 * 64, [[0, 64], [64, 64], [1, 64]])),
          reads=("acumT_d",), writes=("abbuf",), sem="qab", n=16)
        E("gpsimd", lambda e, p=p: e.dma_start(
            out=abbuf[64:128, :], in_=bass.AP(acumT_d, (2 * p + 1) * 64 * 64, [[0, 64], [64, 64], [1, 64]])),
          reads=("acumT_d",), writes=("abbuf",), sem="qab", n=16)
        E("sync", lambda e, p=p, xbuf=xbuf: e.dma_start_transpose(xbuf[:, :], xssm_d.ap()[:, p * 128:(p + 1) * 128]),
          reads=("xssm_d",), writes=(xbk,), sem="qxr", n=16)
        E("sync", lambda e, p=p, zbuf=zbuf: e.dma_start_transpose(zbuf[:, :], z_d.ap()[:, p * 128:(p + 1) * 128]),
          reads=("z_d",), writes=(zbk,), sem="qzr", n=16)
        mb, mk = mbuf[p % 2], f"mbuf{p % 2}"
        E("vector", lambda e, p=p: e.tensor_tensor(
            out=argbuf[:, :].rearrange("s (l h) -> s l h", l=64),
            in0=abbuf[:, :].rearrange("s (l h) -> s l h", l=64),
            in1=bass.AP(atp, p * 64, [[NPAIR * 64, 128], [0, 64], [1, 64]]),
            op=ALU.subtract),
          reads=("abbuf", "atp"), writes=("argbuf",), sem="cV")
        E("vector", lambda e: e.tensor_scalar_min(argbuf[:, :], argbuf[:, :], CLAMP),
          reads=("argbuf",), writes=("argbuf",), sem="cV", hard=True)
        E("scalar", lambda e: e.activation(argbuf[:, :], argbuf[:, :], AF.Exp),
          reads=("argbuf",), writes=("argbuf",), sem="cA")
        E("vector", lambda e, p=p, mb=mb, mk=mk: e.tensor_tensor(
            out=mb[:, :].rearrange("s (l h) -> s l h", l=64),
            in0=argbuf[:, :].rearrange("s (l h) -> s l h", l=64),
            in1=bass.AP(gtm, p * 64, [[NPAIR * 64, 128], [1, 64], [0, 64]]),
            op=ALU.mult),
          reads=("argbuf", "gtm"), writes=(mk,), sem="cV")
        E("vector", lambda e, mb=mb, mk=mk: e.tensor_tensor(out=mb[:, :], in0=mb[:, :], in1=dmatb[:, :],
                                                            op=ALU.add),
          reads=("dmatb", mk), writes=(mk,), sem="cV")
        if "dbg_m" in dbg_outs:
            E("sync", lambda e, p=p, mb=mb: e.dma_start(out=dbg_outs["dbg_m"][p * 128:(p + 1) * 128, :],
                                                        in_=mb[:, :]),
              reads=(mk,), sem="qdbg", n=16)
        for ci in range(2):
            c = 2 * p + ci
            E("tensor", lambda e, c=c: e.transpose(pBb[0:64, 0:128], bc[:, c * 64:(c + 1) * 64],
                                                   identb[:, :]),
              reads=("bc", "identb"), writes=("pB",), sem="cT")
            E("vector", lambda e: e.tensor_copy(btpc2[:, :], pBb[0:64, 0:128]),
              reads=("pB",), writes=("btpc2",), sem="cV")
            xdc2, xdk2 = xdc2_[ci], f"xdc2{ci}"
            E("gpsimd", lambda e, c=c, xdc2=xdc2: e.dma_start(out=xdc2[:, :], in_=xd_tm_d.ap()[c * 64:(c + 1) * 64, :]),
              reads=("xd_d",), writes=(xdk2,), sem="qxd2", n=16)
            emit_T_and_scan(sch, S, dscale, btpc2, "btpc2", xdc2, xdk2, pb, c,
                            sbf=sbf[ci], sbfk=f"sbf{ci}")
        for j in range(8):
            ybank, ybk = pb[j % 2], f"pb{j % 2}"
            obank, obk = pb[2 + j % 2], f"pb{2 + j % 2}"
            for ci in range(2):
                r0 = 64 * ci
                tpd = (64, 64) if ci == 1 else None
                for h8 in range(8):
                    h = 8 * j + h8
                    E("tensor", lambda e, h=h, h8=h8, r0=r0, mb=mb, mk=mk, ybank=ybank, tp=tpd,
                      first=(h8 == 0), last=(h8 == 7): e.matmul(
                        ybank.ap()[r0:r0 + 64, h8 * 64:(h8 + 1) * 64],
                        mb[:, :].rearrange("s (l h2) -> s l h2", l=64)[r0:r0 + 64, :, h],
                        xbuf[r0:r0 + 64, h * 64:(h + 1) * 64],
                        start=first, stop=last, tile_position=tp),
                      reads=(mk, xbk), writes=(ybk,),
                      sem=("cT" if (ci == 1 and h8 == 7) else None))
            for ci in range(2):
                c = 2 * p + ci
                tpo = (0, 64) if ci == 1 else None
                E("tensor", lambda e, j=j, c=c, ci=ci, obank=obank, tp=tpo: e.matmul(
                    obank.ap()[64 * ci:64 * ci + 64, :],
                    bc[:, T + c * 64: T + (c + 1) * 64],
                    sbf[ci][:, j * 512:(j + 1) * 512],
                    start=True, stop=True, tile_position=tp),
                  reads=("bc", f"sbf{ci}"), writes=(obk,),
                  sem=("cT" if ci == 1 else None))
            E("vector", lambda e, j=j, p=p, obank=obank: e.tensor_tensor(
                out=t1[:, j * 512:(j + 1) * 512].rearrange("s (h q) -> s h q", h=8),
                in0=obank.ap()[:, :].rearrange("s (h q) -> s h q", h=8),
                in1=bass.AP(eab, p * 64 + j * 8, [[NPAIR * 64, 128], [1, 8], [0, 64]]),
                op=ALU.mult),
              reads=(obk, "eab"), writes=("argbuf",), sem="cV")
            E("vector", lambda e, j=j, ybank=ybank: e.tensor_tensor(
                out=ystage[:, j * 512:(j + 1) * 512], in0=t1[:, j * 512:(j + 1) * 512],
                in1=ybank.ap()[:, :], op=ALU.add),
              reads=(ybk, "argbuf"), writes=("ystage",), sem="cV")
        if "dbg_ypre" in dbg_outs:
            E("sync", lambda e, p=p: e.dma_start(out=dbg_outs["dbg_ypre"][p * 128:(p + 1) * 128, :],
                                                 in_=ystage[:, :]),
              reads=("ystage",), sem="qdbg", n=16)
        E("scalar", lambda e, zbuf=zbuf: e.activation(zbuf[:, :], zbuf[:, :], AF.Silu),
          reads=(zbk,), writes=(zbk,), sem="cA")
        E("vector", lambda e, zbuf=zbuf: e.tensor_tensor(out=ystage[:, :], in0=ystage[:, :], in1=zbuf[:, :],
                                                         op=ALU.mult),
          reads=(zbk,), writes=("ystage",), sem="cV")
        E("scalar", lambda e: e.activation(t1[:, :], ystage[:, :], AF.Square),
          reads=("ystage",), writes=("argbuf",), sem="cA")
        E("vector", lambda e: e.reduce_sum(rmsc[:, 0:1], t1[:, :], axis=AXX),
          reads=("argbuf",), writes=("rmsc",), sem="cV", hard=True)
        E("vector", lambda e: e.tensor_scalar(out=rmsc[:, 8:9], in0=rmsc[:, 0:1], scalar1=1.0 / DI,
                                              scalar2=1e-5, op0=ALU.mult, op1=ALU.add),
          reads=("rmsc",), writes=("rmsc2",), sem="cV", hard=True)
        E("scalar", lambda e: e.activation(rmsc[:, 8:9], rmsc[:, 8:9], AF.Ln),
          reads=("rmsc2", "epscol"), writes=("rmsc2",), sem="cA", hard=True)
        E("scalar", lambda e: e.activation(rmsc[:, 8:9], rmsc[:, 8:9], AF.Exp, scale=-0.5),
          reads=("rmsc2",), writes=("rmsc2",), sem="cA", hard=True)
        E("vector", lambda e: e.tensor_scalar_mul(ystage[:, :], ystage[:, :], rmsc[:, 8:9]),
          reads=("rmsc2",), writes=("ystage",), sem="cV")
        E("gpsimd", lambda e, p=p: e.dma_start(out=y_tm_d.ap()[p * 128:(p + 1) * 128, :], in_=ystage[:, :]),
          reads=("ystage",), writes=("y_d",), sem="qyw", n=16)

    def ffn_prelude(n):
        for k in range(KD):
            E("sync", lambda e, k=k, n=n: e.dma_start_transpose(
                fxn[:, k * 512:(k + 1) * 512],
                xn_tm_d.ap()[n * 512:(n + 1) * 512, k * 128:(k + 1) * 128]),
              reads=("xn_d",), writes=(f"fxn{k}",), sem="qxnr", n=16)
        sch.bump(tuple(f"fxn{k}" for k in range(KD)), "qxnr")
        for tt in range(4):
            for k in range(KD):
                E("tensor", lambda e, k=k, tt=tt: e.matmul(
                    pB.ap()[:, 0:8], fxn[:, k * 512 + tt * 128: k * 512 + (tt + 1) * 128],
                    rwtb[:, k * 8:(k + 1) * 8], start=(k == 0), stop=(k == KD - 1)),
                  reads=(f"fxn{k}", "rwtb"), writes=("pB",), sem=("cT" if k == KD - 1 else None))
            E("vector", lambda e: e.tensor_copy(fsc[:, 0:8], pB.ap()[:, 0:8]),
              reads=("pB",), writes=("flg",), sem="cV", hard=True)
            if "dbg_lg" in dbg_outs:
                E("sync", lambda e, n=n, tt=tt: e.dma_start(
                    out=dbg_outs["dbg_lg"][n * 512 + tt * 128: n * 512 + (tt + 1) * 128, :],
                    in_=fsc[:, 0:8]),
                  reads=("flg",), sem="qdbg", n=16)
            E("vector", lambda e: e.reduce_max(fsc[:, 8:9], fsc[:, 0:8], axis=AXX),
              reads=("flg",), writes=("fm1",), sem="cV", hard=True)
            E("vector", lambda e: e.tensor_tensor(out=fb2[:, :], in0=fsc[:, 0:8],
                                                  in1=fsc[:, 8:9].broadcast_to([128, 8]),
                                                  op=ALU.is_ge),
              reads=("flg", "fm1"), writes=("fb2",), sem="cV", hard=True)
            E("vector", lambda e: e.scalar_tensor_tensor(out=fsc[:, 0:8], in0=fb2[:, :], scalar=-1e30,
                                                         in1=fsc[:, 0:8], op0=ALU.mult, op1=ALU.add),
              reads=("fb2", "flg"), writes=("flg",), sem="cV", hard=True)
            E("vector", lambda e: e.reduce_max(fsc[:, 16:17], fsc[:, 0:8], axis=AXX),
              reads=("flg",), writes=("fm2",), sem="cV", hard=True)
            E("vector", lambda e: e.tensor_tensor(out=fsc[:, 0:8], in0=fsc[:, 0:8],
                                                  in1=fsc[:, 16:17].broadcast_to([128, 8]),
                                                  op=ALU.is_ge),
              reads=("flg", "fm2"), writes=("flg",), sem="cV", hard=True)
            E("vector", lambda e: e.tensor_tensor(out=fsc[:, 24:25], in0=fsc[:, 8:9], in1=fsc[:, 16:17],
                                                  op=ALU.subtract),
              reads=("fm1", "fm2"), writes=("fw",), sem="cV", hard=True)
            E("scalar", lambda e: e.activation(fsc[:, 24:25], fsc[:, 24:25], AF.Sigmoid),
              reads=("fw",), writes=("fw",), sem="cA", hard=True)
            E("vector", lambda e: e.tensor_scalar(out=fsc[:, 32:33], in0=fsc[:, 24:25], scalar1=-1.0,
                                                  scalar2=1.0, op0=ALU.mult, op1=ALU.add),
              reads=("fw",), writes=("fw2",), sem="cV", hard=True)
            E("vector", lambda e: e.tensor_scalar_mul(fb2[:, :], fb2[:, :], fsc[:, 24:25]),
              reads=("fb2", "fw"), writes=("fb2",), sem="cV", hard=True)
            E("vector", lambda e: e.scalar_tensor_tensor(out=fsc[:, 0:8], in0=fsc[:, 0:8],
                                                         scalar=fsc[:, 32:33], in1=fb2[:, :],
                                                         op0=ALU.mult, op1=ALU.add),
              reads=("fb2", "fw2", "flg"), writes=("flg",), sem="cV", hard=True)
            E("tensor", lambda e: e.transpose(pA.ap()[0:8, 0:128], fsc[:, 0:8], identf[:, :]),
              reads=("flg", "identf"), writes=("pA",), sem="cT")
            E("scalar", lambda e, tt=tt: e.copy(wem[0:8, tt * 128:(tt + 1) * 128], pA.ap()[0:8, 0:128]),
              reads=("pA",), writes=("wem",), sem="cA")
        dbg_dump("dbg_wem", (lambda n=n: (lambda: dbg_outs["dbg_wem"]))(), reads=("wem",)) if False else None
        if "dbg_wem" in dbg_outs:
            sch.fence(("gpsimd",))
            E("gpsimd", lambda e, n=n: e.dma_start(out=dbg_outs["dbg_wem"][0:8, n * 512:(n + 1) * 512],
                                                   in_=wem[0:8, :]),
              reads=("wem",), sem="qdbg", n=16)
        E("tensor", lambda e: e.matmul(pB.ap()[:, :], dblkb[:, :], wem[0:8, :], start=True, stop=True),
          reads=("dblkb", "wem"), writes=("pB",), sem="cT")
        E("scalar", lambda e: e.copy(wexpb[:, :], pB.ap()[:, :]), reads=("pB",), writes=("wexpb",), sem="cA")
        for k in range(KD):
            E("tensor", lambda e, k=k: e.matmul(pA.ap()[:, :], astb[:, k * 128:(k + 1) * 128],
                                                fxn[:, k * 512:(k + 1) * 512],
                                                start=(k == 0), stop=(k == KD - 1)),
              reads=("astb", f"fxn{k}"), writes=("pA",), sem=("cT" if k == KD - 1 else None))
        E("scalar", lambda e: e.copy(axsb[:, :], pA.ap()[:, :]), reads=("pA",), writes=("axsb",), sem="cA")
        E("vector", lambda e: e.tensor_tensor(out=axw[:, :], in0=axsb[:, :], in1=wexpb[:, :], op=ALU.mult),
          reads=("axsb", "wexpb"), writes=("axw",), sem="cV")
        for nm, buf, key in (("dbg_axs", axsb, "axsb"), ("dbg_wexp", wexpb, "wexpb"), ("dbg_axw", axw, "axw")):
            if nm in dbg_outs:
                E("sync", lambda e, nm=nm, buf=buf, n=n: e.dma_start(
                    out=dbg_outs[nm][:, n * 512:(n + 1) * 512], in_=buf[:, :]),
                  reads=(key,), sem="qdbg", n=16)
    def ffn_ms(n, ms):
        for m in ms:
            slot, sk = fslot[m % 2], f"fslot{m % 2}"
            E("gpsimd", lambda e, m=m, slot=slot: e.dma_start(
                out=slot[:, :].rearrange("p (k c) -> p k c", k=KD), in_=ftile[m]),
              writes=(sk,), sem=("qw2a" if m % 2 == 0 else "qw2b"), n=16)
            bank, bk = (pA, "pA") if m % 2 == 0 else (pB, "pB")
            if FFN_EXPERT_FIRST and not FFN_NO_EXPERT:
                E("tensor", lambda e, m=m, bank=bank: e.matmul(
                    bank.ap()[:, :], bstb[:, m * 128:(m + 1) * 128], axw[:, :],
                    start=True, stop=False),
                  reads=("bstb", "axw"), writes=(bk,), sem="cT")
            for k in range(KD):
                E("tensor", lambda e, k=k, slot=slot, bank=bank: e.matmul(
                    bank.ap()[:, :], slot[:, k * 128:(k + 1) * 128], fxn[:, k * 512:(k + 1) * 512],
                    start=(k == 0 and not FFN_EXPERT_FIRST),
                    stop=(k == KD - 1 and (FFN_NO_EXPERT or FFN_EXPERT_FIRST))),
                  reads=(sk, f"fxn{k}"), writes=(bk,),
                  sem=("cT" if (m == 15 or (k == KD - 1 and (FFN_NO_EXPERT or FFN_EXPERT_FIRST))) else None))
            if not FFN_NO_EXPERT and not FFN_EXPERT_FIRST:
                E("tensor", lambda e, m=m, bank=bank: e.matmul(
                    bank.ap()[:, :], bstb[:, m * 128:(m + 1) * 128], axw[:, :],
                    start=FFN_NO_BASE, stop=True),
                  reads=("bstb", "axw"), writes=(bk,), sem="cT")
            E("scalar", lambda e, bank=bank: e.activation(gst[:, :], bank.ap()[:, :], AF.Silu),
              reads=(bk,), writes=("gst",), sem="cA")
            E("sync", lambda e, m=m, n=n: e.dma_start(
                out=g_d.ap()[m * 128:(m + 1) * 128, n * 512:(n + 1) * 512], in_=gst[:, :]),
              reads=("gst",), writes=("g_d",), sem="qgw", n=16)

    for p in range(NPAIR):
        cmain_pair(p)
        nb = p // 4
        if p % 4 == 0:
            ffn_prelude(nb)
            ffn_ms(nb, range(0, 4))
        elif p % 4 == 1:
            ffn_ms(nb, range(4, 8))
        elif p % 4 == 2:
            ffn_ms(nb, range(8, 12))
        else:
            ffn_ms(nb, range(12, 16))
    dbg_dump("dbg_y", lambda: y_tm_d.ap())
    dbg_dump("dbg_g", lambda: g_d.ap())
    if upto == "cmain":
        return _finish(nc, sch, ge, [bcm], out_d)

    sch.fence()
    bcm.close()
    gss.close()
    bop = contextlib.ExitStack()
    oslot = [TS_(f"oslot{i}", [128, 32 * 128], BF16, bop) for i in range(16)]
    ybuf = [TS_(f"ybuf{i}", [128, 512], BF16, bop) for i in range(2)]
    sst = TS_("sst", [128, 512], BF16, bop)
    gbuf = TS_("gbuf", [128, 512], BF16, bop)
    sgst = TS_("sgst", [128, 512], BF16, bop)
    x5 = [TS_("x50", [128, D], F32, bop)]
    sgb = [TS_("sgb0", [128, D], BF16, bop)]
    sgf = TS_("sgf", [128, D], F32, bop)
    ob = [TS_("ob0", [128, D], F32, bop)]

    for m in range(16):
        slot, sk = oslot[m], f"oslot{m}"
        E("gpsimd", lambda e, m=m, slot=slot: e.dma_start(
            out=slot[:, :].rearrange("p (k c) -> p k c", k=32), in_=optile[m]),
          writes=(sk,), sem="qw3", n=16)
    sch.bump(tuple(f"oslot{m}" for m in range(16)), "qw3")
    for m in range(16):
        slot, sk = oslot[m], f"oslot{m}"
        E("vector", lambda e, slot=slot: e.tensor_tensor(
            out=slot[:, :].rearrange("p (k c) -> p k c", k=32),
            in0=slot[:, :].rearrange("p (k c) -> p k c", k=32),
            in1=bass.AP(nwb, 0, [[32, 128], [1, 32], [0, 128]]), op=ALU.mult),
          reads=(sk, "nwb"), writes=(sk,), sem="cV")

    for n in range(NT):
        for mg in range(4):
            for k in range(32):
                yb, yk = ybuf[k % 2], f"ybuf{k % 2}"
                E("sync", lambda e, k=k, n=n, yb=yb: e.dma_start_transpose(
                    yb[:, :], y_tm_d.ap()[n * 512:(n + 1) * 512, k * 128:(k + 1) * 128]),
                  reads=("y_d",), writes=(yk,), sem=("qyr0" if k % 2 == 0 else "qyr1"), n=16)
                for i in range(4):
                    m = mg * 4 + i
                    E("tensor", lambda e, m=m, i=i, k=k, yb=yb: e.matmul(
                        pb[i].ap()[:, :], oslot[m][:, k * 128:(k + 1) * 128], yb[:, :],
                        start=(k == 0), stop=(k == 31)),
                      reads=(f"oslot{m}", yk), writes=(f"pb{i}",),
                      sem=("cT" if (i == 3 or k == 31) else None))
            for i in range(4):
                m = mg * 4 + i
                E("scalar", lambda e, i=i: e.copy(sst[:, :], pb[i].ap()[:, :]),
                  reads=(f"pb{i}",), writes=("sst",), sem="cA")
                E("gpsimd", lambda e, m=m, n=n: e.dma_start(
                    out=gbuf[:, :], in_=g_d.ap()[m * 128:(m + 1) * 128, n * 512:(n + 1) * 512]),
                  reads=("g_d",), writes=("gbuf",), sem="qgr", n=16)
                E("vector", lambda e: e.tensor_tensor(out=sgst[:, :], in0=sst[:, :], in1=gbuf[:, :],
                                                      op=ALU.mult),
                  reads=("sst", "gbuf"), writes=("sgst",), sem="cV")
                E("sync", lambda e, m=m, n=n: e.dma_start(
                    out=sg_d.ap()[m * 128:(m + 1) * 128, n * 512:(n + 1) * 512], in_=sgst[:, :]),
                  reads=("sgst",), writes=("sg_d",), sem="qsg", n=16)
        for tt in range(4):
            i = n * 4 + tt
            xb, xk = x5[0], "x50"
            sb, sk = sgb[0], "sgb0"
            o, ok = ob[0], "ob0"
            E("gpsimd", lambda e, i=i, xb=xb: e.dma_start(out=xb[:, :],
                                                          in_=xin[3 + i * 128: 3 + (i + 1) * 128, :]),
              writes=(xk,), sem=("qx2" if i % 2 == 0 else "qx3"), n=16)
            E("sync", lambda e, i=i, sb=sb: e.dma_start_transpose(sb[:, :],
                                                                   sg_d.ap()[:, i * 128:(i + 1) * 128]),
              reads=("sg_d",), writes=(sk,), sem=("qsr0" if i % 2 == 0 else "qsr1"), n=16)
            E("scalar", lambda e, sb=sb: e.copy(sgf[:, :], sb[:, :]),
              reads=(sk,), writes=("sgf",), sem="cA")
            E("vector", lambda e, xb=xb, o=o: e.tensor_tensor(out=o[:, :], in0=xb[:, :], in1=sgf[:, :],
                                                              op=ALU.add),
              reads=(xk, "sgf"), writes=(ok,), sem="cV")
            E("gpsimd", lambda e, i=i, o=o: e.dma_start(out=out_d[i * 128:(i + 1) * 128, :], in_=o[:, :]),
              reads=(ok,), writes=("outw",), sem="qout", n=16)
    dbg_dump("dbg_sg", lambda: sg_d.ap())
    return _finish(nc, sch, ge, [bop], out_d)



def emit_dt_chain(sch, aDT, aSC, aAC, aDC, ascol, dtb_s, acol, ones64, identf,
                  atp, eab, dcyb, dscale, acumT_d, asr_d, rowb, ones1, pA, pB):
    E = sch.emit
    E("vector", lambda e: e.tensor_scalar_add(aDT[:, :], aDT[:, :], dtb_s[:, :]),
      reads=("a1lo", "dtb"), writes=("a1lo",), sem="cV")
    E("scalar", lambda e: e.activation(aSC[:, :], aDT[:, :], AF.Abs),
      reads=("a1lo",), writes=("a2lo",), sem="cA")
    E("scalar", lambda e: e.activation(aSC[:, :], aSC[:, :], AF.Exp, scale=-1.0),
      reads=("a2lo",), writes=("a2lo",))
    E("scalar", lambda e: e.activation(aSC[:, :], aSC[:, :], AF.Ln, bias=1.0),
      reads=("a2lo",), writes=("a2lo",), sem="cA")
    E("vector", lambda e: e.tensor_scalar_max(aDT[:, :], aDT[:, :], 0.0),
      reads=("a1lo",), writes=("a1lo",), sem="cV")
    E("vector", lambda e: e.tensor_tensor(out=aDT[:, :], in0=aDT[:, :], in1=aSC[:, :], op=ALU.add),
      reads=("a1lo", "a2lo"), writes=("a1lo",), sem="cV")
    E("vector", lambda e: e.tensor_scalar_mul(aSC[:, :], aDT[:, :], acol[:, :]),
      reads=("a1lo", "acol"), writes=("a2lo",), sem="cV")
    for c in range(NCH):
        E("vector", lambda e, c=c: e.tensor_tensor_scan(
            out=aAC[:, c * 64:(c + 1) * 64],
            data0=ones64[:, 0:1].broadcast_to([64, 64]),
            data1=aSC[:, c * 64:(c + 1) * 64],
            initial=0.0, op0=ALU.mult, op1=ALU.add),
          reads=("a2lo", "ones64"), writes=("a2hi",), sem="cV", hard=(c == NCH - 1))
    for c in range(NCH):
        E("vector", lambda e, c=c: e.tensor_scalar(
            out=aDC[:, c * 64:(c + 1) * 64], in0=aAC[:, c * 64:(c + 1) * 64],
            scalar1=aAC[:, c * 64 + 63:c * 64 + 64], scalar2=None, op0=ALU.subtract),
          reads=("a2hi",), writes=("adc",), sem="cV", hard=(c == NCH - 1))
    E("scalar", lambda e: e.activation(aDC[:, :], aDC[:, :], AF.Exp, scale=-1.0),
      reads=("adc",), writes=("adc",), sem="cA")
    E("vector", lambda e: e.tensor_tensor(out=aDC[:, :], in0=aDC[:, :], in1=aDT[:, :], op=ALU.mult),
      reads=("adc", "a1lo"), writes=("adc",), sem="cV")
    E("scalar", lambda e: e.activation(aSC[:, :], aDT[:, :], AF.Ln),
      reads=("a1lo", "a2lo"), writes=("a2lo",), sem="cA")
    E("vector", lambda e: e.tensor_tensor(out=aSC[:, :], in0=aAC[:, :], in1=aSC[:, :],
                                          op=ALU.subtract),
      reads=("a2hi", "a2lo"), writes=("a2lo",), sem="cV")
    for p in range(NPAIR):
        E("tensor", lambda e, p=p: e.transpose(pA.ap()[0:128, 0:64], aAC[:, p * 128:(p + 1) * 128],
                                               identf[0:64, 0:64]),
          reads=("a2hi", "identf"), writes=("pA",), sem="cT")
        E("vector", lambda e, p=p: e.tensor_copy(atp[:, p * 64:(p + 1) * 64], pA.ap()[0:128, 0:64]),
          reads=("pA",), writes=("atp_t",), sem="cV")
        E("sync", lambda e, p=p: e.dma_start(out=acumT_d.ap()[p * 128:(p + 1) * 128, :],
                                             in_=atp[:, p * 64:(p + 1) * 64]),
          reads=("atp_t",), writes=("acumT_d",), sem="qat", n=16)
        E("scalar", lambda e, p=p: e.activation(eab[:, p * 64:(p + 1) * 64], pA.ap()[0:128, 0:64], AF.Exp),
          reads=("pA",), writes=("eab",), sem="cA", hard=True)
        E("tensor", lambda e, p=p: e.transpose(pB.ap()[0:128, 0:64], aSC[:, p * 128:(p + 1) * 128],
                                               identf[0:64, 0:64]),
          reads=("a2lo", "identf"), writes=("pB",), sem="cT")
        E("vector", lambda e, p=p: e.tensor_copy(atp[:, p * 64:(p + 1) * 64], pB.ap()[0:128, 0:64]),
          reads=("pB",), writes=("atp", "atp_t"), sem="cV", hard=True)
    for p in range(NPAIR):
        E("tensor", lambda e, p=p: e.transpose(pA.ap()[0:128, 0:64], aDC[:, p * 128:(p + 1) * 128],
                                               identf[0:64, 0:64]),
          reads=("adc", "identf"), writes=("pA",), sem="cT")
        E("scalar", lambda e, p=p: e.copy(dcyb[:, p * 64:(p + 1) * 64], pA.ap()[0:128, 0:64]),
          reads=("pA",), writes=("dcyb",), sem="cA", hard=True)
    E("tensor", lambda e: e.transpose(pB.ap()[0:32, 0:64],
                                      bass.AP(aAC, 63, [[T, 64], [64, 32]]),
                                      identf[0:64, 0:64]),
      reads=("a2hi", "identf"), writes=("pB",), sem="cT")
    E("vector", lambda e: e.tensor_copy(dscale[0:32, 0:64], pB.ap()[0:32, 0:64]),
      reads=("pB",), writes=("dsc_t",), sem="cV")
    E("sync", lambda e: e.dma_start(out=asr_d.ap(), in_=dscale[0:32, 0:64]),
      reads=("dsc_t",), writes=("asr_d",), sem="qat", n=16)
    E("sync", lambda e: e.dma_start(out=rowb[0:1, :], in_=bass.AP(asr_d, 0, [[0, 1], [1, 2048]])),
      reads=("asr_d",), writes=("rowb",), sem="qat", n=16)
    E("scalar", lambda e: e.activation(rowb[0:1, :], rowb[0:1, :], AF.Exp),
      reads=("rowb",), writes=("rowb",), sem="cA")
    for j in range(4):
        E("tensor", lambda e, j=j: e.matmul(pA.ap()[:, :], ones1[:, :], rowb[0:1, j * 512:(j + 1) * 512],
                                            start=True, stop=True),
          reads=("rowb", "ones1", "dsc_t"), writes=("pA",), sem="cT")
        E("vector", lambda e, j=j: e.tensor_copy(dscale[:, j * 512:(j + 1) * 512], pA.ap()[:, :]),
          reads=("pA",), writes=("dscale",), sem="cV")


def emit_T_and_scan(sch, S, dscale, btpc, btk, xdc, xdk, pb, c, sbf=None, sbfk=None):
    E = sch.emit
    E("vector", lambda e, c=c: e.tensor_tensor(
        out=S[:, :].rearrange("n (h q) -> n h q", h=NH),
        in0=S[:, :].rearrange("n (h q) -> n h q", h=NH),
        in1=bass.AP(dscale, c * 64, [[NCH * 64, 128], [1, 64], [0, 64]]), op=ALU.mult),
      reads=("S", "dscale"), writes=("S",), sem="cV")
    for j in range(8):
        bank_i = 4 + j % 2
        bank, bk = pb[bank_i], f"pb{bank_i}"
        E("tensor", lambda e, j=j, bank=bank, btpc=btpc, xdc=xdc: e.matmul(
            bank.ap()[:, :], btpc[0:64, :], xdc[0:64, j * 512:(j + 1) * 512],
            start=True, stop=True),
          reads=(btk, xdk), writes=(bk,), sem="cT")
        E("vector", lambda e, j=j, bank=bank: e.tensor_tensor(
            out=S[:, j * 512:(j + 1) * 512], in0=S[:, j * 512:(j + 1) * 512], in1=bank.ap()[:, :],
            op=ALU.add),
          reads=(bk, "S"), writes=("S",), sem="cV")
    if sbf is not None:
        E("scalar", lambda e, sbf=sbf: e.copy(sbf[:, :], S[:, :]),
          reads=("S",), writes=(sbfk,), sem="cA")


def _finish(nc, sch, ge, stacks, out_d):
    n_out = sch.count.get("qout", 0)
    if n_out:
        sch.q["gpsimd"].append(lambda e, h=sch.sems["qout"], v=n_out: e.wait_ge(h, v))
    v = sch.count.get("qdbg", 0)
    if v:
        sch.q["gpsimd"].append(lambda e, h=sch.sems["qdbg"], v=v: e.wait_ge(h, v))
    with nc.Block() as block:
        @block.sync
        def _(eng):
            for f in sch.q["sync"]:
                f(eng)

        @block.scalar
        def _(eng):
            for f in sch.q["scalar"]:
                f(eng)

        @block.vector
        def _(eng):
            for f in sch.q["vector"]:
                f(eng)

        @block.tensor
        def _(eng):
            for f in sch.q["tensor"]:
                f(eng)

        @block.gpsimd
        def _(eng):
            for f in sch.q["gpsimd"]:
                f(eng)
    for st in stacks:
        try:
            st.close()
        except Exception:
            pass
    try:
        ge.close()
    except Exception:
        pass
    return nc



def _prep_inputs(inputs):
    x = np.ascontiguousarray(inputs["L0_attn_x"][0])
    W = np.asarray(inputs["L0_attn_in_proj_w"])
    W_pad = np.concatenate([W, np.zeros((64, D), np.float32)], 0)
    wtile = np.ascontiguousarray(W_pad.reshape(67, 128, KD, 128).transpose(0, 3, 2, 1))
    cw = np.asarray(inputs["L0_attn_conv_w"])[:, 0, :]
    cwd = np.zeros((34, 128, 4, 128), np.float32)
    idx = np.arange(128)
    cwr = cw.reshape(34, 128, 4)
    cwd[:, idx, :, idx] = cwr.transpose(1, 0, 2)
    convb = np.ascontiguousarray(np.asarray(inputs["L0_attn_conv_b"]).reshape(34, 128).T)
    dtb = np.ascontiguousarray(np.asarray(inputs["L0_attn_dt_bias"]).reshape(64, 1))
    alog = np.ascontiguousarray(np.asarray(inputs["L0_attn_A_log"]).reshape(64, 1))
    Dv = np.asarray(inputs["L0_attn_D"])
    dmat = np.zeros((128, 64, 64), np.float32)
    for s_ in range(128):
        dmat[s_, s_ % 64, :] = Dv
    dmat = dmat.reshape(128, DI)
    nw = np.ascontiguousarray(np.asarray(inputs["L0_attn_norm_w"]).reshape(32, 128).T)
    OP = np.asarray(inputs["L0_attn_out_proj_w"])
    optile = np.ascontiguousarray(OP.reshape(16, 128, 32, 128).transpose(0, 3, 2, 1))
    FW = np.asarray(inputs["L0_ffn_weight"])
    ftile = np.ascontiguousarray(FW.reshape(16, 128, KD, 128).transpose(0, 3, 2, 1))
    A = np.asarray(inputs["L0_ffn_lora_A"])
    ast = np.ascontiguousarray(A.transpose(2, 0, 1).reshape(KD, 128, 128).transpose(1, 0, 2))
    B = np.asarray(inputs["L0_ffn_lora_B"])
    bst = np.ascontiguousarray(B.transpose(0, 2, 1).reshape(128, 16, 128))
    RW = np.asarray(inputs["L0_ffn_router_w"])
    rwt = np.ascontiguousarray(RW.T.reshape(KD, 128, 8).transpose(1, 0, 2))
    identb = np.eye(128, dtype=ml_dtypes.bfloat16)
    identf = np.eye(128, dtype=np.float32)
    triu = (np.arange(64)[None, :] >= (np.arange(128) % 64)[:, None]).astype(np.float32)
    triu = np.ascontiguousarray(triu)
    dblk = np.zeros((8, 128), np.float32)
    for e_ in range(8):
        dblk[e_, e_ * 16:(e_ + 1) * 16] = 2.0
    maps = []
    for c in range(NCORES):
        r0 = c * T
        halo = x[r0 - 3:r0] if c > 0 else np.zeros((3, D), np.float32)
        xin_ = np.ascontiguousarray(np.concatenate([halo, x[r0:r0 + T]], 0))
        hm = np.zeros((128, 7), np.float32)
        hm[:, :c] = 1.0
        maps.append({
            "xin": xin_, "wtile": wtile, "cwdiag": cwd, "convb": convb,
            "dtb": dtb, "alog": alog, "dmat": dmat, "nw_i": nw,
            "optile": optile, "ftile": ftile, "ast": ast, "bst": bst, "rwt": rwt,
            "identb_i": identb, "identf_i": identf, "triu_i": triu, "dblk": dblk,
            "hm": hm, "hm1": np.ascontiguousarray(1.0 - hm),
        })
    return maps


def kernel(**inputs):
    maps = _prep_inputs(inputs)
    nc = build_nc()
    res = run_bass_kernel_spmd(nc, maps, core_ids=list(range(NCORES)))
    out = np.concatenate([r["out"] for r in res.results], axis=0)
    return np.ascontiguousarray(out[None, :, :]).astype(np.float32)


# version 16 (final)
# speedup vs baseline: 1.0443x; 1.0359x over previous
import contextlib
import numpy as np
import ml_dtypes
import concourse.bass as bass
import concourse.mybir as mybir
from concourse.bass_utils import run_bass_kernel_spmd

F32 = mybir.dt.float32
BF16 = mybir.dt.bfloat16
ALU = mybir.AluOpType
AF = mybir.ActivationFunctionType
AXX = mybir.AxisListType.X

NCORES = 8
T = 2048
D = 2048
KD = 16
DI = 4096
NH = 64
BL = 64
NCH = 32
NPAIR = 16
NT = 4
NTT = 16
CLAMP = 60.0
FFN_NO_EXPERT = False
FFN_NO_BASE = False
FFN_EXPERT_FIRST = True

ENGS = ("sync", "scalar", "vector", "tensor", "gpsimd")


class Sched:
    def __init__(self, nc, es):
        self.nc = nc
        self.q = {e: [] for e in ENGS}
        self.sems = {}
        self.count = {}
        self.water = {}
        self.writer = {}
        self.readers = {}
        self.es = es

    def sem(self, name):
        h = self.es.enter_context(self.nc.semaphore(name))
        self.sems[name] = h
        self.count[name] = 0
        return h

    def _wait(self, eng, semname, val, src_eng=None, hard=False):
        if val <= 0:
            return
        if (not hard) and src_eng == eng and semname in ("cT", "cV", "cA", "cP"):
            return
        if self.water.get((eng, semname), 0) >= val:
            return
        self.water[(eng, semname)] = val
        h = self.sems[semname]
        self.q[eng].append(lambda e, h=h, v=val: e.wait_ge(h, v))

    def emit(self, eng, fn, reads=(), writes=(), sem=None, n=1, hard=False):
        for b in reads:
            w = self.writer.get(b)
            if w is not None:
                self._wait(eng, w[0], w[1], w[2], w[3])
        for b in writes:
            w = self.writer.get(b)
            if w is not None:
                self._wait(eng, w[0], w[1], w[2], w[3])
            for (s, v, se, hh) in self.readers.get(b, ()):
                self._wait(eng, s, v, se, hh)
        tok = None
        if sem is not None:
            self.count[sem] += n
            tok = (sem, self.count[sem], eng, hard)
            h = self.sems[sem]
            self.q[eng].append(lambda e, fn=fn, h=h, n=n: fn(e).then_inc(h, n))
        else:
            self.q[eng].append(lambda e, fn=fn: fn(e))
        if tok is not None:
            for b in writes:
                self.writer[b] = tok
                self.readers[b] = []
            for b in reads:
                self.readers.setdefault(b, []).append(tok)
        return tok

    def bump(self, keys, sem):
        v = self.count[sem]
        for k in keys:
            if k in self.writer and self.writer[k] is not None:
                self.writer[k] = (sem, v, self.writer[k][2], self.writer[k][3])

    def fence(self, engines=ENGS):
        snap = [(s, v) for s, v in self.count.items() if v > 0]
        for eng in engines:
            for s, v in snap:
                self._wait(eng, s, v, None, True)


def build_nc(dbg=(), upto="full"):
    nc = bass.Bass("TRN2", target_bir_lowering=False)
    ge = contextlib.ExitStack()

    def din(name, shape, dt=F32):
        return nc.dram_tensor(name, shape, dt, kind="ExternalInput").ap()

    xin = din("xin", [T + 3, D])
    wtile = din("wtile", [67, 128, KD, 128])
    cwdiag = din("cwdiag", [34, 128, 4, 128])
    convb_in = din("convb", [128, 34])
    dtb_in = din("dtb", [64, 1])
    alog_in = din("alog", [64, 1])
    dmat_in = din("dmat", [128, DI])
    nw_in = din("nw_i", [128, 32])
    optile = din("optile", [16, 128, 32, 128])
    ftile = din("ftile", [16, 128, KD, 128])
    ast_in = din("ast", [128, KD, 128])
    bst_in = din("bst", [128, 16, 128])
    rwt_in = din("rwt", [128, KD, 8])
    identb_in = din("identb_i", [128, 128], BF16)
    identf_in = din("identf_i", [128, 128])
    triu_in = din("triu_i", [128, 64])
    dblk_in = din("dblk", [8, 128])
    hm_in = din("hm", [128, 7])
    hm1_in = din("hm1", [128, 7])

    out_d = nc.dram_tensor("out", [T, D], F32, kind="ExternalOutput").ap()

    xn_tm_d = nc.dram_tensor("xn_tm_d", [T, D], BF16)
    z_d = nc.dram_tensor("z_d", [DI, T], BF16)
    xssm_d = nc.dram_tensor("xssm_d", [DI, T], BF16)
    xd_tm_d = nc.dram_tensor("xd_tm_d", [T, DI], BF16)
    acumT_d = nc.dram_tensor("acumT_d", [T, 64], F32)
    asr_d = nc.dram_tensor("asr_d", [32, 64], F32)
    y_tm_d = nc.dram_tensor("y_tm_d", [T, DI], BF16)
    g_d = nc.dram_tensor("g_d", [D, T], BF16)
    sg_d = nc.dram_tensor("sg_d", [D, T], BF16)
    cc_in_d = nc.dram_tensor("cc_in_d", [129, DI], F32)
    cc_out_d = nc.dram_tensor("cc_out_d", [NCORES * 129, DI], F32, addr_space="Shared")

    dbg_specs = {
        "dbg_xn": ([T, D], BF16), "dbg_dt": ([64, T], F32),
        "dbg_acum": ([64, T], F32), "dbg_bc": ([128, 2 * T], BF16),
        "dbg_xbc": ([DI, T], BF16), "dbg_z": ([DI, T], BF16),
        "dbg_S": ([128, DI], F32), "dbg_gtm": ([128, NPAIR * 64], F32),
        "dbg_y": ([T, DI], BF16), "dbg_wem": ([8, T], BF16),
        "dbg_g": ([D, T], BF16), "dbg_sg": ([D, T], BF16),
        "dbg_sin": ([128, DI], F32), "dbg_atp": ([128, NPAIR * 64], F32),
        "dbg_ypre": ([T, DI], BF16), "dbg_lg": ([T, 8], F32),
        "dbg_m": ([T, DI], BF16),
        "dbg_axs": ([128, T], BF16), "dbg_wexp": ([128, T], BF16), "dbg_axw": ([128, T], BF16),
        "dbg_eab": ([128, NPAIR * 64], F32), "dbg_dcy": ([128, NPAIR * 64], BF16),
        "dbg_dscale": ([128, NCH * 64], F32), "dbg_xd": ([T, DI], BF16),
    }
    dbg_outs = {}
    for name in dbg:
        shape, dt = dbg_specs[name]
        dbg_outs[name] = nc.dram_tensor(name, shape, dt, kind="ExternalOutput").ap()

    def TS_(name, shape, dt=F32, es=ge):
        return es.enter_context(nc.sbuf_tensor(name, shape, dt))

    identb = TS_("identb", [128, 128], BF16)
    identf = TS_("identf", [128, 128])
    triu = TS_("triu", [128, 64])
    nwb = TS_("nwb", [128, 32], BF16)
    nw = TS_("nw", [128, 32])
    convb_s = TS_("convb_s", [128, 34])
    dtb_s = TS_("dtb_s", [64, 1])
    acol = TS_("acol", [64, 1])
    ones1 = TS_("ones1", [1, 128])
    ones64 = TS_("ones64", [64, 1])
    epscol = TS_("epscol", [128, 1])
    rowb = TS_("rowb", [1, 2048])
    hmask = TS_("hmask", [128, 7])
    hmask1 = TS_("hmask1", [128, 7])
    xbchalo = TS_("xbchalo", [128, 34 * 3], BF16)
    xnhalo = TS_("xnhalo", [128, KD * 3], BF16)
    dblkb = TS_("dblkb", [8, 128], BF16)
    gss = contextlib.ExitStack()
    dmatb = TS_("dmatb", [128, DI], BF16, gss)
    bc = TS_("bc", [128, 2 * T], BF16, gss)
    S = TS_("S", [128, DI], F32, gss)
    atp = TS_("atp", [128, NPAIR * 64], F32, gss)
    eab = TS_("eab", [128, NPAIR * 64], F32, gss)
    dcyb = TS_("dcyb", [128, NPAIR * 64], BF16, gss)
    gtm = TS_("gtm", [128, NPAIR * 64], F32, gss)
    dscale = TS_("dscale", [128, NCH * 64], F32, gss)

    pe_ = contextlib.ExitStack()
    ge.enter_context(pe_)
    pb = [pe_.enter_context(nc.psum_tensor(f"pb{i}", [128, 512], F32)) for i in range(6)]
    pA = pe_.enter_context(nc.psum_tensor("pA", [128, 512], F32))
    pB = pe_.enter_context(nc.psum_tensor("pB", [128, 512], F32))
    pBb = pB.ap().bitcast(BF16)

    sch = Sched(nc, ge)
    for s in ("cT", "cV", "cA", "cP", "ccs"):
        sch.sem(s)
    for s in ("qc", "qx0", "qx1", "qxn0", "qxn1", "qxnr", "qw1a", "qw1b",
              "qcv0", "qcv1", "qst0", "qst1", "qst2", "qst3", "qxr", "qxd",
              "qat", "qcc", "qab", "qzr", "qyw", "qyr0", "qyr1", "qw2a", "qw2b", "qw3",
              "qgw", "qgr", "qsg", "qsr0", "qsr1", "qout", "qm", "qrb", "qxd2", "qx2", "qx3", "qdbg"):
        sch.sem(s)
    E = sch.emit

    def dbg_dump(name, src_fn, reads=()):
        if name in dbg_outs:
            sch.fence(("gpsimd",))
            E("gpsimd", lambda e, f=src_fn: e.dma_start(out=dbg_outs[name], in_=f()),
              reads=reads, sem="qdbg", n=16)

    for dst, src, key in ((identb, identb_in, "identb"), (identf, identf_in, "identf"),
                          (triu, triu_in, "triu"), (nw, nw_in, "nw"),
                          (convb_s, convb_in, "convb"), (dtb_s, dtb_in, "dtb"),
                          (hmask, hm_in, "hmask"), (hmask1, hm1_in, "hmask1")):
        E("sync", lambda e, d=dst, s=src: e.dma_start(out=d[:, :], in_=s),
          writes=(key,), sem="qc", n=16)
    E("sync", lambda e: e.dma_start(out=acol[:, :], in_=alog_in), writes=("acol",), sem="qc", n=16)
    E("sync", lambda e: e.dma_start(out=S[:, :], in_=dmat_in), writes=("S",), sem="qc", n=16)
    E("scalar", lambda e: e.copy(dmatb[:, :], S[:, :]), reads=("S",), writes=("dmatb",), sem="cA")
    E("scalar", lambda e: e.copy(nwb[:, :], nw[:, :]), reads=("nw",), writes=("nwb",), sem="cA")
    E("vector", lambda e: e.memset(ones1[:, :], 1.0), writes=("ones1",), sem="cV", hard=True)
    E("vector", lambda e: e.memset(ones64[:, :], 1.0), writes=("ones64",), sem="cV", hard=True)
    E("vector", lambda e: e.memset(epscol[:, :], 1e-5), writes=("epscol",), sem="cV", hard=True)
    E("scalar", lambda e: e.activation(acol[:, :], acol[:, :], AF.Exp), reads=("acol",), writes=("acol",), sem="cA", hard=True)
    E("vector", lambda e: e.tensor_scalar_mul(acol[:, :], acol[:, :], -1.0),
      reads=("acol",), writes=("acol",), sem="cV", hard=True)
    E("gpsimd", lambda e: e.dma_start(out=dblkb[:, :], in_=dblk_in), writes=("dblkb",), sem="qm", n=16)
    sch.bump(("identb", "identf", "triu", "nw", "convb", "dtb", "hmask", "hmask1", "acol", "S"), "qc")

    bln = contextlib.ExitStack()
    xt = [TS_(f"xt{i}", [128, D], F32, bln) for i in range(2)]
    lnscr = TS_("lnscr", [128, D], F32, bln)
    xnst = [TS_(f"xnst{i}", [128, D], BF16, bln) for i in range(2)]
    lncol = TS_("lncol", [128, 32], F32, bln)
    xnhalo_tm = TS_("xnhalo_tm", [128, D], BF16, bln)

    def ln_tile(rows0, nrows, buf, dram_row, sbuf_dst=None):
        xb = xt[buf]
        st = sbuf_dst if sbuf_dst is not None else xnst[buf]
        kx = f"xt{buf}"
        ks = "xnhalo_tm" if sbuf_dst is not None else f"xnst{buf}"
        E("sync", lambda e, r=rows0, n=nrows, xb=xb: e.dma_start(out=xb[0:n, :], in_=xin[r:r + n, :]),
          writes=(kx,), sem=f"qx{buf}", n=16)
        E("vector", lambda e, n=nrows, xb=xb: e.reduce_sum(lncol[0:n, 0:1], xb[0:n, :], axis=AXX),
          reads=(kx,), writes=("lnm",), sem="cV", hard=True)
        E("vector", lambda e, n=nrows: e.tensor_scalar_mul(lncol[0:n, 0:1], lncol[0:n, 0:1], 1.0 / D),
          reads=("lnm",), writes=("lnm",), sem="cV", hard=True)
        E("scalar", lambda e, n=nrows, xb=xb: e.activation(lnscr[0:n, :], xb[0:n, :], AF.Square),
          reads=(kx,), writes=("lnscr",), sem="cA")
        E("vector", lambda e, n=nrows: e.reduce_sum(lncol[0:n, 8:9], lnscr[0:n, :], axis=AXX),
          reads=("lnscr",), writes=("lnsq",), sem="cV", hard=True)
        E("vector", lambda e, n=nrows: e.tensor_scalar_mul(lncol[0:n, 16:17], lncol[0:n, 8:9], 1.0 / D),
          reads=("lnsq",), writes=("lnv",), sem="cV", hard=True)
        E("vector", lambda e, n=nrows: e.tensor_tensor(out=lncol[0:n, 24:25], in0=lncol[0:n, 0:1],
                                                       in1=lncol[0:n, 0:1], op=ALU.mult),
          reads=("lnm",), writes=("lnt",), sem="cV", hard=True)
        E("vector", lambda e, n=nrows: e.tensor_tensor(out=lncol[0:n, 16:17], in0=lncol[0:n, 16:17],
                                                       in1=lncol[0:n, 24:25], op=ALU.subtract),
          reads=("lnt", "lnv"), writes=("lnv",), sem="cV", hard=True)
        E("scalar", lambda e, n=nrows: e.activation(lncol[0:n, 16:17], lncol[0:n, 16:17], AF.Ln,
                                                    bias=epscol[0:n, :]),
          reads=("lnv", "epscol"), writes=("lnv",), sem="cA", hard=True)
        E("scalar", lambda e, n=nrows: e.activation(lncol[0:n, 16:17], lncol[0:n, 16:17], AF.Exp, scale=-0.5),
          reads=("lnv",), writes=("lnv",), sem="cA", hard=True)
        E("vector", lambda e, n=nrows, xb=xb, st=st: e.tensor_scalar(
            out=st[0:n, :], in0=xb[0:n, :], scalar1=lncol[0:n, 0:1], scalar2=lncol[0:n, 16:17],
            op0=ALU.subtract, op1=ALU.mult),
          reads=(kx, "lnm", "lnv"), writes=(ks,), sem="cV")
        if sbuf_dst is None:
            E("sync", lambda e, r=dram_row, n=nrows, st=st: e.dma_start(
                out=xn_tm_d.ap()[r:r + n, :], in_=st[0:n, :]),
              reads=(ks,), writes=("xn_d",), sem=f"qxn{buf}", n=16)

    for i in range(NTT):
        ln_tile(3 + i * 128, 128, i % 2, i * 128)
    ln_tile(0, 3, 0, 0, sbuf_dst=xnhalo_tm)
    for k in range(KD):
        E("tensor", lambda e, k=k: e.transpose(pBb[0:128, 0:3], xnhalo_tm[0:3, k * 128:(k + 1) * 128],
                                               identb[0:3, 0:3]),
          reads=("xnhalo_tm", "identb"), writes=("pB",), sem="cT")
        E("vector", lambda e, k=k: e.tensor_copy(xnhalo[:, k * 3:(k + 1) * 3], pBb[0:128, 0:3]),
          reads=("pB",), writes=("xnhalo",), sem="cV")

    dbg_dump("dbg_xn", lambda: xn_tm_d.ap())
    if upto == "ln":
        return _finish(nc, sch, ge, [bln], out_d)

    sch.fence()
    bln.close()
    bip = contextlib.ExitStack()
    xnf = TS_("xnf", [128, KD * T], BF16, bip)
    wslot = [TS_(f"wslot{i}", [128, KD * 128], BF16, bip) for i in range(2)]
    cvw = [TS_(f"cvw{i}", [128, 4 * 128], BF16, bip) for i in range(2)]
    preconv = TS_("preconv", [128, 3 + 1024], BF16, bip)
    aDT = TS_("aDT", [64, T], F32, bip)
    aSC = TS_("aSC", [64, T], F32, bip)
    aAC = TS_("aAC", [64, T], F32, bip)
    aDC = TS_("aDC", [64, T], F32, bip)
    ascol = TS_("ascol", [64, 8], F32, bip)
    stg = [TS_(f"stg{i}", [128, 512], BF16, bip) for i in range(4)]

    for k in range(KD):
        E("sync", lambda e, k=k: e.dma_start_transpose(xnf[:, k * T:(k + 1) * T],
                                                       xn_tm_d.ap()[:, k * 128:(k + 1) * 128]),
          reads=("xn_d",), writes=(f"xnf{k}",), sem="qxnr", n=16)
    sch.bump(tuple(f"xnf{k}" for k in range(KD)), "qxnr")

    def w_load(mc, slot_i):
        slot, sk = wslot[slot_i], f"wslot{slot_i}"
        E("gpsimd", lambda e, mc=mc, slot=slot: e.dma_start(
            out=slot[:, :].rearrange("p (k c) -> p k c", k=KD), in_=wtile[mc]),
          writes=(sk,), sem=("qw1a" if slot_i == 0 else "qw1b"), n=16)

    for i, mc in enumerate(range(32, 66)):
        w_load(mc, i % 2)
        slot, sk = wslot[i % 2], f"wslot{i % 2}"
        bank, bk = pb[i % 2], f"pb{i % 2}"
        for k in range(KD):
            E("tensor", lambda e, k=k, slot=slot, bank=bank: e.matmul(
                bank.ap()[:, 0:3], slot[:, k * 128:(k + 1) * 128],
                xnhalo[:, k * 3:(k + 1) * 3], start=(k == 0), stop=(k == KD - 1)),
              reads=(sk, "xnhalo"), writes=(bk,), sem=("cT" if k == KD - 1 else None))
        E("scalar", lambda e, i=i, bank=bank: e.copy(xbchalo[:, i * 3:(i + 1) * 3], bank.ap()[:, 0:3]),
          reads=(bk,), writes=("xbchalo",), sem="cA", hard=True)

    xpair1 = TS_("xpair1", [128, DI], BF16, bip)
    xdpair1 = TS_("xdpair1", [128, DI], BF16, bip)
    xdc1 = TS_("xdc1", [64, DI], BF16, bip)
    btpc1 = TS_("btpc1", [64, 128], BF16, bip)

    def emit_sweep_pair(p):
        E("sync", lambda e, p=p: e.dma_start_transpose(xpair1[:, :],
                                                       xssm_d.ap()[:, p * 128:(p + 1) * 128]),
          reads=("xssm_d",), writes=("xpair1",), sem="qxr", n=16)
        E("vector", lambda e, p=p: e.tensor_tensor(
            out=xdpair1[:, :].rearrange("s (h q) -> s h q", h=NH),
            in0=xpair1[:, :].rearrange("s (h q) -> s h q", h=NH),
            in1=bass.AP(dcyb, p * 64, [[NPAIR * 64, 128], [1, 64], [0, 64]]),
            op=ALU.mult),
          reads=("xpair1", "dcyb"), writes=("xdpair1",), sem="cV")
        E("sync", lambda e, p=p: e.dma_start(out=xd_tm_d.ap()[p * 128:(p + 1) * 128, :], in_=xdpair1[:, :]),
          reads=("xdpair1",), writes=("xd_d",), sem="qxd", n=16)
        for ci in range(2):
            c = 2 * p + ci
            E("tensor", lambda e, c=c: e.transpose(pBb[0:64, 0:128], bc[:, c * 64:(c + 1) * 64],
                                                   identb[:, :]),
              reads=("bc", "identb"), writes=("pB",), sem="cT")
            E("vector", lambda e: e.tensor_copy(btpc1[:, :], pBb[0:64, 0:128]),
              reads=("pB",), writes=("btpc1",), sem="cV")
            E("sync", lambda e, c=c: e.dma_start(out=xdc1[:, :], in_=xd_tm_d.ap()[c * 64:(c + 1) * 64, :]),
              reads=("xd_d",), writes=("xdc1",), sem="qxd", n=16)
            emit_T_and_scan(sch, S, dscale, btpc1, "btpc1", xdc1, "xdc1", pb, c)
            tp = (0, 64) if ci == 1 else None
            E("tensor", lambda e, c=c, ci=ci, tp=tp: e.matmul(
                pA.ap()[64 * ci:64 * ci + 64, 0:64], bc[:, c * 64:(c + 1) * 64],
                bc[:, T + c * 64: T + (c + 1) * 64], start=True, stop=True,
                tile_position=tp),
              reads=("bc",), writes=("pA",), sem=("cT" if ci == 1 else None))
            if ci == 1:
                E("vector", lambda e, p=p: e.tensor_tensor(out=gtm[:, p * 64:(p + 1) * 64],
                                                           in0=pA.ap()[:, 0:64],
                                                           in1=triu[:, :], op=ALU.mult),
                  reads=("pA", "triu"), writes=("gtm",), sem="cV", hard=True)
    dbg_dump("dbg_S", lambda: S[:, :], reads=("S",))
    dbg_dump("dbg_gtm", lambda: gtm[:, :], reads=("gtm",))
    dbg_dump("dbg_xd", lambda: xd_tm_d.ap())
    if upto == "sweep1":
        return _finish(nc, sch, ge, [bip], out_d)


    mc_order = [66] + list(range(32, 66)) + list(range(0, 32))
    bank_rot = 0
    stg_rot = 0

    def emit_conv(ci, n, cs, csk):
        nonlocal stg_rot
        off = 512 * (n % 2)
        if n % 2 == 0 and n > 0:
            E("scalar", lambda e: e.copy(preconv[:, 0:3], preconv[:, 1024:1027]),
              reads=("preconv",), writes=("pchead",), sem="cA", hard=True)
        cbank_i = 4 + (n % 2)
        cbank, cbk = pb[cbank_i], f"pb{cbank_i}"
        for tap in range(4):
            E("tensor", lambda e, tap=tap, off=off, cs=cs, cbank=cbank: e.matmul(
                cbank.ap()[:, :], cs[:, tap * 128:(tap + 1) * 128],
                preconv[:, off + tap:off + tap + 512], start=(tap == 0), stop=(tap == 3)),
              reads=(csk, "preconv", "pchead"), writes=(cbk,),
              sem=("cT" if tap == 3 else None))
        si = stg_rot % 4
        st_, stk = stg[si], f"stg{si}"
        stg_rot += 1
        E("scalar", lambda e, ci=ci, cbank=cbank, st_=st_: e.activation(
            st_[:, :], cbank.ap()[:, :], AF.Silu, bias=convb_s[:, ci:ci + 1]),
          reads=(cbk, "convb"), writes=(stk,), sem="cA")
        if ci < 32:
            E("sync", lambda e, ci=ci, n=n, st_=st_: e.dma_start(
                out=xssm_d.ap()[ci * 128:(ci + 1) * 128, n * 512:(n + 1) * 512], in_=st_[:, :]),
              reads=(stk,), writes=("xssm_d",), sem=f"qst{si}", n=16)
        else:
            boff = (ci - 32) * T
            E("vector", lambda e, boff=boff, n=n, st_=st_: e.tensor_copy(
                bc[:, boff + n * 512: boff + (n + 1) * 512], st_[:, :]),
              reads=(stk,), writes=("bc",), sem="cV")

    for mi, mc in enumerate(mc_order):
        w_load(mc, mi % 2)
        slot, sk = wslot[mi % 2], f"wslot{mi % 2}"
        is_xbc = 32 <= mc < 66
        ci = mc - 32 if is_xbc else None
        if is_xbc:
            cs, csk = cvw[mi % 2], f"cvw{mi % 2}"
            E("gpsimd", lambda e, ci=ci, cs=cs: e.dma_start(
                out=cs[:, :].rearrange("p (t c) -> p t c", t=4), in_=cwdiag[ci]),
              writes=(csk,), sem=("qcv0" if mi % 2 == 0 else "qcv1"), n=16)
            E("scalar", lambda e, ci=ci: e.copy(preconv[:, 0:3], xbchalo[:, ci * 3:(ci + 1) * 3]),
              reads=("xbchalo",), writes=("pchead",), sem="cA", hard=True)
        for n in range(NT + 1):
            if n < NT:
                bank_i = bank_rot % 4
                bank, bk = pb[bank_i], f"pb{bank_i}"
                bank_rot += 1
                for k in range(KD):
                    E("tensor", lambda e, k=k, n=n, slot=slot, bank=bank: e.matmul(
                        bank.ap()[:, :], slot[:, k * 128:(k + 1) * 128],
                        xnf[:, k * T + n * 512: k * T + (n + 1) * 512],
                        start=(k == 0), stop=(k == KD - 1)),
                      reads=(sk, f"xnf{k}"), writes=(bk,), sem=("cT" if k == KD - 1 else None))
            if is_xbc and n >= 1:
                emit_conv(ci, n - 1, cs, csk)
            if n < NT:
                if mc == 66:
                    E("vector", lambda e, n=n, bank=bank: e.tensor_copy(
                        aDT[:, n * 512:(n + 1) * 512], bank.ap()[0:64, :]),
                      reads=(bk,), writes=("a1lo",), sem="cV")
                elif is_xbc:
                    E("scalar", lambda e, n=n, bank=bank: e.copy(
                        preconv[:, 3 + 512 * (n % 2):3 + 512 * (n % 2) + 512], bank.ap()[:, :]),
                      reads=(bk, "pchead"), writes=("preconv",), sem="cA", hard=True)
                else:
                    si = stg_rot % 4
                    st_, stk = stg[si], f"stg{si}"
                    stg_rot += 1
                    E("scalar", lambda e, bank=bank, st_=st_: e.copy(st_[:, :], bank.ap()[:, :]),
                      reads=(bk,), writes=(stk,), sem="cA")
                    E("sync", lambda e, mc=mc, n=n, st_=st_: e.dma_start(
                        out=z_d.ap()[mc * 128:(mc + 1) * 128, n * 512:(n + 1) * 512], in_=st_[:, :]),
                      reads=(stk,), writes=("z_d",), sem=f"qst{si}", n=16)
        if mc == 66:
            emit_dt_chain(sch, aDT, aSC, aAC, aDC, ascol, dtb_s, acol, ones64, identf,
                          atp, eab, dcyb, dscale, acumT_d, asr_d, rowb, ones1, pA, pB)
        if mc < 32 and mc % 2 == 1:
            if mc == 1:
                E("vector", lambda e: e.memset(S[:, :], 0.0), reads=("S",), writes=("S",), sem="cV")
            emit_sweep_pair(mc // 2)

    dbg_dump("dbg_dt", lambda: aDT[:, :], reads=("a1lo",))
    dbg_dump("dbg_acum", lambda: aAC[:, :], reads=("a2hi",))
    dbg_dump("dbg_atp", lambda: atp[:, :], reads=("atp",))
    dbg_dump("dbg_eab", lambda: eab[:, :], reads=("eab",))
    dbg_dump("dbg_dcy", lambda: dcyb[:, :], reads=("dcyb",))
    dbg_dump("dbg_dscale", lambda: dscale[:, :], reads=("dscale",))
    dbg_dump("dbg_bc", lambda: bc[:, :], reads=("bc",))
    dbg_dump("dbg_xbc", lambda: xssm_d.ap())
    dbg_dump("dbg_z", lambda: z_d.ap())
    if upto == "inproj":
        return _finish(nc, sch, ge, [bip], out_d)

    E("sync", lambda e: e.dma_start(out=cc_in_d.ap()[0:128, :], in_=S[:, :]),
      reads=("S",), writes=("ccin",), sem="qcc", n=16)
    E("vector", lambda e: e.reduce_sum(
        ascol[:, 0:1],
        bass.AP(aAC, 63, [[T, 64], [64, 32]]), axis=AXX),
      reads=("a2hi",), writes=("asumcore",), sem="cV", hard=True)
    E("tensor", lambda e: e.transpose(pA.ap()[0:1, 0:64], ascol[:, 0:1], identf[0:64, 0:64]),
      reads=("asumcore", "identf"), writes=("pA",), sem="cT")
    E("vector", lambda e: e.tensor_copy(rowb[0:1, 0:64], pA.ap()[0:1, 0:64]),
      reads=("pA",), writes=("rowb",), sem="cV")
    E("sync", lambda e: e.dma_start(out=cc_in_d.ap()[128:129, 0:64], in_=rowb[0:1, 0:64]),
      reads=("rowb",), writes=("ccin",), sem="qcc", n=16)
    sch.fence(("gpsimd",))
    E("gpsimd", lambda e: e.collective_compute(
        "AllGather", ALU.bypass, replica_groups=[list(range(NCORES))],
        ins=[cc_in_d.ap()], outs=[cc_out_d.ap()]),
      reads=("ccin",), writes=("ccout",), sem="ccs", n=1)

    sch.fence()
    bip.close()
    bho = contextlib.ExitStack()
    tjb = [TS_("tjb0", [128, DI], F32, bho)]
    dbc = TS_("dbc", [128, 64], F32, bho)
    E("vector", lambda e: e.memset(S[:, :], 0.0), reads=("S",), writes=("S",), sem="cV")
    for j in range(7):
        tj, tk = tjb[0], "tjb0"
        E("sync", lambda e, j=j, tj=tj: e.dma_start(out=tj[:, :],
                                                    in_=cc_out_d.ap()[j * 129:j * 129 + 128, :]),
          reads=("ccout",), writes=(tk,), sem="qcc", n=16)
        E("sync", lambda e, j=j: e.dma_start(out=rowb[0:1, 0:64],
                                             in_=cc_out_d.ap()[j * 129 + 128:j * 129 + 129, 0:64]),
          reads=("ccout",), writes=("rowb",), sem="qrb", n=16)
        E("scalar", lambda e: e.activation(rowb[0:1, 0:64], rowb[0:1, 0:64], AF.Exp),
          reads=("rowb",), writes=("rowb",), sem="cA")
        E("tensor", lambda e: e.matmul(pA.ap()[:, 0:64], ones1[:, :], rowb[0:1, 0:64],
                                       start=True, stop=True),
          reads=("rowb", "ones1"), writes=("pA",), sem="cT")
        E("vector", lambda e, j=j: e.tensor_scalar(out=dbc[:, :], in0=pA.ap()[:, 0:64],
                                                   scalar1=hmask[:, j:j + 1], scalar2=hmask1[:, j:j + 1],
                                                   op0=ALU.mult, op1=ALU.add),
          reads=("pA", "hmask", "hmask1"), writes=("dbc",), sem="cV", hard=True)
        E("vector", lambda e: e.tensor_tensor(
            out=S[:, :].rearrange("n (h q) -> n h q", h=NH),
            in0=S[:, :].rearrange("n (h q) -> n h q", h=NH),
            in1=bass.AP(dbc, 0, [[64, 128], [1, 64], [0, 64]]), op=ALU.mult),
          reads=("S", "dbc"), writes=("S",), sem="cV")
        E("vector", lambda e, j=j, tj=tj: e.scalar_tensor_tensor(
            out=S[:, :], in0=tj[:, :], scalar=hmask[:, j:j + 1], in1=S[:, :],
            op0=ALU.mult, op1=ALU.add),
          reads=(tk, "S", "hmask"), writes=("S",), sem="cV")
    dbg_dump("dbg_sin", lambda: S[:, :], reads=("S",))
    if upto == "exchange":
        return _finish(nc, sch, ge, [bho], out_d)

    sch.fence()
    bho.close()
    bcm = contextlib.ExitStack()
    sbf = [TS_(f"sbf{i}", [128, DI], BF16, bcm) for i in range(2)]
    abbuf = TS_("abbuf", [128, DI], F32, bcm)
    argbuf = TS_("argbuf", [128, DI], F32, bcm)
    mbuf = [TS_(f"mbuf{i}", [128, DI], BF16, bcm) for i in range(2)]
    xbuf_ = [TS_("xbufA", [128, DI], BF16, bcm)]
    xdc2_ = [TS_(f"xdc2{i}", [64, DI], BF16, bcm) for i in range(2)]
    zbuf_ = [TS_("zbufA", [128, DI], BF16, bcm)]
    t1 = argbuf
    ystage = TS_("ystage", [128, DI], BF16, bcm)
    btpc2 = TS_("btpc2", [64, 128], BF16, bcm)
    rmsc = TS_("rmsc", [128, 16], F32, bcm)
    fxn = TS_("fxn", [128, KD * 512], BF16, bcm)
    fslot = [TS_(f"fslot{i}", [128, KD * 128], BF16, bcm) for i in range(2)]
    axsb = TS_("axsb", [128, 512], BF16, bcm)
    axw = TS_("axw", [128, 512], BF16, bcm)
    wexpb = TS_("wexpb", [128, 512], BF16, bcm)
    wem = TS_("wem", [8, 512], BF16, bcm)
    astb = TS_("astb", [128, KD * 128], BF16, bcm)
    bstb = TS_("bstb", [128, 16 * 128], BF16, bcm)
    rwtb = TS_("rwtb", [128, KD * 8], BF16, bcm)
    fsc = TS_("fsc", [128, 40], F32, bcm)
    fb2 = TS_("fb2", [128, 8], F32, bcm)
    gst = TS_("gst", [128, 512], BF16, bcm)

    E("gpsimd", lambda e: e.dma_start(out=astb[:, :].rearrange("p (k c) -> p k c", k=KD), in_=ast_in),
      writes=("astb",), sem="qm", n=16)
    E("gpsimd", lambda e: e.dma_start(out=bstb[:, :].rearrange("p (m c) -> p m c", m=16), in_=bst_in),
      writes=("bstb",), sem="qm", n=16)
    E("gpsimd", lambda e: e.dma_start(out=rwtb[:, :].rearrange("p (k c) -> p k c", k=KD), in_=rwt_in),
      writes=("rwtb",), sem="qm", n=16)
    sch.bump(("dblkb", "astb", "bstb", "rwtb"), "qm")

    def cmain_pair(p):
        xbuf, xbk = xbuf_[0], "xbufA"
        zbuf, zbk = zbuf_[0], "zbufA"
        E("gpsimd", lambda e, p=p: e.dma_start(
            out=abbuf[0:64, :], in_=bass.AP(acumT_d, (2 * p) * 64 * 64, [[0, 64], [64, 64], [1, 64]])),
          reads=("acumT_d",), writes=("abbuf",), sem="qab", n=16)
        E("gpsimd", lambda e, p=p: e.dma_start(
            out=abbuf[64:128, :], in_=bass.AP(acumT_d, (2 * p + 1) * 64 * 64, [[0, 64], [64, 64], [1, 64]])),
          reads=("acumT_d",), writes=("abbuf",), sem="qab", n=16)
        E("sync", lambda e, p=p, xbuf=xbuf: e.dma_start_transpose(xbuf[:, :], xssm_d.ap()[:, p * 128:(p + 1) * 128]),
          reads=("xssm_d",), writes=(xbk,), sem="qxr", n=16)
        E("sync", lambda e, p=p, zbuf=zbuf: e.dma_start_transpose(zbuf[:, :], z_d.ap()[:, p * 128:(p + 1) * 128]),
          reads=("z_d",), writes=(zbk,), sem="qzr", n=16)
        mb, mk = mbuf[p % 2], f"mbuf{p % 2}"
        E("vector", lambda e, p=p: e.tensor_tensor(
            out=argbuf[:, :].rearrange("s (l h) -> s l h", l=64),
            in0=abbuf[:, :].rearrange("s (l h) -> s l h", l=64),
            in1=bass.AP(atp, p * 64, [[NPAIR * 64, 128], [0, 64], [1, 64]]),
            op=ALU.subtract),
          reads=("abbuf", "atp"), writes=("argbuf",), sem="cV")
        E("vector", lambda e: e.tensor_scalar_min(argbuf[:, :], argbuf[:, :], CLAMP),
          reads=("argbuf",), writes=("argbuf",), sem="cV", hard=True)
        E("scalar", lambda e: e.activation(argbuf[:, :], argbuf[:, :], AF.Exp),
          reads=("argbuf",), writes=("argbuf",), sem="cA")
        E("vector", lambda e, p=p, mb=mb, mk=mk: e.tensor_tensor(
            out=mb[:, :].rearrange("s (l h) -> s l h", l=64),
            in0=argbuf[:, :].rearrange("s (l h) -> s l h", l=64),
            in1=bass.AP(gtm, p * 64, [[NPAIR * 64, 128], [1, 64], [0, 64]]),
            op=ALU.mult),
          reads=("argbuf", "gtm"), writes=(mk,), sem="cV")
        E("vector", lambda e, mb=mb, mk=mk: e.tensor_tensor(out=mb[:, :], in0=mb[:, :], in1=dmatb[:, :],
                                                            op=ALU.add),
          reads=("dmatb", mk), writes=(mk,), sem="cV")
        if "dbg_m" in dbg_outs:
            E("sync", lambda e, p=p, mb=mb: e.dma_start(out=dbg_outs["dbg_m"][p * 128:(p + 1) * 128, :],
                                                        in_=mb[:, :]),
              reads=(mk,), sem="qdbg", n=16)
        for ci in range(2):
            c = 2 * p + ci
            E("tensor", lambda e, c=c: e.transpose(pBb[0:64, 0:128], bc[:, c * 64:(c + 1) * 64],
                                                   identb[:, :]),
              reads=("bc", "identb"), writes=("pB",), sem="cT")
            E("vector", lambda e: e.tensor_copy(btpc2[:, :], pBb[0:64, 0:128]),
              reads=("pB",), writes=("btpc2",), sem="cV")
            xdc2, xdk2 = xdc2_[ci], f"xdc2{ci}"
            E("gpsimd", lambda e, c=c, xdc2=xdc2: e.dma_start(out=xdc2[:, :], in_=xd_tm_d.ap()[c * 64:(c + 1) * 64, :]),
              reads=("xd_d",), writes=(xdk2,), sem="qxd2", n=16)
            emit_T_and_scan(sch, S, dscale, btpc2, "btpc2", xdc2, xdk2, pb, c,
                            sbf=sbf[ci], sbfk=f"sbf{ci}")
        for j in range(8):
            ybank, ybk = pb[j % 2], f"pb{j % 2}"
            obank, obk = pb[2 + j % 2], f"pb{2 + j % 2}"
            for ci in range(2):
                r0 = 64 * ci
                tpd = (64, 64) if ci == 1 else None
                for h8 in range(8):
                    h = 8 * j + h8
                    E("tensor", lambda e, h=h, h8=h8, r0=r0, mb=mb, mk=mk, ybank=ybank, tp=tpd,
                      first=(h8 == 0), last=(h8 == 7): e.matmul(
                        ybank.ap()[r0:r0 + 64, h8 * 64:(h8 + 1) * 64],
                        mb[:, :].rearrange("s (l h2) -> s l h2", l=64)[r0:r0 + 64, :, h],
                        xbuf[r0:r0 + 64, h * 64:(h + 1) * 64],
                        start=first, stop=last, tile_position=tp),
                      reads=(mk, xbk), writes=(ybk,),
                      sem=("cT" if (ci == 1 and h8 == 7) else None))
            for ci in range(2):
                c = 2 * p + ci
                tpo = (0, 64) if ci == 1 else None
                E("tensor", lambda e, j=j, c=c, ci=ci, obank=obank, tp=tpo: e.matmul(
                    obank.ap()[64 * ci:64 * ci + 64, :],
                    bc[:, T + c * 64: T + (c + 1) * 64],
                    sbf[ci][:, j * 512:(j + 1) * 512],
                    start=True, stop=True, tile_position=tp),
                  reads=("bc", f"sbf{ci}"), writes=(obk,),
                  sem=("cT" if ci == 1 else None))
            E("vector", lambda e, j=j, p=p, obank=obank: e.tensor_tensor(
                out=t1[:, j * 512:(j + 1) * 512].rearrange("s (h q) -> s h q", h=8),
                in0=obank.ap()[:, :].rearrange("s (h q) -> s h q", h=8),
                in1=bass.AP(eab, p * 64 + j * 8, [[NPAIR * 64, 128], [1, 8], [0, 64]]),
                op=ALU.mult),
              reads=(obk, "eab"), writes=("argbuf",), sem="cV")
            E("vector", lambda e, j=j, ybank=ybank: e.tensor_tensor(
                out=ystage[:, j * 512:(j + 1) * 512], in0=t1[:, j * 512:(j + 1) * 512],
                in1=ybank.ap()[:, :], op=ALU.add),
              reads=(ybk, "argbuf"), writes=("ystage",), sem="cV")
        if "dbg_ypre" in dbg_outs:
            E("sync", lambda e, p=p: e.dma_start(out=dbg_outs["dbg_ypre"][p * 128:(p + 1) * 128, :],
                                                 in_=ystage[:, :]),
              reads=("ystage",), sem="qdbg", n=16)
        E("scalar", lambda e, zbuf=zbuf: e.activation(zbuf[:, :], zbuf[:, :], AF.Silu),
          reads=(zbk,), writes=(zbk,), sem="cA")
        E("vector", lambda e, zbuf=zbuf: e.tensor_tensor(out=ystage[:, :], in0=ystage[:, :], in1=zbuf[:, :],
                                                         op=ALU.mult),
          reads=(zbk,), writes=("ystage",), sem="cV")
        E("scalar", lambda e: e.activation(t1[:, :], ystage[:, :], AF.Square),
          reads=("ystage",), writes=("argbuf",), sem="cA")
        E("vector", lambda e: e.reduce_sum(rmsc[:, 0:1], t1[:, :], axis=AXX),
          reads=("argbuf",), writes=("rmsc",), sem="cV", hard=True)
        E("vector", lambda e: e.tensor_scalar(out=rmsc[:, 8:9], in0=rmsc[:, 0:1], scalar1=1.0 / DI,
                                              scalar2=1e-5, op0=ALU.mult, op1=ALU.add),
          reads=("rmsc",), writes=("rmsc2",), sem="cV", hard=True)
        E("scalar", lambda e: e.activation(rmsc[:, 8:9], rmsc[:, 8:9], AF.Ln),
          reads=("rmsc2", "epscol"), writes=("rmsc2",), sem="cA", hard=True)
        E("scalar", lambda e: e.activation(rmsc[:, 8:9], rmsc[:, 8:9], AF.Exp, scale=-0.5),
          reads=("rmsc2",), writes=("rmsc2",), sem="cA", hard=True)
        E("vector", lambda e: e.tensor_scalar_mul(ystage[:, :], ystage[:, :], rmsc[:, 8:9]),
          reads=("rmsc2",), writes=("ystage",), sem="cV")
        E("scalar", lambda e, p=p: e.dma_start(out=y_tm_d.ap()[p * 128:(p + 1) * 128, :], in_=ystage[:, :]),
          reads=("ystage",), writes=("y_d",), sem="qyw", n=16)

    def ffn_prelude(n):
        for k in range(KD):
            E("sync", lambda e, k=k, n=n: e.dma_start_transpose(
                fxn[:, k * 512:(k + 1) * 512],
                xn_tm_d.ap()[n * 512:(n + 1) * 512, k * 128:(k + 1) * 128]),
              reads=("xn_d",), writes=(f"fxn{k}",), sem="qxnr", n=16)
        sch.bump(tuple(f"fxn{k}" for k in range(KD)), "qxnr")
        for tt in range(4):
            for k in range(KD):
                E("tensor", lambda e, k=k, tt=tt: e.matmul(
                    pB.ap()[:, 0:8], fxn[:, k * 512 + tt * 128: k * 512 + (tt + 1) * 128],
                    rwtb[:, k * 8:(k + 1) * 8], start=(k == 0), stop=(k == KD - 1)),
                  reads=(f"fxn{k}", "rwtb"), writes=("pB",), sem=("cT" if k == KD - 1 else None))
            E("vector", lambda e: e.tensor_copy(fsc[:, 0:8], pB.ap()[:, 0:8]),
              reads=("pB",), writes=("flg",), sem="cV", hard=True)
            if "dbg_lg" in dbg_outs:
                E("sync", lambda e, n=n, tt=tt: e.dma_start(
                    out=dbg_outs["dbg_lg"][n * 512 + tt * 128: n * 512 + (tt + 1) * 128, :],
                    in_=fsc[:, 0:8]),
                  reads=("flg",), sem="qdbg", n=16)
            E("vector", lambda e: e.reduce_max(fsc[:, 8:9], fsc[:, 0:8], axis=AXX),
              reads=("flg",), writes=("fm1",), sem="cV", hard=True)
            E("vector", lambda e: e.tensor_tensor(out=fb2[:, :], in0=fsc[:, 0:8],
                                                  in1=fsc[:, 8:9].broadcast_to([128, 8]),
                                                  op=ALU.is_ge),
              reads=("flg", "fm1"), writes=("fb2",), sem="cV", hard=True)
            E("vector", lambda e: e.scalar_tensor_tensor(out=fsc[:, 0:8], in0=fb2[:, :], scalar=-1e30,
                                                         in1=fsc[:, 0:8], op0=ALU.mult, op1=ALU.add),
              reads=("fb2", "flg"), writes=("flg",), sem="cV", hard=True)
            E("vector", lambda e: e.reduce_max(fsc[:, 16:17], fsc[:, 0:8], axis=AXX),
              reads=("flg",), writes=("fm2",), sem="cV", hard=True)
            E("vector", lambda e: e.tensor_tensor(out=fsc[:, 0:8], in0=fsc[:, 0:8],
                                                  in1=fsc[:, 16:17].broadcast_to([128, 8]),
                                                  op=ALU.is_ge),
              reads=("flg", "fm2"), writes=("flg",), sem="cV", hard=True)
            E("vector", lambda e: e.tensor_tensor(out=fsc[:, 24:25], in0=fsc[:, 8:9], in1=fsc[:, 16:17],
                                                  op=ALU.subtract),
              reads=("fm1", "fm2"), writes=("fw",), sem="cV", hard=True)
            E("scalar", lambda e: e.activation(fsc[:, 24:25], fsc[:, 24:25], AF.Sigmoid),
              reads=("fw",), writes=("fw",), sem="cA", hard=True)
            E("vector", lambda e: e.tensor_scalar(out=fsc[:, 32:33], in0=fsc[:, 24:25], scalar1=-1.0,
                                                  scalar2=1.0, op0=ALU.mult, op1=ALU.add),
              reads=("fw",), writes=("fw2",), sem="cV", hard=True)
            E("vector", lambda e: e.tensor_scalar_mul(fb2[:, :], fb2[:, :], fsc[:, 24:25]),
              reads=("fb2", "fw"), writes=("fb2",), sem="cV", hard=True)
            E("vector", lambda e: e.scalar_tensor_tensor(out=fsc[:, 0:8], in0=fsc[:, 0:8],
                                                         scalar=fsc[:, 32:33], in1=fb2[:, :],
                                                         op0=ALU.mult, op1=ALU.add),
              reads=("fb2", "fw2", "flg"), writes=("flg",), sem="cV", hard=True)
            E("tensor", lambda e: e.transpose(pA.ap()[0:8, 0:128], fsc[:, 0:8], identf[:, :]),
              reads=("flg", "identf"), writes=("pA",), sem="cT")
            E("scalar", lambda e, tt=tt: e.copy(wem[0:8, tt * 128:(tt + 1) * 128], pA.ap()[0:8, 0:128]),
              reads=("pA",), writes=("wem",), sem="cA")
        dbg_dump("dbg_wem", (lambda n=n: (lambda: dbg_outs["dbg_wem"]))(), reads=("wem",)) if False else None
        if "dbg_wem" in dbg_outs:
            sch.fence(("gpsimd",))
            E("gpsimd", lambda e, n=n: e.dma_start(out=dbg_outs["dbg_wem"][0:8, n * 512:(n + 1) * 512],
                                                   in_=wem[0:8, :]),
              reads=("wem",), sem="qdbg", n=16)
        E("tensor", lambda e: e.matmul(pB.ap()[:, :], dblkb[:, :], wem[0:8, :], start=True, stop=True),
          reads=("dblkb", "wem"), writes=("pB",), sem="cT")
        E("scalar", lambda e: e.copy(wexpb[:, :], pB.ap()[:, :]), reads=("pB",), writes=("wexpb",), sem="cA")
        for k in range(KD):
            E("tensor", lambda e, k=k: e.matmul(pA.ap()[:, :], astb[:, k * 128:(k + 1) * 128],
                                                fxn[:, k * 512:(k + 1) * 512],
                                                start=(k == 0), stop=(k == KD - 1)),
              reads=("astb", f"fxn{k}"), writes=("pA",), sem=("cT" if k == KD - 1 else None))
        E("scalar", lambda e: e.copy(axsb[:, :], pA.ap()[:, :]), reads=("pA",), writes=("axsb",), sem="cA")
        E("vector", lambda e: e.tensor_tensor(out=axw[:, :], in0=axsb[:, :], in1=wexpb[:, :], op=ALU.mult),
          reads=("axsb", "wexpb"), writes=("axw",), sem="cV")
        for nm, buf, key in (("dbg_axs", axsb, "axsb"), ("dbg_wexp", wexpb, "wexpb"), ("dbg_axw", axw, "axw")):
            if nm in dbg_outs:
                E("sync", lambda e, nm=nm, buf=buf, n=n: e.dma_start(
                    out=dbg_outs[nm][:, n * 512:(n + 1) * 512], in_=buf[:, :]),
                  reads=(key,), sem="qdbg", n=16)
    def ffn_ms(n, ms):
        for m in ms:
            slot, sk = fslot[m % 2], f"fslot{m % 2}"
            E("gpsimd", lambda e, m=m, slot=slot: e.dma_start(
                out=slot[:, :].rearrange("p (k c) -> p k c", k=KD), in_=ftile[m]),
              writes=(sk,), sem=("qw2a" if m % 2 == 0 else "qw2b"), n=16)
            bank, bk = (pA, "pA") if m % 2 == 0 else (pB, "pB")
            if FFN_EXPERT_FIRST and not FFN_NO_EXPERT:
                E("tensor", lambda e, m=m, bank=bank: e.matmul(
                    bank.ap()[:, :], bstb[:, m * 128:(m + 1) * 128], axw[:, :],
                    start=True, stop=False),
                  reads=("bstb", "axw"), writes=(bk,), sem="cT")
            for k in range(KD):
                E("tensor", lambda e, k=k, slot=slot, bank=bank: e.matmul(
                    bank.ap()[:, :], slot[:, k * 128:(k + 1) * 128], fxn[:, k * 512:(k + 1) * 512],
                    start=(k == 0 and not FFN_EXPERT_FIRST),
                    stop=(k == KD - 1 and (FFN_NO_EXPERT or FFN_EXPERT_FIRST))),
                  reads=(sk, f"fxn{k}"), writes=(bk,),
                  sem=("cT" if (m == 15 or (k == KD - 1 and (FFN_NO_EXPERT or FFN_EXPERT_FIRST))) else None))
            if not FFN_NO_EXPERT and not FFN_EXPERT_FIRST:
                E("tensor", lambda e, m=m, bank=bank: e.matmul(
                    bank.ap()[:, :], bstb[:, m * 128:(m + 1) * 128], axw[:, :],
                    start=FFN_NO_BASE, stop=True),
                  reads=("bstb", "axw"), writes=(bk,), sem="cT")
            E("scalar", lambda e, bank=bank: e.activation(gst[:, :], bank.ap()[:, :], AF.Silu),
              reads=(bk,), writes=("gst",), sem="cA")
            E("sync", lambda e, m=m, n=n: e.dma_start(
                out=g_d.ap()[m * 128:(m + 1) * 128, n * 512:(n + 1) * 512], in_=gst[:, :]),
              reads=("gst",), writes=("g_d",), sem="qgw", n=16)

    for p in range(NPAIR):
        cmain_pair(p)
        nb = p // 4
        if p % 4 == 0:
            ffn_prelude(nb)
            ffn_ms(nb, range(0, 4))
        elif p % 4 == 1:
            ffn_ms(nb, range(4, 8))
        elif p % 4 == 2:
            ffn_ms(nb, range(8, 12))
        else:
            ffn_ms(nb, range(12, 16))
    dbg_dump("dbg_y", lambda: y_tm_d.ap())
    dbg_dump("dbg_g", lambda: g_d.ap())
    if upto == "cmain":
        return _finish(nc, sch, ge, [bcm], out_d)

    sch.fence()
    bcm.close()
    gss.close()
    bop = contextlib.ExitStack()
    oslot = [TS_(f"oslot{i}", [128, 32 * 128], BF16, bop) for i in range(16)]
    ybuf = [TS_(f"ybuf{i}", [128, 512], BF16, bop) for i in range(2)]
    sst = TS_("sst", [128, 512], BF16, bop)
    gbuf = TS_("gbuf", [128, 512], BF16, bop)
    sgst = TS_("sgst", [128, 512], BF16, bop)
    x5 = [TS_("x50", [128, D], F32, bop)]
    sgb = [TS_("sgb0", [128, D], BF16, bop)]
    sgf = TS_("sgf", [128, D], F32, bop)
    ob = [TS_("ob0", [128, D], F32, bop)]

    for m in range(16):
        slot, sk = oslot[m], f"oslot{m}"
        E("gpsimd", lambda e, m=m, slot=slot: e.dma_start(
            out=slot[:, :].rearrange("p (k c) -> p k c", k=32), in_=optile[m]),
          writes=(sk,), sem="qw3", n=16)
    sch.bump(tuple(f"oslot{m}" for m in range(16)), "qw3")
    for m in range(16):
        slot, sk = oslot[m], f"oslot{m}"
        E("vector", lambda e, slot=slot: e.tensor_tensor(
            out=slot[:, :].rearrange("p (k c) -> p k c", k=32),
            in0=slot[:, :].rearrange("p (k c) -> p k c", k=32),
            in1=bass.AP(nwb, 0, [[32, 128], [1, 32], [0, 128]]), op=ALU.mult),
          reads=(sk, "nwb"), writes=(sk,), sem="cV")

    for n in range(NT):
        for mg in range(4):
            for k in range(32):
                yb, yk = ybuf[k % 2], f"ybuf{k % 2}"
                E("sync", lambda e, k=k, n=n, yb=yb: e.dma_start_transpose(
                    yb[:, :], y_tm_d.ap()[n * 512:(n + 1) * 512, k * 128:(k + 1) * 128]),
                  reads=("y_d",), writes=(yk,), sem=("qyr0" if k % 2 == 0 else "qyr1"), n=16)
                for i in range(4):
                    m = mg * 4 + i
                    E("tensor", lambda e, m=m, i=i, k=k, yb=yb: e.matmul(
                        pb[i].ap()[:, :], oslot[m][:, k * 128:(k + 1) * 128], yb[:, :],
                        start=(k == 0), stop=(k == 31)),
                      reads=(f"oslot{m}", yk), writes=(f"pb{i}",),
                      sem=("cT" if (i == 3 or k == 31) else None))
            for i in range(4):
                m = mg * 4 + i
                E("scalar", lambda e, i=i: e.copy(sst[:, :], pb[i].ap()[:, :]),
                  reads=(f"pb{i}",), writes=("sst",), sem="cA")
                E("gpsimd", lambda e, m=m, n=n: e.dma_start(
                    out=gbuf[:, :], in_=g_d.ap()[m * 128:(m + 1) * 128, n * 512:(n + 1) * 512]),
                  reads=("g_d",), writes=("gbuf",), sem="qgr", n=16)
                E("vector", lambda e: e.tensor_tensor(out=sgst[:, :], in0=sst[:, :], in1=gbuf[:, :],
                                                      op=ALU.mult),
                  reads=("sst", "gbuf"), writes=("sgst",), sem="cV")
                E("sync", lambda e, m=m, n=n: e.dma_start(
                    out=sg_d.ap()[m * 128:(m + 1) * 128, n * 512:(n + 1) * 512], in_=sgst[:, :]),
                  reads=("sgst",), writes=("sg_d",), sem="qsg", n=16)
        for tt in range(4):
            i = n * 4 + tt
            xb, xk = x5[0], "x50"
            sb, sk = sgb[0], "sgb0"
            o, ok = ob[0], "ob0"
            E("gpsimd", lambda e, i=i, xb=xb: e.dma_start(out=xb[:, :],
                                                          in_=xin[3 + i * 128: 3 + (i + 1) * 128, :]),
              writes=(xk,), sem=("qx2" if i % 2 == 0 else "qx3"), n=16)
            E("sync", lambda e, i=i, sb=sb: e.dma_start_transpose(sb[:, :],
                                                                   sg_d.ap()[:, i * 128:(i + 1) * 128]),
              reads=("sg_d",), writes=(sk,), sem=("qsr0" if i % 2 == 0 else "qsr1"), n=16)
            E("scalar", lambda e, sb=sb: e.copy(sgf[:, :], sb[:, :]),
              reads=(sk,), writes=("sgf",), sem="cA")
            E("vector", lambda e, xb=xb, o=o: e.tensor_tensor(out=o[:, :], in0=xb[:, :], in1=sgf[:, :],
                                                              op=ALU.add),
              reads=(xk, "sgf"), writes=(ok,), sem="cV")
            E("gpsimd", lambda e, i=i, o=o: e.dma_start(out=out_d[i * 128:(i + 1) * 128, :], in_=o[:, :]),
              reads=(ok,), writes=("outw",), sem="qout", n=16)
    dbg_dump("dbg_sg", lambda: sg_d.ap())
    return _finish(nc, sch, ge, [bop], out_d)



def emit_dt_chain(sch, aDT, aSC, aAC, aDC, ascol, dtb_s, acol, ones64, identf,
                  atp, eab, dcyb, dscale, acumT_d, asr_d, rowb, ones1, pA, pB):
    E = sch.emit
    E("vector", lambda e: e.tensor_scalar_add(aDT[:, :], aDT[:, :], dtb_s[:, :]),
      reads=("a1lo", "dtb"), writes=("a1lo",), sem="cV")
    E("scalar", lambda e: e.activation(aSC[:, :], aDT[:, :], AF.Abs),
      reads=("a1lo",), writes=("a2lo",), sem="cA")
    E("scalar", lambda e: e.activation(aSC[:, :], aSC[:, :], AF.Exp, scale=-1.0),
      reads=("a2lo",), writes=("a2lo",))
    E("scalar", lambda e: e.activation(aSC[:, :], aSC[:, :], AF.Ln, bias=1.0),
      reads=("a2lo",), writes=("a2lo",), sem="cA")
    E("vector", lambda e: e.tensor_scalar_max(aDT[:, :], aDT[:, :], 0.0),
      reads=("a1lo",), writes=("a1lo",), sem="cV")
    E("vector", lambda e: e.tensor_tensor(out=aDT[:, :], in0=aDT[:, :], in1=aSC[:, :], op=ALU.add),
      reads=("a1lo", "a2lo"), writes=("a1lo",), sem="cV")
    E("vector", lambda e: e.tensor_scalar_mul(aSC[:, :], aDT[:, :], acol[:, :]),
      reads=("a1lo", "acol"), writes=("a2lo",), sem="cV")
    for c in range(NCH):
        E("vector", lambda e, c=c: e.tensor_tensor_scan(
            out=aAC[:, c * 64:(c + 1) * 64],
            data0=ones64[:, 0:1].broadcast_to([64, 64]),
            data1=aSC[:, c * 64:(c + 1) * 64],
            initial=0.0, op0=ALU.mult, op1=ALU.add),
          reads=("a2lo", "ones64"), writes=("a2hi",), sem="cV", hard=(c == NCH - 1))
    for c in range(NCH):
        E("vector", lambda e, c=c: e.tensor_scalar(
            out=aDC[:, c * 64:(c + 1) * 64], in0=aAC[:, c * 64:(c + 1) * 64],
            scalar1=aAC[:, c * 64 + 63:c * 64 + 64], scalar2=None, op0=ALU.subtract),
          reads=("a2hi",), writes=("adc",), sem="cV", hard=(c == NCH - 1))
    E("scalar", lambda e: e.activation(aDC[:, :], aDC[:, :], AF.Exp, scale=-1.0),
      reads=("adc",), writes=("adc",), sem="cA")
    E("vector", lambda e: e.tensor_tensor(out=aDC[:, :], in0=aDC[:, :], in1=aDT[:, :], op=ALU.mult),
      reads=("adc", "a1lo"), writes=("adc",), sem="cV")
    E("scalar", lambda e: e.activation(aSC[:, :], aDT[:, :], AF.Ln),
      reads=("a1lo", "a2lo"), writes=("a2lo",), sem="cA")
    E("vector", lambda e: e.tensor_tensor(out=aSC[:, :], in0=aAC[:, :], in1=aSC[:, :],
                                          op=ALU.subtract),
      reads=("a2hi", "a2lo"), writes=("a2lo",), sem="cV")
    for p in range(NPAIR):
        E("tensor", lambda e, p=p: e.transpose(pA.ap()[0:128, 0:64], aAC[:, p * 128:(p + 1) * 128],
                                               identf[0:64, 0:64]),
          reads=("a2hi", "identf"), writes=("pA",), sem="cT")
        E("vector", lambda e, p=p: e.tensor_copy(atp[:, p * 64:(p + 1) * 64], pA.ap()[0:128, 0:64]),
          reads=("pA",), writes=("atp_t",), sem="cV")
        E("sync", lambda e, p=p: e.dma_start(out=acumT_d.ap()[p * 128:(p + 1) * 128, :],
                                             in_=atp[:, p * 64:(p + 1) * 64]),
          reads=("atp_t",), writes=("acumT_d",), sem="qat", n=16)
        E("scalar", lambda e, p=p: e.activation(eab[:, p * 64:(p + 1) * 64], pA.ap()[0:128, 0:64], AF.Exp),
          reads=("pA",), writes=("eab",), sem="cA", hard=True)
        E("tensor", lambda e, p=p: e.transpose(pB.ap()[0:128, 0:64], aSC[:, p * 128:(p + 1) * 128],
                                               identf[0:64, 0:64]),
          reads=("a2lo", "identf"), writes=("pB",), sem="cT")
        E("vector", lambda e, p=p: e.tensor_copy(atp[:, p * 64:(p + 1) * 64], pB.ap()[0:128, 0:64]),
          reads=("pB",), writes=("atp", "atp_t"), sem="cV", hard=True)
    for p in range(NPAIR):
        E("tensor", lambda e, p=p: e.transpose(pA.ap()[0:128, 0:64], aDC[:, p * 128:(p + 1) * 128],
                                               identf[0:64, 0:64]),
          reads=("adc", "identf"), writes=("pA",), sem="cT")
        E("scalar", lambda e, p=p: e.copy(dcyb[:, p * 64:(p + 1) * 64], pA.ap()[0:128, 0:64]),
          reads=("pA",), writes=("dcyb",), sem="cA", hard=True)
    E("tensor", lambda e: e.transpose(pB.ap()[0:32, 0:64],
                                      bass.AP(aAC, 63, [[T, 64], [64, 32]]),
                                      identf[0:64, 0:64]),
      reads=("a2hi", "identf"), writes=("pB",), sem="cT")
    E("vector", lambda e: e.tensor_copy(dscale[0:32, 0:64], pB.ap()[0:32, 0:64]),
      reads=("pB",), writes=("dsc_t",), sem="cV")
    E("sync", lambda e: e.dma_start(out=asr_d.ap(), in_=dscale[0:32, 0:64]),
      reads=("dsc_t",), writes=("asr_d",), sem="qat", n=16)
    E("sync", lambda e: e.dma_start(out=rowb[0:1, :], in_=bass.AP(asr_d, 0, [[0, 1], [1, 2048]])),
      reads=("asr_d",), writes=("rowb",), sem="qat", n=16)
    E("scalar", lambda e: e.activation(rowb[0:1, :], rowb[0:1, :], AF.Exp),
      reads=("rowb",), writes=("rowb",), sem="cA")
    for j in range(4):
        E("tensor", lambda e, j=j: e.matmul(pA.ap()[:, :], ones1[:, :], rowb[0:1, j * 512:(j + 1) * 512],
                                            start=True, stop=True),
          reads=("rowb", "ones1", "dsc_t"), writes=("pA",), sem="cT")
        E("vector", lambda e, j=j: e.tensor_copy(dscale[:, j * 512:(j + 1) * 512], pA.ap()[:, :]),
          reads=("pA",), writes=("dscale",), sem="cV")


def emit_T_and_scan(sch, S, dscale, btpc, btk, xdc, xdk, pb, c, sbf=None, sbfk=None):
    E = sch.emit
    E("vector", lambda e, c=c: e.tensor_tensor(
        out=S[:, :].rearrange("n (h q) -> n h q", h=NH),
        in0=S[:, :].rearrange("n (h q) -> n h q", h=NH),
        in1=bass.AP(dscale, c * 64, [[NCH * 64, 128], [1, 64], [0, 64]]), op=ALU.mult),
      reads=("S", "dscale"), writes=("S",), sem="cV")
    for j in range(8):
        bank_i = 4 + j % 2
        bank, bk = pb[bank_i], f"pb{bank_i}"
        E("tensor", lambda e, j=j, bank=bank, btpc=btpc, xdc=xdc: e.matmul(
            bank.ap()[:, :], btpc[0:64, :], xdc[0:64, j * 512:(j + 1) * 512],
            start=True, stop=True),
          reads=(btk, xdk), writes=(bk,), sem="cT")
        E("vector", lambda e, j=j, bank=bank: e.tensor_tensor(
            out=S[:, j * 512:(j + 1) * 512], in0=S[:, j * 512:(j + 1) * 512], in1=bank.ap()[:, :],
            op=ALU.add),
          reads=(bk, "S"), writes=("S",), sem="cV")
    if sbf is not None:
        E("scalar", lambda e, sbf=sbf: e.copy(sbf[:, :], S[:, :]),
          reads=("S",), writes=(sbfk,), sem="cA")


def _finish(nc, sch, ge, stacks, out_d):
    n_out = sch.count.get("qout", 0)
    if n_out:
        sch.q["gpsimd"].append(lambda e, h=sch.sems["qout"], v=n_out: e.wait_ge(h, v))
    v = sch.count.get("qdbg", 0)
    if v:
        sch.q["gpsimd"].append(lambda e, h=sch.sems["qdbg"], v=v: e.wait_ge(h, v))
    with nc.Block() as block:
        @block.sync
        def _(eng):
            for f in sch.q["sync"]:
                f(eng)

        @block.scalar
        def _(eng):
            for f in sch.q["scalar"]:
                f(eng)

        @block.vector
        def _(eng):
            for f in sch.q["vector"]:
                f(eng)

        @block.tensor
        def _(eng):
            for f in sch.q["tensor"]:
                f(eng)

        @block.gpsimd
        def _(eng):
            for f in sch.q["gpsimd"]:
                f(eng)
    for st in stacks:
        try:
            st.close()
        except Exception:
            pass
    try:
        ge.close()
    except Exception:
        pass
    return nc



def _prep_inputs(inputs):
    x = np.ascontiguousarray(inputs["L0_attn_x"][0])
    W = np.asarray(inputs["L0_attn_in_proj_w"])
    W_pad = np.concatenate([W, np.zeros((64, D), np.float32)], 0)
    wtile = np.ascontiguousarray(W_pad.reshape(67, 128, KD, 128).transpose(0, 3, 2, 1))
    cw = np.asarray(inputs["L0_attn_conv_w"])[:, 0, :]
    cwd = np.zeros((34, 128, 4, 128), np.float32)
    idx = np.arange(128)
    cwr = cw.reshape(34, 128, 4)
    cwd[:, idx, :, idx] = cwr.transpose(1, 0, 2)
    convb = np.ascontiguousarray(np.asarray(inputs["L0_attn_conv_b"]).reshape(34, 128).T)
    dtb = np.ascontiguousarray(np.asarray(inputs["L0_attn_dt_bias"]).reshape(64, 1))
    alog = np.ascontiguousarray(np.asarray(inputs["L0_attn_A_log"]).reshape(64, 1))
    Dv = np.asarray(inputs["L0_attn_D"])
    dmat = np.zeros((128, 64, 64), np.float32)
    for s_ in range(128):
        dmat[s_, s_ % 64, :] = Dv
    dmat = dmat.reshape(128, DI)
    nw = np.ascontiguousarray(np.asarray(inputs["L0_attn_norm_w"]).reshape(32, 128).T)
    OP = np.asarray(inputs["L0_attn_out_proj_w"])
    optile = np.ascontiguousarray(OP.reshape(16, 128, 32, 128).transpose(0, 3, 2, 1))
    FW = np.asarray(inputs["L0_ffn_weight"])
    ftile = np.ascontiguousarray(FW.reshape(16, 128, KD, 128).transpose(0, 3, 2, 1))
    A = np.asarray(inputs["L0_ffn_lora_A"])
    ast = np.ascontiguousarray(A.transpose(2, 0, 1).reshape(KD, 128, 128).transpose(1, 0, 2))
    B = np.asarray(inputs["L0_ffn_lora_B"])
    bst = np.ascontiguousarray(B.transpose(0, 2, 1).reshape(128, 16, 128))
    RW = np.asarray(inputs["L0_ffn_router_w"])
    rwt = np.ascontiguousarray(RW.T.reshape(KD, 128, 8).transpose(1, 0, 2))
    identb = np.eye(128, dtype=ml_dtypes.bfloat16)
    identf = np.eye(128, dtype=np.float32)
    triu = (np.arange(64)[None, :] >= (np.arange(128) % 64)[:, None]).astype(np.float32)
    triu = np.ascontiguousarray(triu)
    dblk = np.zeros((8, 128), np.float32)
    for e_ in range(8):
        dblk[e_, e_ * 16:(e_ + 1) * 16] = 2.0
    maps = []
    for c in range(NCORES):
        r0 = c * T
        halo = x[r0 - 3:r0] if c > 0 else np.zeros((3, D), np.float32)
        xin_ = np.ascontiguousarray(np.concatenate([halo, x[r0:r0 + T]], 0))
        hm = np.zeros((128, 7), np.float32)
        hm[:, :c] = 1.0
        maps.append({
            "xin": xin_, "wtile": wtile, "cwdiag": cwd, "convb": convb,
            "dtb": dtb, "alog": alog, "dmat": dmat, "nw_i": nw,
            "optile": optile, "ftile": ftile, "ast": ast, "bst": bst, "rwt": rwt,
            "identb_i": identb, "identf_i": identf, "triu_i": triu, "dblk": dblk,
            "hm": hm, "hm1": np.ascontiguousarray(1.0 - hm),
        })
    return maps


def kernel(**inputs):
    maps = _prep_inputs(inputs)
    nc = build_nc()
    res = run_bass_kernel_spmd(nc, maps, core_ids=list(range(NCORES)))
    out = np.concatenate([r["out"] for r in res.results], axis=0)
    return np.ascontiguousarray(out[None, :, :]).astype(np.float32)
